# version 11
# baseline (speedup 1.0000x reference)
import contextlib
import numpy as np

import concourse.bass as bass
import concourse.mybir as mybir
from concourse.bass_utils import run_bass_kernel_spmd
from concourse.hw_specs import get_activation_tables

F32 = mybir.dt.float32
BF16 = mybir.dt.bfloat16
AF = mybir.ActivationFunctionType
ALU = mybir.AluOpType
AX = mybir.AxisListType

N_CORES = 8
N_BATCH = 4
S = 2048
D = 1024
NH = 16
DH = 64
DI = 2816
SB = S // 128
KT = D // 128
NT = DI // 128
NPAIR = NH // 2
EPS = 1e-5
N_CHUNKS = [(c, min(c + 512, DI)) for c in range(0, DI, 512)]


class Ctr:
    def __init__(self, nc, es, name):
        self.sem = es.enter_context(nc.semaphore(name))
        self.n = 0

    def inc(self, instr, k=1):
        instr.then_inc(self.sem, k)
        self.n += k
        return self.n


def build(n_batch=N_BATCH, debug=False):
    nc = bass.Bass("TRN2", target_bir_lowering=False,
                   detect_race_conditions=False)
    tbl_ids = {name: i for i, name in
               enumerate(get_activation_tables(nc.m.arch))}
    SET_LNEXP = tbl_ids["natural_log_exp_and_others"]
    SET_SILU = tbl_ids["silu_and_others"]

    def load_tbl(eng, set_id):
        eng.add_instruction(mybir.InstLoadActFuncSet(
            name=nc.get_next_instruction_name(), act_func_set_id=set_id,
            ins=[], outs=[]))

    d_x = nc.dram_tensor("x", [n_batch, S, D], F32, kind="ExternalInput").ap()
    d_wq = nc.dram_tensor("wq", [D, D], F32, kind="ExternalInput").ap()
    d_wkv = nc.dram_tensor("wkv", [D, 2 * D], F32, kind="ExternalInput").ap()
    d_wo = nc.dram_tensor("wo", [D, D], F32, kind="ExternalInput").ap()
    d_bo = nc.dram_tensor("bo_t", [128, D], F32, kind="ExternalInput").ap()
    d_wg = nc.dram_tensor("wg", [DI, D], F32, kind="ExternalInput").ap()
    d_wu = nc.dram_tensor("wu", [DI, D], F32, kind="ExternalInput").ap()
    d_wd = nc.dram_tensor("wd", [D, DI], F32, kind="ExternalInput").ap()
    d_id = nc.dram_tensor("ident", [128, 128], F32, kind="ExternalInput").ap()
    d_out = nc.dram_tensor("out", [n_batch, S, D], F32,
                           kind="ExternalOutput").ap()

    if debug:
        d_dbg = {}
        for nm, shape in (
            ("lnxt", [128, KT * 2048]), ("qt", [128, KT * 2048]),
            ("ktb", [128, KT * 2048]), ("v", [128, SB * 1024]),
            ("ot", [128, KT * 2048]), ("xq", [128, D]), ("hq", [128, DI]),
            ("hbuf", [128, DI]), ("xln2", [128, D]), ("xqt", [128, D]),
            ("aqbm", [128, D // 16]), ("axln", [128, D]), ("asq", [128, D]),
            ("ata", [128, D]), ("atsg", [128, D]),
            ("hqt", [128, DI]), ("wgt", [128, KT * DI]),
            ("wut", [128, KT * DI]), ("wdt", [128, NT * D]),
            ("x2", [n_batch * S, D]),
        ):
            d_dbg[nm] = nc.dram_tensor("dbg_" + nm, shape, F32,
                                       kind="ExternalOutput").ap()
    d_x2 = nc.dram_tensor("x2_s", [n_batch, S, D], F32, kind="Internal").ap()
    d_xq = nc.dram_tensor("xq_s", [n_batch, S, D], BF16, kind="Internal").ap()
    d_wgt = nc.dram_tensor("wgt_s", [128, KT * DI], BF16, kind="Internal").ap()
    d_wut = nc.dram_tensor("wut_s", [128, KT * DI], BF16, kind="Internal").ap()
    d_wdt = nc.dram_tensor("wdt_s", [128, NT * D], BF16, kind="Internal").ap()

    es = contextlib.ExitStack()
    with es:
        def sbuf(name, shape, dt):
            return es.enter_context(nc.sbuf_tensor(name, shape, dt))

        cin = Ctr(nc, es, "cin")
        cxl = [Ctr(nc, es, f"cxl{i}") for i in range(2)]
        c3l = [Ctr(nc, es, f"c3l{i}") for i in range(2)]
        cfl = [Ctr(nc, es, f"cfl{i}") for i in range(2)]
        cpl = [Ctr(nc, es, f"cpl{i}") for i in range(2)]
        cso = [Ctr(nc, es, f"cso{i}") for i in range(2)]
        cqo = [Ctr(nc, es, f"cqo{i}") for i in range(2)]
        cql = [Ctr(nc, es, f"cql{i}") for i in range(2)]
        cfo = [Ctr(nc, es, f"cfo{i}") for i in range(2)]
        cwl = Ctr(nc, es, "cwl")
        cst = Ctr(nc, es, "cst")
        ct = Ctr(nc, es, "ct")
        cv = Ctr(nc, es, "cv")
        cs = Ctr(nc, es, "cs")

        blk = es.enter_context(nc.Block())

        bo_bc = sbuf("bo_bc", [128, D], F32)
        idb = sbuf("idb", [128, 128], BF16)
        ones64 = sbuf("ones64", [128, 64], BF16)
        cbias = sbuf("cbias", [128, 2], F32)

        def _(g):
            g.dma_start(out=idb[:], in_=d_id).then_inc(cwl.sem, 16)
            g.dma_start(out=bo_bc[:], in_=d_bo).then_inc(cwl.sem, 16)
        blk.gpsimd(_)
        cwl.n += 32
        n_ld0 = cwl.n

        def _(v):
            v.wait_ge(cwl.sem, n_ld0)
            v.memset(ones64[:], 1.0)
            v.memset(cbias[:, 0:1], EPS)
            cv.inc(v.memset(cbias[:, 1:2], -3.0))
        blk.vector(_)
        n_consts = cv.n

        last_qd = [0]

        def emit_quant(src_ap, dst_ap, F, sq, ta, tsg, bmax, scl, rscl,
                       rscl2, sclb, wait_v=(), wait_s=(), ext_prep=None):
            G = F // 16

            if ext_prep is None:
                def _(s):
                    for ctr, n in wait_s:
                        if n > 0:
                            s.wait_ge(ctr.sem, n)
                    if last_qd[0] > 0:
                        s.wait_ge(cv.sem, last_qd[0])
                    load_tbl(s, SET_LNEXP)
                    s.activation(tsg[:, 0:F], src_ap, AF.Sign)
                    cs.inc(s.activation(sq[:, 0:F], src_ap, AF.Square))
                blk.scalar(_)
                n_prep = cs.n
            else:
                n_prep = ext_prep

            def _(v):
                for ctr, n in wait_v:
                    if n > 0:
                        v.wait_ge(ctr.sem, n)
                src3 = src_ap.rearrange("p (g w) -> p g w", w=16)
                n_red = cv.inc(v.tensor_reduce(
                    bmax[:, 0:G].rearrange("p (g o) -> p g o", o=1), src3,
                    axis=AX.X, op=ALU.max, apply_absolute_value=True))
                v.wait_ge(cv.sem, n_red)
                v.tensor_scalar(scl[:, 0:G], bmax[:, 0:G], 1.0 / 6.0, 1e-12,
                                op0=ALU.mult, op1=ALU.max)
                n_rc = cv.inc(v.reciprocal(rscl[:, 0:G], scl[:, 0:G]))
                v.wait_ge(cv.sem, n_rc)
                v.tensor_tensor(rscl2[:, 0:G], rscl[:, 0:G], rscl[:, 0:G],
                                op=ALU.mult)
                v.tensor_copy(sclb[:, 0:G], scl[:, 0:G])
                v.wait_ge(cs.sem, n_prep)
                r2b = rscl2[:, 0:G].rearrange("p (g o) -> p g o", o=1) \
                    .broadcast_to([128, G, 16])
                sqf = sq[:, 0:F]
                v.tensor_tensor(sqf.rearrange("p (g w) -> p g w", w=16),
                                sqf.rearrange("p (g w) -> p g w", w=16),
                                r2b, op=ALU.mult)
                v.tensor_copy(dst_ap, sqf)
                v.tensor_scalar(ta[:, 0:F], dst_ap, 0.0625, None,
                                op0=ALU.is_gt)
                for t in (0.5625, 1.5625, 3.0625):
                    v.scalar_tensor_tensor(ta[:, 0:F], dst_ap, t, ta[:, 0:F],
                                           op0=ALU.is_gt, op1=ALU.add)
                v.tensor_scalar(ta[:, 0:F], ta[:, 0:F], 0.5, None,
                                op0=ALU.mult)
                for t in (6.25, 12.25):
                    v.scalar_tensor_tensor(ta[:, 0:F], sqf, t, ta[:, 0:F],
                                           op0=ALU.is_gt, op1=ALU.add)
                v.tensor_scalar(dst_ap, sqf, 25.0, 2.0,
                                op0=ALU.is_gt, op1=ALU.mult)
                v.tensor_tensor(ta[:, 0:F], ta[:, 0:F], dst_ap, op=ALU.add)
                sb3 = sclb[:, 0:G].rearrange("p (g o) -> p g o", o=1) \
                    .broadcast_to([128, G, 16])
                v.tensor_tensor(tsg[:, 0:F].rearrange("p (g w) -> p g w", w=16),
                                tsg[:, 0:F].rearrange("p (g w) -> p g w", w=16),
                                sb3, op=ALU.mult)
                cv.inc(v.tensor_tensor(dst_ap, ta[:, 0:F], tsg[:, 0:F],
                                       op=ALU.mult))
            blk.vector(_)
            last_qd[0] = cv.n
            return cv.n

        def emit_ln(src_ap, scratch_ap, dst_ap, stat_t, c0, wait_load,
                    col_guard=0, sq_waits=(), norm_waits=(), sign_dst=None):
            xs, mn, nmn, ssr, nm2, nm2e, var_, lnv, rs, nb = (
                stat_t[:, c0 + i:c0 + i + 1] for i in range(10))

            def _(v):
                for ctr, n in wait_load:
                    v.wait_ge(ctr.sem, n)
                if col_guard > 0:
                    v.wait_ge(cs.sem, col_guard)
                n_red = cv.inc(v.reduce_sum(xs, src_ap, axis=AX.X))
                v.wait_ge(cv.sem, n_red)
                v.tensor_scalar(mn, xs, 1.0 / D, None, op0=ALU.mult)
                cv.inc(v.tensor_scalar(nmn, xs, -1.0 / D, None, op0=ALU.mult))
            blk.vector(_)
            n1 = cv.n

            def _(s):
                for ctr, n in wait_load:
                    s.wait_ge(ctr.sem, n)
                for ctr, n in sq_waits:
                    s.wait_ge(ctr.sem, n)
                if sign_dst is not None and last_qd[0] > 0:
                    s.wait_ge(cv.sem, last_qd[0])
                load_tbl(s, SET_LNEXP)
                s.activation(scratch_ap, src_ap, AF.Square, accum_out=ssr)
                s.wait_ge(cv.sem, n1)
                s.activation(nm2, nmn, AF.Identity, scale=mn, bias=0.0)
                n_a = cs.inc(s.activation(nm2e, nm2, AF.Identity,
                                          bias=cbias[:, 0:1]))
                s.wait_ge(cs.sem, n_a)
                s.activation(var_, ssr, AF.Identity, scale=1.0 / D,
                             bias=nm2e)
                s.activation(lnv, var_, AF.Ln)
                n_b = cs.inc(s.activation(rs, lnv, AF.Exp, scale=-0.5))
                s.wait_ge(cs.sem, n_b)
                n_c = cs.inc(s.activation(nb, nmn, AF.Identity, scale=rs,
                                          bias=0.0))
                s.wait_ge(cs.sem, n_c)
                for ctr, n in norm_waits:
                    s.wait_ge(ctr.sem, n)
                cs.inc(s.activation(dst_ap, src_ap, AF.Identity,
                                    scale=rs, bias=nb))
                if sign_dst is not None:
                    sgn_ap, sq_ap = sign_dst
                    s.activation(sgn_ap, dst_ap, AF.Sign)
                    cs.inc(s.activation(sq_ap, dst_ap, AF.Square))
            blk.scalar(_)
            return cs.n

        p0 = contextlib.ExitStack()
        with p0:
            def p0_sbuf(name, shape, dt):
                return p0.enter_context(nc.sbuf_tensor(name, shape, dt))

            p0_wtile = [p0_sbuf(f"p0w{i}", [128, DI], F32) for i in range(2)]
            p0_q = [p0_sbuf(f"p0q{i}", [128, DI], BF16) for i in range(2)]
            p0_tv = p0_sbuf("p0tv", [128, DI], F32)
            p0_ta = p0_sbuf("p0ta", [128, DI], BF16)
            p0_sg = p0_sbuf("p0sg", [128, DI], BF16)
            p0_bm = p0_sbuf("p0bm", [128, DI // 16], F32)
            p0_r2 = p0_sbuf("p0r2", [128, DI // 16], F32)
            p0_sb = p0_sbuf("p0sb", [128, DI // 16], BF16)
            p0_sc = p0_sbuf("p0sc", [128, DI // 16], F32)
            p0_rs = p0_sbuf("p0rs", [128, DI // 16], F32)
            p0_stage = p0_sbuf("p0_stage", [128, KT * DI], BF16)
            p0_tp = p0.enter_context(
                nc.psum_tensor("p0_tp", [128, 1024], BF16))

            def prep_weight(d_src, d_dst, n_tiles, tile_f, tp_rows):
                nblk = tile_f // 128
                quant_done = {}
                evac_done = [0]
                tp_done = [0, 0]
                entry_st = cst.n
                for t in range(n_tiles):
                    r = t % 2

                    def _(g, t=t, r=r):
                        if t >= 2:
                            g.wait_ge(cv.sem, quant_done[t - 2])
                        g.dma_start(out=p0_wtile[r][:, 0:tile_f],
                                    in_=d_src[t * 128:(t + 1) * 128, 0:tile_f]
                                    ).then_inc(cpl[r].sem, 16)
                    blk.gpsimd(_)
                    cpl[r].n += 16
                    n_ld = cpl[r].n
                    qd = emit_quant(
                        p0_wtile[r][:, 0:tile_f], p0_q[r][:, 0:tile_f],
                        tile_f,
                        p0_tv, p0_ta, p0_sg, p0_bm, p0_sc, p0_rs,
                        p0_r2, p0_sb,
                        wait_v=[(cpl[r], n_ld), (ct, tp_done[1])],
                        wait_s=[(cpl[r], n_ld)])
                    quant_done[t] = qd
                    for g0 in range(0, nblk, 8):
                        ng = min(8, nblk - g0)

                        def _(te, g0=g0, ng=ng, qd=qd, t=t):
                            te.wait_ge(cv.sem, max(qd, n_consts))
                            if evac_done[0] > 0:
                                te.wait_ge(cv.sem, evac_done[0])
                            last = None
                            for j in range(ng):
                                last = te.transpose(
                                    p0_tp[:, j * 128:(j + 1) * 128],
                                    p0_q[t % 2][:, (g0 + j) * 128:
                                                 (g0 + j + 1) * 128],
                                    idb[:])
                            ct.inc(last)
                        blk.tensor(_)
                        n_tp = ct.n

                        def _(v, g0=g0, ng=ng, t=t, n_tp=n_tp):
                            v.wait_ge(ct.sem, n_tp)
                            if t == 0 and g0 == 0 and entry_st > 0:
                                v.wait_ge(cst.sem, entry_st)
                            src = p0_tp[:, 0:ng * 128].rearrange(
                                "p (j c) -> p j c", c=128)
                            dst = p0_stage[:, 0:nblk * tp_rows].rearrange(
                                "p (j n) -> p j n", n=tp_rows)[
                                :, g0:g0 + ng, t * 128:(t + 1) * 128]
                            cv.inc(v.tensor_copy(dst, src))
                        blk.vector(_)
                        evac_done[0] = cv.n
                    tp_done[1] = tp_done[0]
                    tp_done[0] = ct.n

                def _(g):
                    g.wait_ge(cv.sem, evac_done[0])
                    g.dma_start(out=d_dst, in_=p0_stage[:, 0:nblk * tp_rows]
                                ).then_inc(cst.sem, 16)
                blk.gpsimd(_)
                cst.n += 16
                return cst.n

            prep_weight(d_wg, d_wgt, NT, D, DI)
            prep_weight(d_wu, d_wut, NT, D, DI)
            st_prep = prep_weight(d_wd, d_wdt, KT, DI, D)

        pa = contextlib.ExitStack()
        with pa:
            def pas(name, shape, dt):
                return pa.enter_context(nc.sbuf_tensor(name, shape, dt))

            WW = pas("WW", [128, KT * 2048], BF16)
            WoB = pas("WoB", [128, KT * 1024], BF16)
            LNXT = pas("LNXT", [128, KT * 2048], BF16)
            QT = pas("QT", [128, KT * 2048], BF16)
            KTb = pas("KTb", [128, KT * 2048], BF16)
            V = pas("V", [128, SB * 1024], BF16)
            EK = [pas(f"EK{i}", [128, 1024], BF16) for i in range(2)]
            RC = pas("RC", [128, 512], F32)
            xbuf = [pas(f"xbuf{i}", [128, D], F32) for i in range(2)]
            aq_bm = pas("aq_bm", [128, D // 16], F32)
            aq_sc = pas("aq_sc", [128, D // 16], F32)
            aq_rs = pas("aq_rs", [128, D // 16], F32)
            aq_r2 = pas("aq_r2", [128, D // 16], F32)
            aq_sb = pas("aq_sb", [128, D // 16], BF16)
            x3buf = [pas(f"x3buf{i}", [128, D], F32) for i in range(2)]
            lnx = [pas(f"lnx{i}", [128, D], BF16) for i in range(2)]
            stat = pas("stat", [128, 40], F32)

            def _(g):
                g.wait_ge(cst.sem, st_prep)
                for k in range(KT):
                    g.dma_start(out=WoB[:, k * 1024:(k + 1) * 1024],
                                in_=d_wo[k * 128:(k + 1) * 128, :]
                                ).then_inc(cwl.sem, 16)
            blk.gpsimd(_)
            cwl.n += 16 * 8

            dbg_gate = [0]
            prev_a2_cs = [0]
            prev_lnx_guard = [0]
            prev_ln2_end = [0]
            prev_xq_store = [(0, 0)]
            x2_store_all = {}
            xq_store_all = {}
            prev_t_end = 0
            prev_v_end = 0
            prev_qkv_t = 0
            prev_store_last = (0, 0)

            for b in range(n_batch):
                bps = contextlib.ExitStack()
                with bps:
                    pq = [bps.enter_context(nc.psum_tensor(
                        f"pq{b}_{i}", [128, 1024], F32)) for i in range(3)]
                    tp2 = [bps.enter_context(nc.psum_tensor(
                        f"tp{b}_{i}", [128, 1024], BF16)) for i in range(2)]

                    def _(g, b=b):
                        if b > 0:
                            g.wait_ge(ct.sem, prev_qkv_t)
                        for k in range(KT):
                            g.dma_start(out=WW[:, k * 2048:k * 2048 + 1024],
                                        in_=d_wq[k * 128:(k + 1) * 128, :]
                                        ).then_inc(cwl.sem, 16)
                    blk.gpsimd(_)
                    cwl.n += 16 * KT
                    n_wq = cwl.n

                    norm_done = {}
                    tpos_done = {}
                    lnxt_evac = {}
                    for sb in range(SB):
                        r = sb % 2

                        def _(sy, b=b, sb=sb, r=r):
                            if sb >= 2:
                                sy.wait_ge(cs.sem, norm_done[sb - 2])
                            if b == 0 and sb == 0:
                                sy.wait_ge(cst.sem, st_prep)
                            if b > 0 and sb < 2:
                                sy.wait_ge(cs.sem, prev_ln2_end[0])
                                sy.wait_ge(cv.sem, prev_lnx_guard[0])
                            sy.dma_start(
                                out=xbuf[r][:],
                                in_=d_x[b, sb * 128:(sb + 1) * 128, :]
                            ).then_inc(cxl[r].sem, 16)
                        blk.sync(_)
                        cxl[r].n += 16
                        n_ldx = cxl[r].n

                        sq_w = []
                        if b > 0 and sb < 2:
                            sq_w.append((cso[0], prev_store_last[0]))
                            sq_w.append((cso[1], prev_store_last[1]))
                        nrm_w = []
                        if sb >= 2:
                            nrm_w.append((ct, tpos_done[sb - 2]))
                        if b > 0 and sb < 2:
                            nrm_w.append((cv, prev_lnx_guard[0]))
                        norm_done[sb] = emit_ln(
                            xbuf[r][:], x3buf[r][:], lnx[r][:], stat, r * 10,
                            wait_load=[(cxl[r], n_ldx)],
                            col_guard=(norm_done[sb - 2] if sb >= 2 else 0),
                            sq_waits=sq_w, norm_waits=nrm_w)

                        def _(te, r=r, sb=sb, b=b):
                            te.wait_ge(cs.sem, norm_done[sb])
                            te.wait_ge(cv.sem, n_consts)
                            if sb >= 2:
                                te.wait_ge(cv.sem, lnxt_evac[sb - 2])
                            if b > 0 and sb == 0:
                                te.wait_ge(cv.sem, prev_v_end)
                            last = None
                            for j in range(KT):
                                last = te.transpose(
                                    tp2[sb % 2][:, j * 128:(j + 1) * 128],
                                    lnx[r][:, j * 128:(j + 1) * 128], idb[:])
                            ct.inc(last)
                        blk.tensor(_)
                        tpos_done[sb] = ct.n

                        def _(v, sb=sb, b=b):
                            v.wait_ge(ct.sem, tpos_done[sb])
                            src = tp2[sb % 2][:].rearrange(
                                "p (j c) -> p j c", c=128)
                            dst = LNXT[:].rearrange(
                                "p (j q) -> p j q", q=2048)[
                                :, :, sb * 128:(sb + 1) * 128]
                            cv.inc(v.tensor_copy(dst, src))
                        blk.vector(_)
                        lnxt_evac[sb] = cv.n
                    n_lnxt_all = cv.n

                    pq_evac = []
                    uidx = [0]

                    def qkv_unit(stat_fn, rhs_fn, out_ap):
                        u = uidx[0] % 3
                        myu = uidx[0]
                        my_attw = n_attw

                        def _(te):
                            te.wait_ge(cv.sem, n_lnxt_all)
                            te.wait_ge(cwl.sem, my_attw)
                            if myu >= 3:
                                te.wait_ge(cv.sem, pq_evac[myu - 3])
                            last = None
                            for k in range(KT):
                                lhsT = stat_fn(k)
                                for c in range(2):
                                    last = te.matmul(
                                        pq[u][:, c * 512:(c + 1) * 512],
                                        lhsT, rhs_fn(k, c),
                                        start=(k == 0), stop=(k == KT - 1))
                            ct.inc(last)
                        blk.tensor(_)
                        n_mm = ct.n

                        def _(v, u=u, out_ap=out_ap, n_mm=n_mm):
                            v.wait_ge(ct.sem, n_mm)
                            cv.inc(v.tensor_copy(out_ap, pq[u][:]))
                        blk.vector(_)
                        pq_evac.append(cv.n)
                        uidx[0] += 1

                    n_attw = n_wq
                    for j in range(KT):
                        for half in range(2):
                            qkv_unit(
                                (lambda k, j=j:
                                 WW[:, k * 2048 + j * 128:
                                    k * 2048 + (j + 1) * 128]),
                                (lambda k, c, half=half:
                                 LNXT[:, k * 2048 + half * 1024 + c * 512:
                                      k * 2048 + half * 1024 + (c + 1) * 512]),
                                QT[:, j * 2048 + half * 1024:
                                   j * 2048 + (half + 1) * 1024])
                    n_q_mm = ct.n

                    def _(g, n_q_mm=n_q_mm):
                        g.wait_ge(ct.sem, n_q_mm)
                        for k in range(KT):
                            g.dma_start(out=WW[:, k * 2048:(k + 1) * 2048],
                                        in_=d_wkv[k * 128:(k + 1) * 128, :]
                                        ).then_inc(cwl.sem, 16)
                    blk.gpsimd(_)
                    cwl.n += 16 * KT
                    n_attw = cwl.n

                    for j in range(KT):
                        for half in range(2):
                            qkv_unit(
                                (lambda k, j=j:
                                 WW[:, k * 2048 + j * 128:
                                    k * 2048 + (j + 1) * 128]),
                                (lambda k, c, half=half:
                                 LNXT[:, k * 2048 + half * 1024 + c * 512:
                                      k * 2048 + half * 1024 + (c + 1) * 512]),
                                KTb[:, j * 2048 + half * 1024:
                                    j * 2048 + (half + 1) * 1024])
                    for t in range(SB):
                        qkv_unit(
                            (lambda k, t=t:
                             LNXT[:, k * 2048 + t * 128:
                                  k * 2048 + (t + 1) * 128]),
                            (lambda k, c:
                             WW[:, k * 2048 + 1024 + c * 512:
                                k * 2048 + 1024 + (c + 1) * 512]),
                            V[:, t * 1024:(t + 1) * 1024])
                    n_qkv_t = ct.n
                    n_qkv_v = cv.n

                    if debug and b == 0:
                        def _(g):
                            g.wait_ge(cv.sem, n_qkv_v)
                            g.dma_start(out=d_dbg["lnxt"], in_=LNXT[:]
                                        ).then_inc(cst.sem, 16)
                            g.dma_start(out=d_dbg["qt"], in_=QT[:]
                                        ).then_inc(cst.sem, 16)
                            g.dma_start(out=d_dbg["ktb"], in_=KTb[:]
                                        ).then_inc(cst.sem, 16)
                            g.dma_start(out=d_dbg["v"], in_=V[:]
                                        ).then_inc(cst.sem, 16)
                        blk.gpsimd(_)
                        cst.n += 64
                        dbg_gate[0] = cst.n

                ba2 = contextlib.ExitStack()
                with ba2:
                    sps = [ba2.enter_context(nc.psum_tensor(
                        f"sps{b}_{i}", [128, 1024], F32)) for i in range(3)]
                    pvo = ba2.enter_context(
                        nc.psum_tensor(f"pvo{b}", [128, 512], F32))
                    pvd = ba2.enter_context(
                        nc.psum_tensor(f"pvd{b}", [128, 512], F32))

                    s_done = {}
                    exp_done = {}
                    pv_done = {}
                    ot_write = [0]
                    for p in range(NPAIR):
                        key0 = p * 64
                        cs_pre = cs.n
                        for i in range(64):
                            exp_done[key0 + i] = cs_pre + i + 1
                        for qc in range(4):
                            def _(te, p=p, qc=qc):
                                def emit_s(kb):
                                    key = p * 64 + qc * SB + kb
                                    sp = sps[kb % 3]
                                    if key == 0:
                                        te.wait_ge(ct.sem, n_qkv_t)
                                        te.wait_ge(cv.sem, n_qkv_v)
                                        if prev_a2_cs[0] > 0:
                                            te.wait_ge(cs.sem, prev_a2_cs[0])
                                    if key >= 3:
                                        te.wait_ge(cs.sem, exp_done[key - 3])
                                    la = KTb[0:64, p * 2048 + kb * 128:
                                             p * 2048 + (kb + 1) * 128]
                                    ra = QT[0:64, p * 2048 + qc * 512:
                                            p * 2048 + (qc + 1) * 512]
                                    lb = KTb[64:128, p * 2048 + kb * 128:
                                             p * 2048 + (kb + 1) * 128]
                                    rb = QT[64:128, p * 2048 + qc * 512:
                                            p * 2048 + (qc + 1) * 512]
                                    te.matmul(sp[:, 0:512], la, ra,
                                              start=True, stop=True,
                                              tile_position=(0, 0))
                                    ct.inc(te.matmul(sp[:, 512:1024], lb, rb,
                                                     start=True, stop=True,
                                                     tile_position=(64, 0)))
                                    s_done[key] = ct.n

                                def emit_pv(kb):
                                    key = p * 64 + qc * SB + kb
                                    te.wait_ge(cs.sem, exp_done[key])
                                    if kb == 0:
                                        te.wait_ge(cv.sem, ot_write[0])
                                    v3 = V[:].rearrange(
                                        "p (t c) -> p t c", c=1024)
                                    va = v3[:, kb, (2 * p) * 64:
                                            (2 * p) * 64 + 64]
                                    vb = v3[:, kb, (2 * p + 1) * 64:
                                            (2 * p + 1) * 64 + 64]
                                    ea = EK[kb % 2][:, 0:512]
                                    eb = EK[kb % 2][:, 512:1024]
                                    st_, sp_ = (kb == 0), (kb == SB - 1)
                                    te.matmul(pvo[0:64, :], va, ea,
                                              start=st_, stop=sp_,
                                              tile_position=(0, 0),
                                              skip_group_check=True)
                                    te.matmul(pvo[64:128, :], vb, eb,
                                              start=st_, stop=sp_,
                                              tile_position=(0, 64),
                                              skip_group_check=True)
                                    te.matmul(pvd[0:64, :], ones64[:], ea,
                                              start=st_, stop=sp_,
                                              tile_position=(0, 0),
                                              skip_group_check=True)
                                    ct.inc(te.matmul(pvd[64:128, :],
                                                     ones64[:], eb,
                                                     start=st_, stop=sp_,
                                                     tile_position=(0, 64),
                                                     skip_group_check=True))
                                    pv_done[key] = ct.n

                                for kb in range(SB):
                                    emit_s(kb)
                                    if kb >= 2:
                                        emit_pv(kb - 2)
                                emit_pv(SB - 2)
                                emit_pv(SB - 1)
                            blk.tensor(_)
                            n_pv = ct.n

                            def _(v, p=p, qc=qc, n_pv=n_pv):
                                v.wait_ge(ct.sem, n_pv)
                                if dbg_gate[0]:
                                    v.wait_ge(cst.sem, dbg_gate[0])
                                    dbg_gate[0] = 0
                                n_rc = cv.inc(v.reciprocal(RC[:], pvd[:]))
                                v.wait_ge(cv.sem, n_rc)
                                base = p * 2048 + qc * 512
                                v.tensor_tensor(LNXT[0:64, base:base + 512],
                                                pvo[0:64, :], RC[0:64, :],
                                                op=ALU.mult)
                                cv.inc(v.tensor_tensor(
                                    LNXT[64:128, base:base + 512],
                                    pvo[64:128, :], RC[64:128, :],
                                    op=ALU.mult))
                            blk.vector(_)
                            ot_write[0] = cv.n

                        def _(s, p=p, cs_pre=cs_pre):
                            load_tbl(s, SET_LNEXP)
                            for qc in range(4):
                                for kb in range(SB):
                                    key = p * 64 + qc * SB + kb
                                    s.wait_ge(ct.sem, s_done[key])
                                    if key < 2 and prev_xq_store[0][kb] > 0:
                                        s.wait_ge(cqo[kb].sem,
                                                  prev_xq_store[0][kb])
                                    if key >= 2:
                                        s.wait_ge(ct.sem, pv_done[key - 2])
                                    cs.inc(s.activation(
                                        EK[kb % 2][:], sps[kb % 3][:],
                                        AF.Exp, bias=cbias[:, 1:2],
                                        scale=0.125))
                                    assert cs.n == exp_done[key], \
                                        (cs.n, exp_done[key])
                        blk.scalar(_)
                    n_ot_all = cv.n

                if debug and b == 0:
                    def _(g):
                        g.wait_ge(cv.sem, n_ot_all)
                        g.dma_start(out=d_dbg["ot"], in_=LNXT[:]
                                    ).then_inc(cst.sem, 16)
                    blk.gpsimd(_)
                    cst.n += 16
                    dbg_a3 = [True]

                prev_a2_cs[0] = cs.n

                ba3 = contextlib.ExitStack()
                with ba3:
                    wop = [ba3.enter_context(nc.psum_tensor(
                        f"wop{b}_{i}", [128, 1024], F32)) for i in range(2)]
                    x2_evac = {}
                    x2_store = {}
                    xq_store = {}
                    ln2a_done = {}
                    ln2a_qd = {}
                    for sb in range(SB):
                        r = sb % 2

                        def _(sy, b=b, sb=sb, r=r):
                            if sb >= 2:
                                sy.wait_ge(cso[r].sem, x2_store[sb - 2])
                                sy.wait_ge(cv.sem, ln2a_qd[sb - 2])
                            else:
                                sy.wait_ge(cs.sem, norm_done[SB - 1])
                            sy.dma_start(
                                out=x3buf[r][:],
                                in_=d_x[b, sb * 128:(sb + 1) * 128, :]
                            ).then_inc(c3l[r].sem, 16)
                        blk.sync(_)
                        c3l[r].n += 16
                        n_ldx3 = c3l[r].n

                        def _(te, sb=sb, r=r):
                            te.wait_ge(cv.sem, n_ot_all)
                            if sb >= 2:
                                te.wait_ge(cv.sem, x2_evac[sb - 2])
                            last = None
                            for j in range(KT):
                                statp = LNXT[:, j * 2048 + sb * 128:
                                             j * 2048 + (sb + 1) * 128]
                                for c in range(2):
                                    last = te.matmul(
                                        wop[r][:, c * 512:(c + 1) * 512],
                                        statp,
                                        WoB[:, j * 1024 + c * 512:
                                            j * 1024 + (c + 1) * 512],
                                        start=(j == 0), stop=(j == KT - 1))
                            ct.inc(last)
                        blk.tensor(_)
                        n_wo = ct.n

                        def _(v, sb=sb, r=r, n_wo=n_wo, n_ldx3=n_ldx3):
                            v.wait_ge(ct.sem, n_wo)
                            v.wait_ge(c3l[r].sem, n_ldx3)
                            v.tensor_tensor(x3buf[r][:], x3buf[r][:],
                                            wop[r][:], op=ALU.add)
                            cv.inc(v.tensor_tensor(x3buf[r][:], x3buf[r][:],
                                                   bo_bc[:], op=ALU.add))
                        blk.vector(_)
                        x2_evac[sb] = cv.n

                        def _(g, b=b, sb=sb, r=r):
                            g.wait_ge(cv.sem, x2_evac[sb])
                            g.dma_start(
                                out=d_x2[b, sb * 128:(sb + 1) * 128, :],
                                in_=x3buf[r][:]).then_inc(cso[r].sem, 16)
                        blk.gpsimd(_)
                        cso[r].n += 16
                        x2_store[sb] = cso[r].n

                        nrm2_w = []
                        if b > 0 and sb < 2:
                            nrm2_w.append((cv, prev_lnx_guard[0]))
                        ln2a_done[sb] = emit_ln(
                            x3buf[r][:], xbuf[0][:], xbuf[1][:], stat, 20,
                            wait_load=[(cv, x2_evac[sb])],
                            col_guard=(ln2a_done[sb - 1] if sb >= 1 else 0),
                            sq_waits=([(cv, ln2a_qd[sb - 1])] if sb >= 1
                                      else []),
                            norm_waits=nrm2_w,
                            sign_dst=(lnx[1][:], xbuf[0][:]))
                        qd = emit_quant(
                            xbuf[1][:], EK[r][:], D,
                            xbuf[0], lnx[0], lnx[1],
                            aq_bm, aq_sc, aq_rs, aq_r2, aq_sb,
                            wait_v=[(cs, ln2a_done[sb])] +
                                   ([] if sb < 2 else
                                    [(cqo[r], xq_store[sb - 2])]),
                            ext_prep=ln2a_done[sb])
                        ln2a_qd[sb] = qd

                        def _(g, b=b, sb=sb, r=r, qd=qd):
                            g.wait_ge(cv.sem, qd)
                            g.dma_start(
                                out=d_xq[b, sb * 128:(sb + 1) * 128, :],
                                in_=EK[r][:]).then_inc(cqo[r].sem, 16)
                        blk.gpsimd(_)
                        cqo[r].n += 16
                        xq_store[sb] = cqo[r].n
                    if debug and b == 0:
                        def _(g):
                            g.wait_ge(cv.sem, ln2a_qd[SB - 1])
                            g.dma_start(out=d_dbg["aqbm"], in_=aq_bm[:]
                                        ).then_inc(cst.sem, 16)
                            g.dma_start(out=d_dbg["axln"], in_=xbuf[1][:]
                                        ).then_inc(cst.sem, 16)
                            g.dma_start(out=d_dbg["asq"], in_=xbuf[0][:]
                                        ).then_inc(cst.sem, 16)
                            g.dma_start(out=d_dbg["ata"], in_=lnx[0][:]
                                        ).then_inc(cst.sem, 16)
                            g.dma_start(out=d_dbg["atsg"], in_=lnx[1][:]
                                        ).then_inc(cst.sem, 16)
                        blk.gpsimd(_)
                        cst.n += 80
                    prev_lnx_guard[0] = ln2a_qd[SB - 1]
                    prev_ln2_end[0] = ln2a_done[SB - 1]
                    prev_xq_store[0] = (cqo[0].n, cqo[1].n)
                    prev_t_end = ct.n
                    prev_v_end = cv.n
                    prev_qkv_t = n_qkv_t
                    prev_store_last = (cso[0].n, cso[1].n)
                    for _sb in range(SB):
                        x2_store_all[b * SB + _sb] = (_sb % 2, x2_store[_sb])
                        xq_store_all[b * SB + _sb] = (_sb % 2, xq_store[_sb])

        pa_t_end, pa_v_end, pa_s_end, pa_st_end = ct.n, cv.n, cs.n, cst.n
        pa_so = (cso[0].n, cso[1].n)
        pa_qo = (cqo[0].n, cqo[1].n)

        pf = contextlib.ExitStack()
        with pf:
            def pfs(name, shape, dt):
                return pf.enter_context(nc.sbuf_tensor(name, shape, dt))

            WGT = pfs("WGT", [128, KT * DI], BF16)
            WUT = pfs("WUT", [128, KT * DI], BF16)
            WDT = pfs("WDT", [128, NT * D], BF16)
            x2b = [pfs(f"x2b{i}", [128, D], F32) for i in range(2)]
            xqb = [pfs(f"xqb{i}", [128, D], BF16) for i in range(2)]
            XqT = [pfs("XqTa", [128, D], BF16)] * 2
            hbuf = pfs("hbuf", [128, DI], F32)
            Hq = pfs("Hq", [128, DI], BF16)
            HqT = [pfs("HqTa", [128, DI], BF16)] * 2
            q_tv = pfs("q_tv", [128, DI], F32)
            q_ta = pfs("q_ta", [128, DI], BF16)
            q_sg = pfs("q_sg", [128, DI], BF16)
            q_bm = pfs("q_bm", [128, DI // 16], F32)
            q_r2 = pfs("q_r2", [128, DI // 16], F32)
            q_slb = pfs("q_slb", [128, DI // 16], BF16)
            q_sc = pfs("q_sc", [128, DI // 16], F32)
            q_rs = pfs("q_rs", [128, DI // 16], F32)
            stat2 = pfs("stat2", [128, 24], F32)

            gps = [pf.enter_context(nc.psum_tensor(
                f"gps{i}", [128, 512], F32)) for i in range(2)]
            ups = [pf.enter_context(nc.psum_tensor(
                f"ups{i}", [128, 512], F32)) for i in range(2)]
            tps = [pf.enter_context(nc.psum_tensor(
                f"tps{i}", [128, 1024], BF16)) for i in range(2)]
            dps = pf.enter_context(nc.psum_tensor("dps", [128, 1024], F32))

            def _(g):
                g.wait_ge(ct.sem, pa_t_end)
                g.wait_ge(cv.sem, pa_v_end)
                g.wait_ge(cs.sem, pa_s_end)
                g.wait_ge(cst.sem, pa_st_end)
                g.wait_ge(cso[0].sem, pa_so[0])
                g.wait_ge(cso[1].sem, pa_so[1])
                g.wait_ge(cqo[0].sem, pa_qo[0])
                g.wait_ge(cqo[1].sem, pa_qo[1])
                g.dma_start(out=WGT[:], in_=d_wgt).then_inc(cwl.sem, 16)
                g.dma_start(out=WUT[:], in_=d_wut).then_inc(cwl.sem, 16)
                g.dma_start(out=WDT[:], in_=d_wdt).then_inc(cwl.sem, 16)
            blk.gpsimd(_)
            cwl.n += 48
            n_ffnw = cwl.n

            out_store = {}
            xq_tp_done = {}
            xqt_evac = {}
            hq_tp_done = {}
            hqt_last = {}
            gu_last = {}
            guk_done = {}
            h_done = {}
            qd_h_hist = {}
            down_done = {}
            x2_used = {}

            for b in range(n_batch):
                for sb in range(SB):
                    gi = b * SB + sb
                    r = sb % 2

                    def _(sy, b=b, sb=sb, r=r, gi=gi):
                        if gi >= 2:
                            sy.wait_ge(cfo[r].sem, out_store[gi - 2])
                        slot, val = x2_store_all[gi]
                        sy.wait_ge(cso[slot].sem, val)
                        sy.dma_start(
                            out=x2b[r][:],
                            in_=d_x2[b, sb * 128:(sb + 1) * 128, :]
                        ).then_inc(cfl[r].sem, 16)
                    blk.sync(_)
                    cfl[r].n += 16
                    n_ldx2 = cfl[r].n

                    def _(sy, b=b, sb=sb, gi=gi):
                        slot = gi % 2
                        if gi >= 2:
                            sy.wait_ge(ct.sem, xq_tp_done[gi - 2])
                        qslot, qval = xq_store_all[gi]
                        sy.wait_ge(cqo[qslot].sem, qval)
                        sy.dma_start(
                            out=xqb[slot][:],
                            in_=d_xq[b, sb * 128:(sb + 1) * 128, :]
                        ).then_inc(cql[slot].sem, 16)
                    blk.sync(_)
                    cql[gi % 2].n += 16
                    n_ldxq = cql[gi % 2].n

                    def _(te, gi=gi, n_ldxq=n_ldxq):
                        te.wait_ge(cql[gi % 2].sem, n_ldxq)
                        if gi == 0:
                            te.wait_ge(cwl.sem, n_ffnw)
                        if gi >= 1:
                            te.wait_ge(cv.sem, xqt_evac[gi - 1])
                        last = None
                        for j in range(KT):
                            last = te.transpose(
                                tps[0][:, j * 128:(j + 1) * 128],
                                xqb[gi % 2][:, j * 128:(j + 1) * 128], idb[:])
                        ct.inc(last)
                    blk.tensor(_)
                    xq_tp_done[gi] = ct.n
                    n_xqtp = ct.n

                    def _(v, gi=gi, r=r, n_xqtp=n_xqtp):
                        v.wait_ge(ct.sem, n_xqtp)
                        if gi >= 1:
                            v.wait_ge(ct.sem, gu_last[gi - 1])
                        cv.inc(v.tensor_copy(XqT[r][:], tps[0][:]))
                    blk.vector(_)
                    xqt_evac[gi] = cv.n
                    n_xqt = cv.n

                    gu_rec = {}
                    cv_pre_tt = cv.n
                    for ci in range(len(N_CHUNKS)):
                        h_done[gi * len(N_CHUNKS) + ci] = cv_pre_tt + ci + 1
                    for ci, (nc0, nc1) in enumerate(N_CHUNKS):
                        cw = nc1 - nc0
                        pr = ci % 2
                        kk = gi * len(N_CHUNKS) + ci

                        def _(te, r=r, nc0=nc0, nc1=nc1, cw=cw, pr=pr, kk=kk,
                              n_xqt=n_xqt, gi=gi, ci=ci):
                            te.wait_ge(cv.sem, n_xqt)
                            if kk >= 2:
                                te.wait_ge(cv.sem, h_done[kk - 2])
                            for k in range(KT):
                                te.matmul(gps[pr][:, 0:cw],
                                          XqT[r][:, k * 128:(k + 1) * 128],
                                          WGT[:, k * DI + nc0:k * DI + nc1],
                                          start=(k == 0), stop=(k == KT - 1))
                            last = None
                            for k in range(KT):
                                last = te.matmul(
                                    ups[pr][:, 0:cw],
                                    XqT[r][:, k * 128:(k + 1) * 128],
                                    WUT[:, k * DI + nc0:k * DI + nc1],
                                    start=(k == 0), stop=(k == KT - 1))
                            ct.inc(last)
                        blk.tensor(_)
                        guk_done[kk] = ct.n
                        gu_rec[ci] = ct.n
                    gu_last[gi] = ct.n

                    cs_pre_silu = cs.n

                    def _(s, gi=gi, cv_pre_tt=cv_pre_tt):
                        load_tbl(s, SET_SILU)
                        if gi >= 1:
                            s.wait_ge(cv.sem, qd_h_hist[gi - 1])
                        for ci in range(len(N_CHUNKS)):
                            nc0, nc1 = N_CHUNKS[ci]
                            cw = nc1 - nc0
                            pr = ci % 2
                            s.wait_ge(ct.sem, gu_rec[ci])
                            if ci >= 2:
                                s.wait_ge(cv.sem, cv_pre_tt + ci - 1)
                            cs.inc(s.activation(hbuf[:, nc0:nc0 + cw],
                                                gps[pr][:, 0:cw], AF.Silu))
                        s.wait_ge(cv.sem, cv_pre_tt + len(N_CHUNKS))
                        if last_qd[0] > 0:
                            s.wait_ge(cv.sem, last_qd[0])
                        s.activation(q_sg[:, 0:DI], hbuf[:], AF.Sign)
                        cs.inc(s.activation(q_tv[:], hbuf[:], AF.Square))
                    blk.scalar(_)
                    n_sgn_h = cs.n

                    for ci, (nc0, nc1) in enumerate(N_CHUNKS):
                        cw = nc1 - nc0
                        pr = ci % 2

                        def _(v, nc0=nc0, cw=cw, pr=pr, ci=ci,
                              cs_pre_silu=cs_pre_silu):
                            v.wait_ge(cs.sem, cs_pre_silu + ci + 1)
                            cv.inc(v.tensor_tensor(hbuf[:, nc0:nc0 + cw],
                                                   hbuf[:, nc0:nc0 + cw],
                                                   ups[pr][:, 0:cw],
                                                   op=ALU.mult))
                        blk.vector(_)
                        h_done[gi * len(N_CHUNKS) + ci] = cv.n
                    assert cv.n == cv_pre_tt + len(N_CHUNKS)
                    n_h_all = cv.n

                    qd_h = emit_quant(
                        hbuf[:], Hq[:], DI,
                        q_tv, q_ta, q_sg, q_bm, q_sc, q_rs,
                        q_r2, q_slb,
                        wait_v=[(cv, n_h_all),
                                (ct, hq_tp_done.get(gi - 1, 0))],
                        ext_prep=n_sgn_h)
                    qd_h_hist[gi] = qd_h

                    hqt_evacs = []
                    for g0 in range(0, NT, 8):
                        ng = min(8, NT - g0)

                        def _(te, g0=g0, ng=ng, qd_h=qd_h, gi=gi):
                            te.wait_ge(cv.sem, qd_h)
                            if hqt_evacs:
                                te.wait_ge(cv.sem, hqt_evacs[-1])
                            elif gi >= 1:
                                te.wait_ge(cv.sem, hqt_last[gi - 1])
                            last = None
                            for j in range(ng):
                                last = te.transpose(
                                    tps[1][:, j * 128:(j + 1) * 128],
                                    Hq[:, (g0 + j) * 128:(g0 + j + 1) * 128],
                                    idb[:])
                            ct.inc(last)
                        blk.tensor(_)
                        n_tp = ct.n

                        def _(v, g0=g0, ng=ng, r=r, n_tp=n_tp, gi=gi):
                            v.wait_ge(ct.sem, n_tp)
                            if g0 == 0 and gi >= 1:
                                v.wait_ge(ct.sem, down_done[gi - 1])
                            cv.inc(v.tensor_copy(
                                HqT[r][:, g0 * 128:(g0 + ng) * 128],
                                tps[1][:, 0:ng * 128]))
                        blk.vector(_)
                        hqt_evacs.append(cv.n)
                    hq_tp_done[gi] = ct.n
                    hqt_last[gi] = cv.n
                    n_hqt = cv.n

                    def _(te, r=r, n_hqt=n_hqt, gi=gi):
                        te.wait_ge(cv.sem, n_hqt)
                        if gi >= 1:
                            te.wait_ge(cv.sem, x2_used[gi - 1])
                        last = None
                        for t in range(NT):
                            for c in range(2):
                                last = te.matmul(
                                    dps[:, c * 512:(c + 1) * 512],
                                    HqT[r][:, t * 128:(t + 1) * 128],
                                    WDT[:, t * D + c * 512:
                                        t * D + (c + 1) * 512],
                                    start=(t == 0), stop=(t == NT - 1))
                        ct.inc(last)
                    blk.tensor(_)
                    down_done[gi] = ct.n
                    n_down = ct.n

                    def _(v, r=r, n_down=n_down):
                        v.wait_ge(ct.sem, n_down)
                        cv.inc(v.tensor_tensor(x2b[r][:], x2b[r][:], dps[:],
                                               op=ALU.add))
                    blk.vector(_)
                    x2_used[gi] = cv.n
                    n_out = cv.n

                    def _(g, b=b, sb=sb, r=r, n_out=n_out):
                        g.wait_ge(cv.sem, n_out)
                        g.dma_start(
                            out=d_out[b, sb * 128:(sb + 1) * 128, :],
                            in_=x2b[r][:]).then_inc(cfo[r].sem, 16)
                    blk.gpsimd(_)
                    cfo[r].n += 16
                    out_store[gi] = cfo[r].n

            if debug:
                def _(g):
                    g.wait_ge(cwl.sem, n_ffnw)
                    g.dma_start(out=d_dbg["wgt"], in_=WGT[:]
                                ).then_inc(cst.sem, 16)
                    g.dma_start(out=d_dbg["wut"], in_=WUT[:]
                                ).then_inc(cst.sem, 16)
                    g.dma_start(out=d_dbg["wdt"], in_=WDT[:]
                                ).then_inc(cst.sem, 16)
                    g.wait_ge(cv.sem, cv.n)
                    g.wait_ge(cs.sem, cs.n)
                    last_r = (SB - 1) % 2
                    g.dma_start(out=d_dbg["xq"], in_=xqb[(SB - 1) % 2][:]
                                ).then_inc(cst.sem, 16)
                    g.dma_start(out=d_dbg["hq"], in_=Hq[:]
                                ).then_inc(cst.sem, 16)
                    g.dma_start(out=d_dbg["hbuf"], in_=hbuf[:]
                                ).then_inc(cst.sem, 16)
                    g.dma_start(out=d_dbg["xln2"], in_=x2b[last_r][:]
                                ).then_inc(cst.sem, 16)
                    g.dma_start(out=d_dbg["xqt"], in_=XqT[last_r][:]
                                ).then_inc(cst.sem, 16)
                    g.dma_start(out=d_dbg["hqt"], in_=HqT[last_r][:]
                                ).then_inc(cst.sem, 16)
                    g.dma_start(out=d_dbg["x2"],
                                in_=d_x2.rearrange("b s d -> (b s) d")
                                ).then_inc(cst.sem, 16)
                blk.gpsimd(_)
                cst.n += 160

            def _(g):
                g.wait_ge(cst.sem, cst.n)
                g.wait_ge(cso[0].sem, cso[0].n)
                g.wait_ge(cso[1].sem, cso[1].n)
                g.wait_ge(cfo[0].sem, cfo[0].n)
                g.wait_ge(cfo[1].sem, cfo[1].n)
                g.wait_ge(cqo[0].sem, cqo[0].n)
                g.wait_ge(cqo[1].sem, cqo[1].n)
            blk.gpsimd(_)

    return nc


def kernel(L0_attn_hidden_states, L0_attn_Wq, L0_attn_Wkv, L0_attn_Wo,
           L0_attn_bo, L0_ffn_w_gate, L0_ffn_w_up, L0_ffn_w_down):
    x = np.ascontiguousarray(L0_attn_hidden_states, dtype=np.float32)
    wq = np.ascontiguousarray(L0_attn_Wq, dtype=np.float32)
    wkv = np.ascontiguousarray(L0_attn_Wkv, dtype=np.float32)
    wo = np.ascontiguousarray(L0_attn_Wo, dtype=np.float32)
    bo = np.ascontiguousarray(L0_attn_bo, dtype=np.float32)
    wg = np.ascontiguousarray(L0_ffn_w_gate, dtype=np.float32)
    wu = np.ascontiguousarray(L0_ffn_w_up, dtype=np.float32)
    wd = np.ascontiguousarray(L0_ffn_w_down, dtype=np.float32)

    bo_t = np.ascontiguousarray(np.broadcast_to(bo[None, :], (128, D)))
    ident = np.eye(128, dtype=np.float32)

    nc = build()
    in_maps = []
    for c in range(N_CORES):
        in_maps.append({
            "x": np.ascontiguousarray(x[c * N_BATCH:(c + 1) * N_BATCH]),
            "wq": wq, "wkv": wkv, "wo": wo, "bo_t": bo_t,
            "wg": wg, "wu": wu, "wd": wd, "ident": ident,
        })
    res = run_bass_kernel_spmd(nc, in_maps, core_ids=list(range(N_CORES)))
    return np.concatenate([r["out"] for r in res.results], axis=0)


# version 12
# speedup vs baseline: 1.1215x; 1.1215x over previous
import contextlib
import numpy as np

import concourse.bass as bass
import concourse.mybir as mybir
from concourse.bass_utils import run_bass_kernel_spmd
from concourse.hw_specs import get_activation_tables

F32 = mybir.dt.float32
BF16 = mybir.dt.bfloat16
AF = mybir.ActivationFunctionType
ALU = mybir.AluOpType
AX = mybir.AxisListType

N_CORES = 8
N_BATCH = 4
S = 2048
D = 1024
NH = 16
DH = 64
DI = 2816
SB = S // 128
KT = D // 128
NT = DI // 128
NPAIR = NH // 2
EPS = 1e-5
N_CHUNKS = [(c, min(c + 512, DI)) for c in range(0, DI, 512)]


class Ctr:
    def __init__(self, nc, es, name):
        self.sem = es.enter_context(nc.semaphore(name))
        self.n = 0

    def inc(self, instr, k=1):
        instr.then_inc(self.sem, k)
        self.n += k
        return self.n


def build(n_batch=N_BATCH, debug=False):
    nc = bass.Bass("TRN2", target_bir_lowering=False,
                   detect_race_conditions=False)
    tbl_ids = {name: i for i, name in
               enumerate(get_activation_tables(nc.m.arch))}
    SET_LNEXP = tbl_ids["natural_log_exp_and_others"]
    SET_SILU = tbl_ids["silu_and_others"]

    def load_tbl(eng, set_id):
        eng.add_instruction(mybir.InstLoadActFuncSet(
            name=nc.get_next_instruction_name(), act_func_set_id=set_id,
            ins=[], outs=[]))

    d_x = nc.dram_tensor("x", [n_batch, S, D], F32, kind="ExternalInput").ap()
    d_wq = nc.dram_tensor("wq", [D, D], F32, kind="ExternalInput").ap()
    d_wkv = nc.dram_tensor("wkv", [D, 2 * D], F32, kind="ExternalInput").ap()
    d_wo = nc.dram_tensor("wo", [D, D], F32, kind="ExternalInput").ap()
    d_bo = nc.dram_tensor("bo_t", [128, D], F32, kind="ExternalInput").ap()
    d_wg = nc.dram_tensor("wg", [DI, D], F32, kind="ExternalInput").ap()
    d_wu = nc.dram_tensor("wu", [DI, D], F32, kind="ExternalInput").ap()
    d_wd = nc.dram_tensor("wd", [D, DI], F32, kind="ExternalInput").ap()
    d_id = nc.dram_tensor("ident", [128, 128], F32, kind="ExternalInput").ap()
    d_out = nc.dram_tensor("out", [n_batch, S, D], F32,
                           kind="ExternalOutput").ap()

    if debug:
        d_dbg = {}
        for nm, shape in (
            ("lnxt", [128, KT * 2048]), ("qt", [128, KT * 2048]),
            ("ktb", [128, KT * 2048]), ("v", [128, SB * 1024]),
            ("ot", [128, KT * 2048]), ("xq", [128, D]), ("hq", [128, DI]),
            ("hbuf", [128, DI]), ("xln2", [128, D]), ("xqt", [128, D]),
            ("aqbm", [128, D // 16]), ("axln", [128, D]), ("asq", [128, D]),
            ("ata", [128, D]), ("atsg", [128, D]),
            ("hqt", [128, DI]), ("wgt", [128, KT * DI]),
            ("wut", [128, KT * DI]), ("wdt", [128, NT * D]),
            ("x2", [n_batch * S, D]),
        ):
            d_dbg[nm] = nc.dram_tensor("dbg_" + nm, shape, F32,
                                       kind="ExternalOutput").ap()
    d_x2 = nc.dram_tensor("x2_s", [n_batch, S, D], F32, kind="Internal").ap()
    d_xq = nc.dram_tensor("xq_s", [n_batch, S, D], BF16, kind="Internal").ap()
    d_wgt = nc.dram_tensor("wgt_s", [128, KT * DI], BF16, kind="Internal").ap()
    d_wut = nc.dram_tensor("wut_s", [128, KT * DI], BF16, kind="Internal").ap()
    d_wdt = nc.dram_tensor("wdt_s", [128, NT * D], BF16, kind="Internal").ap()

    es = contextlib.ExitStack()
    with es:
        def sbuf(name, shape, dt):
            return es.enter_context(nc.sbuf_tensor(name, shape, dt))

        cin = Ctr(nc, es, "cin")
        cxl = [Ctr(nc, es, f"cxl{i}") for i in range(2)]
        c3l = [Ctr(nc, es, f"c3l{i}") for i in range(2)]
        cfl = [Ctr(nc, es, f"cfl{i}") for i in range(2)]
        cpl = [Ctr(nc, es, f"cpl{i}") for i in range(2)]
        cso = [Ctr(nc, es, f"cso{i}") for i in range(2)]
        cqo = [Ctr(nc, es, f"cqo{i}") for i in range(2)]
        cql = [Ctr(nc, es, f"cql{i}") for i in range(2)]
        cfo = [Ctr(nc, es, f"cfo{i}") for i in range(2)]
        cwl = Ctr(nc, es, "cwl")
        cst = Ctr(nc, es, "cst")
        ct = Ctr(nc, es, "ct")
        cv = Ctr(nc, es, "cv")
        cs = Ctr(nc, es, "cs")

        blk = es.enter_context(nc.Block())

        bo_bc = sbuf("bo_bc", [128, D], F32)
        idb = sbuf("idb", [128, 128], BF16)
        ones64 = sbuf("ones64", [128, 64], BF16)
        cbias = sbuf("cbias", [128, 2], F32)

        def _(g):
            g.dma_start(out=idb[:], in_=d_id).then_inc(cwl.sem, 16)
            g.dma_start(out=bo_bc[:], in_=d_bo).then_inc(cwl.sem, 16)
        blk.gpsimd(_)
        cwl.n += 32
        n_ld0 = cwl.n

        def _(v):
            v.wait_ge(cwl.sem, n_ld0)
            v.memset(ones64[:], 1.0)
            v.memset(cbias[:, 0:1], EPS)
            cv.inc(v.memset(cbias[:, 1:2], -3.0))
        blk.vector(_)
        n_consts = cv.n

        last_qd = [0]

        def emit_quant(src_ap, dst_ap, F, sq, ta, tsg, bmax, scl, rscl,
                       rscl2, sclb, wait_v=(), wait_s=(), ext_prep=None,
                       marks=None):
            G = F // 16

            if ext_prep is None:
                def _(s):
                    for ctr, n in wait_s:
                        if n > 0:
                            s.wait_ge(ctr.sem, n)
                    if last_qd[0] > 0:
                        s.wait_ge(cv.sem, last_qd[0])
                    load_tbl(s, SET_LNEXP)
                    s.activation(tsg[:, 0:F], src_ap, AF.Sign)
                    cs.inc(s.activation(sq[:, 0:F], src_ap, AF.Square))
                blk.scalar(_)
                n_prep = cs.n
            else:
                n_prep = ext_prep

            def _(v):
                for ctr, n in wait_v:
                    if n > 0:
                        v.wait_ge(ctr.sem, n)
                src3 = src_ap.rearrange("p (g w) -> p g w", w=16)
                n_red = cv.inc(v.tensor_reduce(
                    bmax[:, 0:G].rearrange("p (g o) -> p g o", o=1), src3,
                    axis=AX.X, op=ALU.max, apply_absolute_value=True))
                if marks is not None:
                    marks["bmax"] = n_red
                v.wait_ge(cv.sem, n_red)
                v.tensor_scalar(scl[:, 0:G], bmax[:, 0:G], 1.0 / 6.0, 1e-12,
                                op0=ALU.mult, op1=ALU.max)
                n_rc = cv.inc(v.reciprocal(rscl[:, 0:G], scl[:, 0:G]))
                v.wait_ge(cv.sem, n_rc)
                v.tensor_tensor(rscl2[:, 0:G], rscl[:, 0:G], rscl[:, 0:G],
                                op=ALU.mult)
                v.tensor_copy(sclb[:, 0:G], scl[:, 0:G])
                v.wait_ge(cs.sem, n_prep)
                r2b = rscl2[:, 0:G].rearrange("p (g o) -> p g o", o=1) \
                    .broadcast_to([128, G, 16])
                sqf = sq[:, 0:F]
                v.tensor_tensor(sqf.rearrange("p (g w) -> p g w", w=16),
                                sqf.rearrange("p (g w) -> p g w", w=16),
                                r2b, op=ALU.mult)
                v.tensor_copy(dst_ap, sqf)
                v.tensor_scalar(ta[:, 0:F], dst_ap, 0.0625, None,
                                op0=ALU.is_gt)
                for t in (0.5625, 1.5625, 3.0625):
                    v.scalar_tensor_tensor(ta[:, 0:F], dst_ap, t, ta[:, 0:F],
                                           op0=ALU.is_gt, op1=ALU.add)
                v.tensor_scalar(ta[:, 0:F], ta[:, 0:F], 0.5, None,
                                op0=ALU.mult)
                for t in (6.25, 12.25):
                    v.scalar_tensor_tensor(ta[:, 0:F], sqf, t, ta[:, 0:F],
                                           op0=ALU.is_gt, op1=ALU.add)
                v.tensor_scalar(dst_ap, sqf, 25.0, 2.0,
                                op0=ALU.is_gt, op1=ALU.mult)
                v.tensor_tensor(ta[:, 0:F], ta[:, 0:F], dst_ap, op=ALU.add)
                sb3 = sclb[:, 0:G].rearrange("p (g o) -> p g o", o=1) \
                    .broadcast_to([128, G, 16])
                v.tensor_tensor(tsg[:, 0:F].rearrange("p (g w) -> p g w", w=16),
                                tsg[:, 0:F].rearrange("p (g w) -> p g w", w=16),
                                sb3, op=ALU.mult)
                cv.inc(v.tensor_tensor(dst_ap, ta[:, 0:F], tsg[:, 0:F],
                                       op=ALU.mult))
            blk.vector(_)
            last_qd[0] = cv.n
            return cv.n

        def emit_ln(src_ap, scratch_ap, dst_ap, stat_t, c0, wait_load,
                    col_guard=0, sq_waits=(), norm_waits=(), sign_dst=None):
            xs, mn, nmn, ssr, nm2, nm2e, var_, lnv, rs, nb = (
                stat_t[:, c0 + i:c0 + i + 1] for i in range(10))

            def _(v):
                for ctr, n in wait_load:
                    v.wait_ge(ctr.sem, n)
                if col_guard > 0:
                    v.wait_ge(cs.sem, col_guard)
                n_red = cv.inc(v.reduce_sum(xs, src_ap, axis=AX.X))
                v.wait_ge(cv.sem, n_red)
                v.tensor_scalar(mn, xs, 1.0 / D, None, op0=ALU.mult)
                cv.inc(v.tensor_scalar(nmn, xs, -1.0 / D, None, op0=ALU.mult))
            blk.vector(_)
            n1 = cv.n

            def _(s):
                for ctr, n in wait_load:
                    s.wait_ge(ctr.sem, n)
                for ctr, n in sq_waits:
                    s.wait_ge(ctr.sem, n)
                if sign_dst is not None and last_qd[0] > 0:
                    s.wait_ge(cv.sem, last_qd[0])
                load_tbl(s, SET_LNEXP)
                s.activation(scratch_ap, src_ap, AF.Square, accum_out=ssr)
                s.wait_ge(cv.sem, n1)
                s.activation(nm2, nmn, AF.Identity, scale=mn, bias=0.0)
                n_a = cs.inc(s.activation(nm2e, nm2, AF.Identity,
                                          bias=cbias[:, 0:1]))
                s.wait_ge(cs.sem, n_a)
                s.activation(var_, ssr, AF.Identity, scale=1.0 / D,
                             bias=nm2e)
                s.activation(lnv, var_, AF.Ln)
                n_b = cs.inc(s.activation(rs, lnv, AF.Exp, scale=-0.5))
                s.wait_ge(cs.sem, n_b)
                n_c = cs.inc(s.activation(nb, nmn, AF.Identity, scale=rs,
                                          bias=0.0))
                s.wait_ge(cs.sem, n_c)
                for ctr, n in norm_waits:
                    s.wait_ge(ctr.sem, n)
                cs.inc(s.activation(dst_ap, src_ap, AF.Identity,
                                    scale=rs, bias=nb))
                if sign_dst is not None:
                    sgn_ap, sq_ap = sign_dst
                    s.activation(sgn_ap, dst_ap, AF.Sign)
                    cs.inc(s.activation(sq_ap, dst_ap, AF.Square))
            blk.scalar(_)
            return cs.n

        p0 = contextlib.ExitStack()
        with p0:
            def p0_sbuf(name, shape, dt):
                return p0.enter_context(nc.sbuf_tensor(name, shape, dt))

            p0_wtile = [p0_sbuf(f"p0w{i}", [128, DI], F32) for i in range(2)]
            p0_q = [p0_sbuf(f"p0q{i}", [128, DI], BF16) for i in range(2)]
            p0_tv = p0_sbuf("p0tv", [128, DI], F32)
            p0_ta = p0_sbuf("p0ta", [128, DI], BF16)
            p0_sg = p0_sbuf("p0sg", [128, DI], BF16)
            p0_bm = p0_sbuf("p0bm", [128, DI // 16], F32)
            p0_r2 = p0_sbuf("p0r2", [128, DI // 16], F32)
            p0_sb = p0_sbuf("p0sb", [128, DI // 16], BF16)
            p0_sc = p0_sbuf("p0sc", [128, DI // 16], F32)
            p0_rs = p0_sbuf("p0rs", [128, DI // 16], F32)
            p0_stage = p0_sbuf("p0_stage", [128, KT * DI], BF16)
            p0_tp = p0.enter_context(
                nc.psum_tensor("p0_tp", [128, 1024], BF16))

            def prep_weight(d_src, d_dst, n_tiles, tile_f, tp_rows):
                nblk = tile_f // 128
                quant_done = {}
                evac_done = [0]
                tp_done = [0, 0]
                entry_st = cst.n
                for t in range(n_tiles):
                    r = t % 2

                    def _(g, t=t, r=r):
                        if t >= 2:
                            g.wait_ge(cv.sem, quant_done[t - 2])
                        g.dma_start(out=p0_wtile[r][:, 0:tile_f],
                                    in_=d_src[t * 128:(t + 1) * 128, 0:tile_f]
                                    ).then_inc(cpl[r].sem, 16)
                    blk.gpsimd(_)
                    cpl[r].n += 16
                    n_ld = cpl[r].n
                    qd = emit_quant(
                        p0_wtile[r][:, 0:tile_f], p0_q[r][:, 0:tile_f],
                        tile_f,
                        p0_tv, p0_ta, p0_sg, p0_bm, p0_sc, p0_rs,
                        p0_r2, p0_sb,
                        wait_v=[(cpl[r], n_ld), (ct, tp_done[1])],
                        wait_s=[(cpl[r], n_ld)])
                    quant_done[t] = qd
                    for g0 in range(0, nblk, 8):
                        ng = min(8, nblk - g0)

                        def _(te, g0=g0, ng=ng, qd=qd, t=t):
                            te.wait_ge(cv.sem, max(qd, n_consts))
                            if evac_done[0] > 0:
                                te.wait_ge(cv.sem, evac_done[0])
                            last = None
                            for j in range(ng):
                                last = te.transpose(
                                    p0_tp[:, j * 128:(j + 1) * 128],
                                    p0_q[t % 2][:, (g0 + j) * 128:
                                                 (g0 + j + 1) * 128],
                                    idb[:])
                            ct.inc(last)
                        blk.tensor(_)
                        n_tp = ct.n

                        def _(v, g0=g0, ng=ng, t=t, n_tp=n_tp):
                            v.wait_ge(ct.sem, n_tp)
                            if t == 0 and g0 == 0 and entry_st > 0:
                                v.wait_ge(cst.sem, entry_st)
                            src = p0_tp[:, 0:ng * 128].rearrange(
                                "p (j c) -> p j c", c=128)
                            dst = p0_stage[:, 0:nblk * tp_rows].rearrange(
                                "p (j n) -> p j n", n=tp_rows)[
                                :, g0:g0 + ng, t * 128:(t + 1) * 128]
                            cv.inc(v.tensor_copy(dst, src))
                        blk.vector(_)
                        evac_done[0] = cv.n
                    tp_done[1] = tp_done[0]
                    tp_done[0] = ct.n

                def _(g):
                    g.wait_ge(cv.sem, evac_done[0])
                    g.dma_start(out=d_dst, in_=p0_stage[:, 0:nblk * tp_rows]
                                ).then_inc(cst.sem, 16)
                blk.gpsimd(_)
                cst.n += 16
                return cst.n

            prep_weight(d_wg, d_wgt, NT, D, DI)
            prep_weight(d_wu, d_wut, NT, D, DI)
            st_prep = prep_weight(d_wd, d_wdt, KT, DI, D)

        pa = contextlib.ExitStack()
        with pa:
            def pas(name, shape, dt):
                return pa.enter_context(nc.sbuf_tensor(name, shape, dt))

            WW = pas("WW", [128, KT * 2048], BF16)
            WoB = pas("WoB", [128, KT * 1024], BF16)
            LNXT = pas("LNXT", [128, KT * 2048], BF16)
            QT = pas("QT", [128, KT * 2048], BF16)
            KTb = pas("KTb", [128, KT * 2048], BF16)
            V = pas("V", [128, SB * 1024], BF16)
            EK = [pas(f"EK{i}", [128, 1024], BF16) for i in range(2)]
            RC = pas("RC", [128, 512], F32)
            xbuf = [pas(f"xbuf{i}", [128, D], F32) for i in range(2)]
            aq_bm = pas("aq_bm", [128, D // 16], F32)
            aq_sc = pas("aq_sc", [128, D // 16], F32)
            aq_rs = pas("aq_rs", [128, D // 16], F32)
            aq_r2 = pas("aq_r2", [128, D // 16], F32)
            aq_sb = pas("aq_sb", [128, D // 16], BF16)
            x3buf = [pas(f"x3buf{i}", [128, D], F32) for i in range(2)]
            lnx = [pas(f"lnx{i}", [128, D], BF16) for i in range(2)]
            stat = pas("stat", [128, 40], F32)

            def _(g):
                g.wait_ge(cst.sem, st_prep)
                for k in range(KT):
                    g.dma_start(out=WoB[:, k * 1024:(k + 1) * 1024],
                                in_=d_wo[k * 128:(k + 1) * 128, :]
                                ).then_inc(cwl.sem, 16)
            blk.gpsimd(_)
            cwl.n += 16 * 8

            dbg_gate = [0]
            prev_a2_cs = [0]
            prev_lnx_guard = [0]
            prev_ln2_end = [0]
            prev_xq_store = [(0, 0)]
            x2_store_all = {}
            xq_store_all = {}
            prev_t_end = 0
            prev_v_end = 0
            prev_qkv_t = 0
            prev_store_last = (0, 0)

            for b in range(n_batch):
                bps = contextlib.ExitStack()
                with bps:
                    pq = [bps.enter_context(nc.psum_tensor(
                        f"pq{b}_{i}", [128, 1024], F32)) for i in range(3)]
                    tp2 = [bps.enter_context(nc.psum_tensor(
                        f"tp{b}_{i}", [128, 1024], BF16)) for i in range(2)]

                    def _(g, b=b):
                        if b > 0:
                            g.wait_ge(ct.sem, prev_qkv_t)
                        for k in range(KT):
                            g.dma_start(out=WW[:, k * 2048:k * 2048 + 1024],
                                        in_=d_wq[k * 128:(k + 1) * 128, :]
                                        ).then_inc(cwl.sem, 16)
                    blk.gpsimd(_)
                    cwl.n += 16 * KT
                    n_wq = cwl.n

                    norm_done = {}
                    tpos_done = {}
                    lnxt_evac = {}
                    for sb in range(SB):
                        r = sb % 2

                        def _(sy, b=b, sb=sb, r=r):
                            if sb >= 2:
                                sy.wait_ge(cs.sem, norm_done[sb - 2])
                            if b == 0 and sb == 0:
                                sy.wait_ge(cst.sem, st_prep)
                            if b > 0 and sb < 2:
                                sy.wait_ge(cs.sem, prev_ln2_end[0])
                                sy.wait_ge(cv.sem, prev_lnx_guard[0])
                            sy.dma_start(
                                out=xbuf[r][:],
                                in_=d_x[b, sb * 128:(sb + 1) * 128, :]
                            ).then_inc(cxl[r].sem, 16)
                        blk.sync(_)
                        cxl[r].n += 16
                        n_ldx = cxl[r].n

                        sq_w = []
                        if b > 0 and sb < 2:
                            sq_w.append((cso[0], prev_store_last[0]))
                            sq_w.append((cso[1], prev_store_last[1]))
                        nrm_w = []
                        if sb >= 2:
                            nrm_w.append((ct, tpos_done[sb - 2]))
                        if b > 0 and sb < 2:
                            nrm_w.append((cv, prev_lnx_guard[0]))
                        norm_done[sb] = emit_ln(
                            xbuf[r][:], x3buf[r][:], lnx[r][:], stat, r * 10,
                            wait_load=[(cxl[r], n_ldx)],
                            col_guard=(norm_done[sb - 2] if sb >= 2 else 0),
                            sq_waits=sq_w, norm_waits=nrm_w)

                        def _(te, r=r, sb=sb, b=b):
                            te.wait_ge(cs.sem, norm_done[sb])
                            te.wait_ge(cv.sem, n_consts)
                            if sb >= 2:
                                te.wait_ge(cv.sem, lnxt_evac[sb - 2])
                            if b > 0 and sb == 0:
                                te.wait_ge(cv.sem, prev_v_end)
                            last = None
                            for j in range(KT):
                                last = te.transpose(
                                    tp2[sb % 2][:, j * 128:(j + 1) * 128],
                                    lnx[r][:, j * 128:(j + 1) * 128], idb[:])
                            ct.inc(last)
                        blk.tensor(_)
                        tpos_done[sb] = ct.n

                        def _(v, sb=sb, b=b):
                            v.wait_ge(ct.sem, tpos_done[sb])
                            src = tp2[sb % 2][:].rearrange(
                                "p (j c) -> p j c", c=128)
                            dst = LNXT[:].rearrange(
                                "p (j q) -> p j q", q=2048)[
                                :, :, sb * 128:(sb + 1) * 128]
                            cv.inc(v.tensor_copy(dst, src))
                        blk.vector(_)
                        lnxt_evac[sb] = cv.n
                    n_lnxt_all = cv.n

                    pq_evac = []
                    uidx = [0]

                    def qkv_unit(stat_fn, rhs_fn, out_ap):
                        u = uidx[0] % 3
                        myu = uidx[0]
                        my_attw = n_attw

                        def _(te):
                            te.wait_ge(cv.sem, n_lnxt_all)
                            te.wait_ge(cwl.sem, my_attw)
                            if myu >= 3:
                                te.wait_ge(cv.sem, pq_evac[myu - 3])
                            last = None
                            for k in range(KT):
                                lhsT = stat_fn(k)
                                for c in range(2):
                                    last = te.matmul(
                                        pq[u][:, c * 512:(c + 1) * 512],
                                        lhsT, rhs_fn(k, c),
                                        start=(k == 0), stop=(k == KT - 1))
                            ct.inc(last)
                        blk.tensor(_)
                        n_mm = ct.n

                        def _(v, u=u, out_ap=out_ap, n_mm=n_mm):
                            v.wait_ge(ct.sem, n_mm)
                            cv.inc(v.tensor_copy(out_ap, pq[u][:]))
                        blk.vector(_)
                        pq_evac.append(cv.n)
                        uidx[0] += 1

                    n_attw = n_wq
                    for j in range(KT):
                        for half in range(2):
                            qkv_unit(
                                (lambda k, j=j:
                                 WW[:, k * 2048 + j * 128:
                                    k * 2048 + (j + 1) * 128]),
                                (lambda k, c, half=half:
                                 LNXT[:, k * 2048 + half * 1024 + c * 512:
                                      k * 2048 + half * 1024 + (c + 1) * 512]),
                                QT[:, j * 2048 + half * 1024:
                                   j * 2048 + (half + 1) * 1024])
                    n_q_mm = ct.n

                    def _(g, n_q_mm=n_q_mm):
                        g.wait_ge(ct.sem, n_q_mm)
                        for k in range(KT):
                            g.dma_start(out=WW[:, k * 2048:(k + 1) * 2048],
                                        in_=d_wkv[k * 128:(k + 1) * 128, :]
                                        ).then_inc(cwl.sem, 16)
                    blk.gpsimd(_)
                    cwl.n += 16 * KT
                    n_attw = cwl.n

                    for j in range(KT):
                        for half in range(2):
                            qkv_unit(
                                (lambda k, j=j:
                                 WW[:, k * 2048 + j * 128:
                                    k * 2048 + (j + 1) * 128]),
                                (lambda k, c, half=half:
                                 LNXT[:, k * 2048 + half * 1024 + c * 512:
                                      k * 2048 + half * 1024 + (c + 1) * 512]),
                                KTb[:, j * 2048 + half * 1024:
                                    j * 2048 + (half + 1) * 1024])
                    for t in range(SB):
                        qkv_unit(
                            (lambda k, t=t:
                             LNXT[:, k * 2048 + t * 128:
                                  k * 2048 + (t + 1) * 128]),
                            (lambda k, c:
                             WW[:, k * 2048 + 1024 + c * 512:
                                k * 2048 + 1024 + (c + 1) * 512]),
                            V[:, t * 1024:(t + 1) * 1024])
                    n_qkv_t = ct.n
                    n_qkv_v = cv.n

                    if debug and b == 0:
                        def _(g):
                            g.wait_ge(cv.sem, n_qkv_v)
                            g.dma_start(out=d_dbg["lnxt"], in_=LNXT[:]
                                        ).then_inc(cst.sem, 16)
                            g.dma_start(out=d_dbg["qt"], in_=QT[:]
                                        ).then_inc(cst.sem, 16)
                            g.dma_start(out=d_dbg["ktb"], in_=KTb[:]
                                        ).then_inc(cst.sem, 16)
                            g.dma_start(out=d_dbg["v"], in_=V[:]
                                        ).then_inc(cst.sem, 16)
                        blk.gpsimd(_)
                        cst.n += 64
                        dbg_gate[0] = cst.n

                ba2 = contextlib.ExitStack()
                with ba2:
                    sps = [ba2.enter_context(nc.psum_tensor(
                        f"sps{b}_{i}", [128, 1024], F32)) for i in range(3)]
                    pvo = ba2.enter_context(
                        nc.psum_tensor(f"pvo{b}", [128, 512], F32))
                    pvd = ba2.enter_context(
                        nc.psum_tensor(f"pvd{b}", [128, 512], F32))

                    s_done = {}
                    exp_done = {}
                    pv_done = {}
                    ot_write = [0]
                    for p in range(NPAIR):
                        key0 = p * 64
                        cs_pre = cs.n
                        for i in range(64):
                            exp_done[key0 + i] = cs_pre + i + 1
                        for qc in range(4):
                            def _(te, p=p, qc=qc):
                                def emit_s(kb):
                                    key = p * 64 + qc * SB + kb
                                    sp = sps[kb % 3]
                                    if key == 0:
                                        te.wait_ge(ct.sem, n_qkv_t)
                                        te.wait_ge(cv.sem, n_qkv_v)
                                        if prev_a2_cs[0] > 0:
                                            te.wait_ge(cs.sem, prev_a2_cs[0])
                                    if key >= 3:
                                        te.wait_ge(cs.sem, exp_done[key - 3])
                                    la = KTb[0:64, p * 2048 + kb * 128:
                                             p * 2048 + (kb + 1) * 128]
                                    ra = QT[0:64, p * 2048 + qc * 512:
                                            p * 2048 + (qc + 1) * 512]
                                    lb = KTb[64:128, p * 2048 + kb * 128:
                                             p * 2048 + (kb + 1) * 128]
                                    rb = QT[64:128, p * 2048 + qc * 512:
                                            p * 2048 + (qc + 1) * 512]
                                    te.matmul(sp[:, 0:512], la, ra,
                                              start=True, stop=True,
                                              tile_position=(0, 0))
                                    ct.inc(te.matmul(sp[:, 512:1024], lb, rb,
                                                     start=True, stop=True,
                                                     tile_position=(64, 0)))
                                    s_done[key] = ct.n

                                def emit_pv(kb):
                                    key = p * 64 + qc * SB + kb
                                    te.wait_ge(cs.sem, exp_done[key])
                                    if kb == 0:
                                        te.wait_ge(cv.sem, ot_write[0])
                                    v3 = V[:].rearrange(
                                        "p (t c) -> p t c", c=1024)
                                    va = v3[:, kb, (2 * p) * 64:
                                            (2 * p) * 64 + 64]
                                    vb = v3[:, kb, (2 * p + 1) * 64:
                                            (2 * p + 1) * 64 + 64]
                                    ea = EK[kb % 2][:, 0:512]
                                    eb = EK[kb % 2][:, 512:1024]
                                    st_, sp_ = (kb == 0), (kb == SB - 1)
                                    te.matmul(pvo[0:64, :], va, ea,
                                              start=st_, stop=sp_,
                                              tile_position=(0, 0),
                                              skip_group_check=True)
                                    te.matmul(pvo[64:128, :], vb, eb,
                                              start=st_, stop=sp_,
                                              tile_position=(0, 64),
                                              skip_group_check=True)
                                    te.matmul(pvd[0:64, :], ones64[:], ea,
                                              start=st_, stop=sp_,
                                              tile_position=(0, 0),
                                              skip_group_check=True)
                                    ct.inc(te.matmul(pvd[64:128, :],
                                                     ones64[:], eb,
                                                     start=st_, stop=sp_,
                                                     tile_position=(0, 64),
                                                     skip_group_check=True))
                                    pv_done[key] = ct.n

                                for kb in range(SB):
                                    emit_s(kb)
                                    if kb >= 2:
                                        emit_pv(kb - 2)
                                emit_pv(SB - 2)
                                emit_pv(SB - 1)
                            blk.tensor(_)
                            n_pv = ct.n

                            def _(v, p=p, qc=qc, n_pv=n_pv):
                                v.wait_ge(ct.sem, n_pv)
                                if dbg_gate[0]:
                                    v.wait_ge(cst.sem, dbg_gate[0])
                                    dbg_gate[0] = 0
                                n_rc = cv.inc(v.reciprocal(RC[:], pvd[:]))
                                v.wait_ge(cv.sem, n_rc)
                                base = p * 2048 + qc * 512
                                v.tensor_tensor(LNXT[0:64, base:base + 512],
                                                pvo[0:64, :], RC[0:64, :],
                                                op=ALU.mult)
                                cv.inc(v.tensor_tensor(
                                    LNXT[64:128, base:base + 512],
                                    pvo[64:128, :], RC[64:128, :],
                                    op=ALU.mult))
                            blk.vector(_)
                            ot_write[0] = cv.n

                        def _(s, p=p, cs_pre=cs_pre):
                            load_tbl(s, SET_LNEXP)
                            for qc in range(4):
                                for kb in range(SB):
                                    key = p * 64 + qc * SB + kb
                                    s.wait_ge(ct.sem, s_done[key])
                                    if key < 2 and prev_xq_store[0][kb] > 0:
                                        s.wait_ge(cqo[kb].sem,
                                                  prev_xq_store[0][kb])
                                    if key >= 2:
                                        s.wait_ge(ct.sem, pv_done[key - 2])
                                    cs.inc(s.activation(
                                        EK[kb % 2][:], sps[kb % 3][:],
                                        AF.Exp, bias=cbias[:, 1:2],
                                        scale=0.125))
                                    assert cs.n == exp_done[key], \
                                        (cs.n, exp_done[key])
                        blk.scalar(_)
                    n_ot_all = cv.n

                if debug and b == 0:
                    def _(g):
                        g.wait_ge(cv.sem, n_ot_all)
                        g.dma_start(out=d_dbg["ot"], in_=LNXT[:]
                                    ).then_inc(cst.sem, 16)
                    blk.gpsimd(_)
                    cst.n += 16
                    dbg_a3 = [True]

                prev_a2_cs[0] = cs.n

                ba3 = contextlib.ExitStack()
                with ba3:
                    wop = [ba3.enter_context(nc.psum_tensor(
                        f"wop{b}_{i}", [128, 1024], F32)) for i in range(2)]
                    x2_evac = {}
                    x2_store = {}
                    xq_store = {}
                    ln2a_done = {}
                    ln2a_qd = {}
                    for sb in range(SB):
                        r = sb % 2

                        def _(sy, b=b, sb=sb, r=r):
                            if sb >= 2:
                                sy.wait_ge(cso[r].sem, x2_store[sb - 2])
                                sy.wait_ge(cv.sem, ln2a_qd[sb - 2])
                            else:
                                sy.wait_ge(cs.sem, norm_done[SB - 1])
                            sy.dma_start(
                                out=x3buf[r][:],
                                in_=d_x[b, sb * 128:(sb + 1) * 128, :]
                            ).then_inc(c3l[r].sem, 16)
                        blk.sync(_)
                        c3l[r].n += 16
                        n_ldx3 = c3l[r].n

                        def _(te, sb=sb, r=r):
                            te.wait_ge(cv.sem, n_ot_all)
                            if sb >= 2:
                                te.wait_ge(cv.sem, x2_evac[sb - 2])
                            last = None
                            for j in range(KT):
                                statp = LNXT[:, j * 2048 + sb * 128:
                                             j * 2048 + (sb + 1) * 128]
                                for c in range(2):
                                    last = te.matmul(
                                        wop[r][:, c * 512:(c + 1) * 512],
                                        statp,
                                        WoB[:, j * 1024 + c * 512:
                                            j * 1024 + (c + 1) * 512],
                                        start=(j == 0), stop=(j == KT - 1))
                            ct.inc(last)
                        blk.tensor(_)
                        n_wo = ct.n

                        def _(v, sb=sb, r=r, n_wo=n_wo, n_ldx3=n_ldx3):
                            v.wait_ge(ct.sem, n_wo)
                            v.wait_ge(c3l[r].sem, n_ldx3)
                            v.tensor_tensor(x3buf[r][:], x3buf[r][:],
                                            wop[r][:], op=ALU.add)
                            cv.inc(v.tensor_tensor(x3buf[r][:], x3buf[r][:],
                                                   bo_bc[:], op=ALU.add))
                        blk.vector(_)
                        x2_evac[sb] = cv.n

                        def _(g, b=b, sb=sb, r=r):
                            g.wait_ge(cv.sem, x2_evac[sb])
                            g.dma_start(
                                out=d_x2[b, sb * 128:(sb + 1) * 128, :],
                                in_=x3buf[r][:]).then_inc(cso[r].sem, 16)
                        blk.gpsimd(_)
                        cso[r].n += 16
                        x2_store[sb] = cso[r].n

                        nrm2_w = []
                        if b > 0 and sb < 2:
                            nrm2_w.append((cv, prev_lnx_guard[0]))
                        ln2a_done[sb] = emit_ln(
                            x3buf[r][:], xbuf[0][:], xbuf[1][:], stat, 20,
                            wait_load=[(cv, x2_evac[sb])],
                            col_guard=(ln2a_done[sb - 1] if sb >= 1 else 0),
                            sq_waits=([(cv, ln2a_qd[sb - 1])] if sb >= 1
                                      else []),
                            norm_waits=nrm2_w,
                            sign_dst=(lnx[1][:], xbuf[0][:]))
                        qd = emit_quant(
                            xbuf[1][:], EK[r][:], D,
                            xbuf[0], lnx[0], lnx[1],
                            aq_bm, aq_sc, aq_rs, aq_r2, aq_sb,
                            wait_v=[(cs, ln2a_done[sb])] +
                                   ([] if sb < 2 else
                                    [(cqo[r], xq_store[sb - 2])]),
                            ext_prep=ln2a_done[sb])
                        ln2a_qd[sb] = qd

                        def _(g, b=b, sb=sb, r=r, qd=qd):
                            g.wait_ge(cv.sem, qd)
                            g.dma_start(
                                out=d_xq[b, sb * 128:(sb + 1) * 128, :],
                                in_=EK[r][:]).then_inc(cqo[r].sem, 16)
                        blk.gpsimd(_)
                        cqo[r].n += 16
                        xq_store[sb] = cqo[r].n
                    if debug and b == 0:
                        def _(g):
                            g.wait_ge(cv.sem, ln2a_qd[SB - 1])
                            g.dma_start(out=d_dbg["aqbm"], in_=aq_bm[:]
                                        ).then_inc(cst.sem, 16)
                            g.dma_start(out=d_dbg["axln"], in_=xbuf[1][:]
                                        ).then_inc(cst.sem, 16)
                            g.dma_start(out=d_dbg["asq"], in_=xbuf[0][:]
                                        ).then_inc(cst.sem, 16)
                            g.dma_start(out=d_dbg["ata"], in_=lnx[0][:]
                                        ).then_inc(cst.sem, 16)
                            g.dma_start(out=d_dbg["atsg"], in_=lnx[1][:]
                                        ).then_inc(cst.sem, 16)
                        blk.gpsimd(_)
                        cst.n += 80
                    prev_lnx_guard[0] = ln2a_qd[SB - 1]
                    prev_ln2_end[0] = ln2a_done[SB - 1]
                    prev_xq_store[0] = (cqo[0].n, cqo[1].n)
                    prev_t_end = ct.n
                    prev_v_end = cv.n
                    prev_qkv_t = n_qkv_t
                    prev_store_last = (cso[0].n, cso[1].n)
                    for _sb in range(SB):
                        x2_store_all[b * SB + _sb] = (_sb % 2, x2_store[_sb])
                        xq_store_all[b * SB + _sb] = (_sb % 2, xq_store[_sb])

        pa_t_end, pa_v_end, pa_s_end, pa_st_end = ct.n, cv.n, cs.n, cst.n
        pa_so = (cso[0].n, cso[1].n)
        pa_qo = (cqo[0].n, cqo[1].n)

        pf = contextlib.ExitStack()
        with pf:
            def pfs(name, shape, dt):
                return pf.enter_context(nc.sbuf_tensor(name, shape, dt))

            WGT = pfs("WGT", [128, KT * DI], BF16)
            WUT = pfs("WUT", [128, KT * DI], BF16)
            WDT = pfs("WDT", [128, NT * D], BF16)
            x2b = [pfs(f"x2b{i}", [128, D], F32) for i in range(2)]
            xqb = [pfs(f"xqb{i}", [128, D], BF16) for i in range(2)]
            XqT = [pfs(f"XqT{i}", [128, D], BF16) for i in range(2)]
            hbuf = pfs("hbuf", [128, DI], F32)
            Hq = pfs("Hq", [128, DI], BF16)
            HqT = [pfs(f"HqT{i}", [128, DI], BF16) for i in range(2)]
            q_tv = pfs("q_tv", [128, DI], F32)
            q_ta = pfs("q_ta", [128, DI], BF16)
            q_sg = pfs("q_sg", [128, DI], BF16)
            q_bm = pfs("q_bm", [128, DI // 16], F32)
            q_r2 = pfs("q_r2", [128, DI // 16], F32)
            q_slb = pfs("q_slb", [128, DI // 16], BF16)
            q_sc = pfs("q_sc", [128, DI // 16], F32)
            q_rs = pfs("q_rs", [128, DI // 16], F32)
            stat2 = pfs("stat2", [128, 24], F32)

            gps = [pf.enter_context(nc.psum_tensor(
                f"gps{i}", [128, 512], F32)) for i in range(2)]
            ups = [pf.enter_context(nc.psum_tensor(
                f"ups{i}", [128, 512], F32)) for i in range(2)]
            tps = [pf.enter_context(nc.psum_tensor(
                f"tps{i}", [128, 1024], BF16)) for i in range(2)]
            dps = pf.enter_context(nc.psum_tensor("dps", [128, 1024], F32))

            def _(g):
                g.wait_ge(ct.sem, pa_t_end)
                g.wait_ge(cv.sem, pa_v_end)
                g.wait_ge(cs.sem, pa_s_end)
                g.wait_ge(cst.sem, pa_st_end)
                g.wait_ge(cso[0].sem, pa_so[0])
                g.wait_ge(cso[1].sem, pa_so[1])
                g.wait_ge(cqo[0].sem, pa_qo[0])
                g.wait_ge(cqo[1].sem, pa_qo[1])
                g.dma_start(out=WGT[:], in_=d_wgt).then_inc(cwl.sem, 16)
                g.dma_start(out=WUT[:], in_=d_wut).then_inc(cwl.sem, 16)
                g.dma_start(out=WDT[:], in_=d_wdt).then_inc(cwl.sem, 16)
            blk.gpsimd(_)
            cwl.n += 48
            n_ffnw = cwl.n

            out_store = {}
            xq_tp_done = {}
            xqt_evac = {}
            hq_tp_done = {}
            hqt_last = {}
            gu_last = {}
            guk_done = {}
            h_done = {}
            qd_h_hist = {}
            down_done = {}
            x2_used = {}

            hbuf_free = {}
            gu_rec_all = {}
            nxqt_all = {}
            nxqtp_all = {}
            nhall_all = {}
            ldx2_all = {}
            NCH = len(N_CHUNKS)

            def emit_gu_chunk(gi, r, ci):
                nc0, nc1 = N_CHUNKS[ci]
                cw = nc1 - nc0
                pr = ci % 2
                kk = gi * NCH + ci
                n_xqt = nxqt_all[gi]

                def _(te, r=r, nc0=nc0, nc1=nc1, cw=cw, pr=pr, kk=kk,
                      n_xqt=n_xqt):
                    te.wait_ge(cv.sem, n_xqt)
                    if kk >= 2:
                        te.wait_ge(cv.sem, h_done[kk - 2])
                    for k in range(KT):
                        te.matmul(gps[pr][:, 0:cw],
                                  XqT[r][:, k * 128:(k + 1) * 128],
                                  WGT[:, k * DI + nc0:k * DI + nc1],
                                  start=(k == 0), stop=(k == KT - 1))
                    last = None
                    for k in range(KT):
                        last = te.matmul(
                            ups[pr][:, 0:cw],
                            XqT[r][:, k * 128:(k + 1) * 128],
                            WUT[:, k * DI + nc0:k * DI + nc1],
                            start=(k == 0), stop=(k == KT - 1))
                    ct.inc(last)
                blk.tensor(_)
                guk_done[kk] = ct.n
                gu_rec_all[gi][ci] = ct.n

            def stage_a1(b, sb, gi, r):
                def _(sy, b=b, sb=sb, gi=gi):
                    slot = gi % 2
                    if gi >= 2:
                        sy.wait_ge(ct.sem, xq_tp_done[gi - 2])
                    qslot, qval = xq_store_all[gi]
                    sy.wait_ge(cqo[qslot].sem, qval)
                    sy.dma_start(
                        out=xqb[slot][:],
                        in_=d_xq[b, sb * 128:(sb + 1) * 128, :]
                    ).then_inc(cql[slot].sem, 16)
                blk.sync(_)
                cql[gi % 2].n += 16
                n_ldxq = cql[gi % 2].n

                def _(te, gi=gi, n_ldxq=n_ldxq):
                    te.wait_ge(cql[gi % 2].sem, n_ldxq)
                    if gi == 0:
                        te.wait_ge(cwl.sem, n_ffnw)
                    if gi >= 1:
                        te.wait_ge(cv.sem, xqt_evac[gi - 1])
                    last = None
                    for j in range(KT):
                        last = te.transpose(
                            tps[0][:, j * 128:(j + 1) * 128],
                            xqb[gi % 2][:, j * 128:(j + 1) * 128], idb[:])
                    ct.inc(last)
                blk.tensor(_)
                xq_tp_done[gi] = ct.n
                nxqtp_all[gi] = ct.n

            def stage_a2(gi, r):
                n_xqtp = nxqtp_all[gi]

                def _(v, gi=gi, r=r, n_xqtp=n_xqtp):
                    v.wait_ge(ct.sem, n_xqtp)
                    if gi >= 2:
                        v.wait_ge(ct.sem, gu_last[gi - 2])
                    cv.inc(v.tensor_copy(XqT[r][:], tps[0][:]))
                blk.vector(_)
                xqt_evac[gi] = cv.n
                nxqt_all[gi] = cv.n

                gu_rec_all[gi] = {}
                for ci in (0, 1):
                    emit_gu_chunk(gi, r, ci)

            def stage_b1a(gi, r):
                qd_h = qd_h_hist[gi]
                hqt_evacs = []
                for g0 in range(0, NT, 8):
                    ng = min(8, NT - g0)

                    def _(te, g0=g0, ng=ng, qd_h=qd_h, gi=gi,
                          hqt_evacs=tuple(hqt_evacs)):
                        te.wait_ge(cv.sem, qd_h)
                        if hqt_evacs:
                            te.wait_ge(cv.sem, hqt_evacs[-1])
                        elif gi >= 1:
                            te.wait_ge(cv.sem, hqt_last[gi - 1])
                        last = None
                        for j in range(ng):
                            last = te.transpose(
                                tps[1][:, j * 128:(j + 1) * 128],
                                Hq[:, (g0 + j) * 128:(g0 + j + 1) * 128],
                                idb[:])
                        ct.inc(last)
                    blk.tensor(_)
                    n_tp = ct.n

                    def _(v, g0=g0, ng=ng, r=r, n_tp=n_tp, gi=gi):
                        v.wait_ge(ct.sem, n_tp)
                        if g0 == 0 and gi >= 2:
                            v.wait_ge(ct.sem, down_done[gi - 2])
                        cv.inc(v.tensor_copy(
                            HqT[r][:, g0 * 128:(g0 + ng) * 128],
                            tps[1][:, 0:ng * 128]))
                    blk.vector(_)
                    hqt_evacs.append(cv.n)
                hq_tp_done[gi] = ct.n
                hqt_last[gi] = cv.n

            def stage_b1b(gi, r):
                n_hqt = hqt_last[gi]

                def _(te, r=r, n_hqt=n_hqt, gi=gi):
                    te.wait_ge(cv.sem, n_hqt)
                    if gi >= 1:
                        te.wait_ge(cv.sem, x2_used[gi - 1])
                    last = None
                    for t in range(NT):
                        for c in range(2):
                            last = te.matmul(
                                dps[:, c * 512:(c + 1) * 512],
                                HqT[r][:, t * 128:(t + 1) * 128],
                                WDT[:, t * D + c * 512:
                                    t * D + (c + 1) * 512],
                                start=(t == 0), stop=(t == NT - 1))
                    ct.inc(last)
                blk.tensor(_)
                down_done[gi] = ct.n

            def emit_x2_load(gi):
                b, sb = gi // SB, gi % SB
                r = gi % 2

                def _(g, b=b, sb=sb, r=r, gi=gi):
                    if gi >= 2:
                        g.wait_ge(cfo[r].sem, out_store[gi - 2])
                    slot, val = x2_store_all[gi]
                    g.wait_ge(cso[slot].sem, val)
                    g.dma_start(
                        out=x2b[r][:],
                        in_=d_x2[b, sb * 128:(sb + 1) * 128, :]
                    ).then_inc(cfl[r].sem, 16)
                blk.gpsimd(_)
                cfl[r].n += 16
                ldx2_all[gi] = cfl[r].n

            def stage_b2(gi, r):
                b, sb = gi // SB, gi % SB
                n_down = down_done[gi]
                n_ldx2 = ldx2_all[gi]

                def _(v, r=r, n_down=n_down, n_ldx2=n_ldx2):
                    v.wait_ge(ct.sem, n_down)
                    v.wait_ge(cfl[r].sem, n_ldx2)
                    cv.inc(v.tensor_tensor(x2b[r][:], x2b[r][:], dps[:],
                                           op=ALU.add))
                blk.vector(_)
                x2_used[gi] = cv.n
                n_out = cv.n

                def _(g, b=b, sb=sb, r=r, n_out=n_out):
                    g.wait_ge(cv.sem, n_out)
                    g.dma_start(
                        out=d_out[b, sb * 128:(sb + 1) * 128, :],
                        in_=x2b[r][:]).then_inc(cfo[r].sem, 16)
                blk.gpsimd(_)
                cfo[r].n += 16
                out_store[gi] = cfo[r].n

                if gi + 2 < n_gi:
                    emit_x2_load(gi + 2)

            def stage_c1(b, sb, gi, r):
                gu_rec = gu_rec_all[gi]
                cv_pre_tt = cv.n
                for ci in range(NCH):
                    h_done[gi * NCH + ci] = cv_pre_tt + ci + 1
                for ci in range(2, NCH):
                    emit_gu_chunk(gi, r, ci)
                gu_last[gi] = ct.n

                cs_pre_silu = cs.n

                def _(s, gi=gi, gu_rec=gu_rec, cv_pre_tt=cv_pre_tt):
                    load_tbl(s, SET_SILU)
                    if gi >= 1:
                        s.wait_ge(cv.sem, hbuf_free[gi - 1])
                    for ci in range(NCH):
                        nc0, nc1 = N_CHUNKS[ci]
                        cw = nc1 - nc0
                        pr = ci % 2
                        s.wait_ge(ct.sem, gu_rec[ci])
                        if ci >= 2:
                            s.wait_ge(cv.sem, cv_pre_tt + ci - 1)
                        cs.inc(s.activation(hbuf[:, nc0:nc0 + cw],
                                            gps[pr][:, 0:cw], AF.Silu))
                blk.scalar(_)

                for ci, (nc0, nc1) in enumerate(N_CHUNKS):
                    cw = nc1 - nc0
                    pr = ci % 2

                    def _(v, nc0=nc0, cw=cw, pr=pr, ci=ci,
                          cs_pre_silu=cs_pre_silu):
                        v.wait_ge(cs.sem, cs_pre_silu + ci + 1)
                        cv.inc(v.tensor_tensor(hbuf[:, nc0:nc0 + cw],
                                               hbuf[:, nc0:nc0 + cw],
                                               ups[pr][:, 0:cw],
                                               op=ALU.mult))
                    blk.vector(_)
                assert cv.n == cv_pre_tt + NCH
                nhall_all[gi] = (cv_pre_tt, cv.n)

            def stage_c2(b, sb, gi, r):
                cv_pre_tt, n_h_all = nhall_all[gi]

                def _(s, cv_pre_tt=cv_pre_tt):
                    load_tbl(s, SET_SILU)
                    if last_qd[0] > 0:
                        s.wait_ge(cv.sem, last_qd[0])
                    for ci in range(NCH):
                        nc0, nc1 = N_CHUNKS[ci]
                        cw = nc1 - nc0
                        s.wait_ge(cv.sem, cv_pre_tt + ci + 1)
                        s.activation(q_sg[:, nc0:nc0 + cw],
                                     hbuf[:, nc0:nc0 + cw], AF.Sign)
                        cs.inc(s.activation(q_tv[:, nc0:nc0 + cw],
                                            hbuf[:, nc0:nc0 + cw],
                                            AF.Square))
                blk.scalar(_)
                n_sgn_h = cs.n

                marks = {}
                qd_h = emit_quant(
                    hbuf[:], Hq[:], DI,
                    q_tv, q_ta, q_sg, q_bm, q_sc, q_rs,
                    q_r2, q_slb,
                    wait_v=[(cv, n_h_all),
                            (ct, hq_tp_done.get(gi - 1, 0))],
                    ext_prep=n_sgn_h, marks=marks)
                qd_h_hist[gi] = qd_h
                hbuf_free[gi] = marks["bmax"]

            n_gi = n_batch * SB
            emit_x2_load(0)
            emit_x2_load(1)
            stage_a1(0, 0, 0, 0)
            stage_a2(0, 0)
            for b in range(n_batch):
                for sb in range(SB):
                    gi = b * SB + sb
                    r = sb % 2
                    if gi + 1 < n_gi:
                        stage_a1((gi + 1) // SB, (gi + 1) % SB, gi + 1,
                                 (gi + 1) % 2)
                    stage_c1(b, sb, gi, r)
                    if gi >= 1:
                        stage_b1b(gi - 1, (gi - 1) % 2)
                    if gi + 1 < n_gi:
                        stage_a2(gi + 1, (gi + 1) % 2)
                    stage_c2(b, sb, gi, r)
                    stage_b1a(gi, r)
                    if gi >= 1:
                        stage_b2(gi - 1, (gi - 1) % 2)
            stage_b1b(n_gi - 1, (n_gi - 1) % 2)
            stage_b2(n_gi - 1, (n_gi - 1) % 2)

            if debug:
                def _(g):
                    g.wait_ge(cwl.sem, n_ffnw)
                    g.dma_start(out=d_dbg["wgt"], in_=WGT[:]
                                ).then_inc(cst.sem, 16)
                    g.dma_start(out=d_dbg["wut"], in_=WUT[:]
                                ).then_inc(cst.sem, 16)
                    g.dma_start(out=d_dbg["wdt"], in_=WDT[:]
                                ).then_inc(cst.sem, 16)
                    g.wait_ge(cv.sem, cv.n)
                    g.wait_ge(cs.sem, cs.n)
                    last_r = (SB - 1) % 2
                    g.dma_start(out=d_dbg["xq"], in_=xqb[(SB - 1) % 2][:]
                                ).then_inc(cst.sem, 16)
                    g.dma_start(out=d_dbg["hq"], in_=Hq[:]
                                ).then_inc(cst.sem, 16)
                    g.dma_start(out=d_dbg["hbuf"], in_=hbuf[:]
                                ).then_inc(cst.sem, 16)
                    g.dma_start(out=d_dbg["xln2"], in_=x2b[last_r][:]
                                ).then_inc(cst.sem, 16)
                    g.dma_start(out=d_dbg["xqt"], in_=XqT[last_r][:]
                                ).then_inc(cst.sem, 16)
                    g.dma_start(out=d_dbg["hqt"], in_=HqT[last_r][:]
                                ).then_inc(cst.sem, 16)
                    g.dma_start(out=d_dbg["x2"],
                                in_=d_x2.rearrange("b s d -> (b s) d")
                                ).then_inc(cst.sem, 16)
                blk.gpsimd(_)
                cst.n += 160

            def _(g):
                g.wait_ge(cst.sem, cst.n)
                g.wait_ge(cso[0].sem, cso[0].n)
                g.wait_ge(cso[1].sem, cso[1].n)
                g.wait_ge(cfo[0].sem, cfo[0].n)
                g.wait_ge(cfo[1].sem, cfo[1].n)
                g.wait_ge(cqo[0].sem, cqo[0].n)
                g.wait_ge(cqo[1].sem, cqo[1].n)
            blk.gpsimd(_)

    return nc


def kernel(L0_attn_hidden_states, L0_attn_Wq, L0_attn_Wkv, L0_attn_Wo,
           L0_attn_bo, L0_ffn_w_gate, L0_ffn_w_up, L0_ffn_w_down):
    x = np.ascontiguousarray(L0_attn_hidden_states, dtype=np.float32)
    wq = np.ascontiguousarray(L0_attn_Wq, dtype=np.float32)
    wkv = np.ascontiguousarray(L0_attn_Wkv, dtype=np.float32)
    wo = np.ascontiguousarray(L0_attn_Wo, dtype=np.float32)
    bo = np.ascontiguousarray(L0_attn_bo, dtype=np.float32)
    wg = np.ascontiguousarray(L0_ffn_w_gate, dtype=np.float32)
    wu = np.ascontiguousarray(L0_ffn_w_up, dtype=np.float32)
    wd = np.ascontiguousarray(L0_ffn_w_down, dtype=np.float32)

    bo_t = np.ascontiguousarray(np.broadcast_to(bo[None, :], (128, D)))
    ident = np.eye(128, dtype=np.float32)

    nc = build()
    in_maps = []
    for c in range(N_CORES):
        in_maps.append({
            "x": np.ascontiguousarray(x[c * N_BATCH:(c + 1) * N_BATCH]),
            "wq": wq, "wkv": wkv, "wo": wo, "bo_t": bo_t,
            "wg": wg, "wu": wu, "wd": wd, "ident": ident,
        })
    res = run_bass_kernel_spmd(nc, in_maps, core_ids=list(range(N_CORES)))
    return np.concatenate([r["out"] for r in res.results], axis=0)


# version 13
# speedup vs baseline: 1.1341x; 1.0112x over previous
import contextlib
import numpy as np

import concourse.bass as bass
import concourse.mybir as mybir
from concourse.bass_utils import run_bass_kernel_spmd
from concourse.hw_specs import get_activation_tables

F32 = mybir.dt.float32
BF16 = mybir.dt.bfloat16
AF = mybir.ActivationFunctionType
ALU = mybir.AluOpType
AX = mybir.AxisListType

N_CORES = 8
N_BATCH = 4
S = 2048
D = 1024
NH = 16
DH = 64
DI = 2816
SB = S // 128
KT = D // 128
NT = DI // 128
NPAIR = NH // 2
EPS = 1e-5
N_CHUNKS = [(c, min(c + 512, DI)) for c in range(0, DI, 512)]


class Ctr:
    def __init__(self, nc, es, name):
        self.sem = es.enter_context(nc.semaphore(name))
        self.n = 0

    def inc(self, instr, k=1):
        instr.then_inc(self.sem, k)
        self.n += k
        return self.n


def build(n_batch=N_BATCH, debug=False):
    nc = bass.Bass("TRN2", target_bir_lowering=False,
                   detect_race_conditions=False)
    tbl_ids = {name: i for i, name in
               enumerate(get_activation_tables(nc.m.arch))}
    SET_LNEXP = tbl_ids["natural_log_exp_and_others"]
    SET_SILU = tbl_ids["silu_and_others"]

    def load_tbl(eng, set_id):
        eng.add_instruction(mybir.InstLoadActFuncSet(
            name=nc.get_next_instruction_name(), act_func_set_id=set_id,
            ins=[], outs=[]))

    d_x = nc.dram_tensor("x", [n_batch, S, D], F32, kind="ExternalInput").ap()
    d_wq = nc.dram_tensor("wq", [D, D], F32, kind="ExternalInput").ap()
    d_wkv = nc.dram_tensor("wkv", [D, 2 * D], F32, kind="ExternalInput").ap()
    d_wo = nc.dram_tensor("wo", [D, D], F32, kind="ExternalInput").ap()
    d_bo = nc.dram_tensor("bo_t", [128, D], F32, kind="ExternalInput").ap()
    d_wg = nc.dram_tensor("wg", [DI, D], F32, kind="ExternalInput").ap()
    d_wu = nc.dram_tensor("wu", [DI, D], F32, kind="ExternalInput").ap()
    d_wd = nc.dram_tensor("wd", [D, DI], F32, kind="ExternalInput").ap()
    d_id = nc.dram_tensor("ident", [128, 128], F32, kind="ExternalInput").ap()
    d_out = nc.dram_tensor("out", [n_batch, S, D], F32,
                           kind="ExternalOutput").ap()

    if debug:
        d_dbg = {}
        for nm, shape in (
            ("lnxt", [128, KT * 2048]), ("qt", [128, KT * 2048]),
            ("ktb", [128, KT * 2048]), ("v", [128, SB * 1024]),
            ("ot", [128, KT * 2048]), ("xq", [128, D]), ("hq", [128, DI]),
            ("hbuf", [128, DI]), ("xln2", [128, D]), ("xqt", [128, D]),
            ("aqbm", [128, D // 16]), ("axln", [128, D]), ("asq", [128, D]),
            ("ata", [128, D]), ("atsg", [128, D]),
            ("hqt", [128, DI]), ("wgt", [128, KT * DI]),
            ("wut", [128, KT * DI]), ("wdt", [128, NT * D]),
            ("x2", [n_batch * S, D]),
        ):
            d_dbg[nm] = nc.dram_tensor("dbg_" + nm, shape, F32,
                                       kind="ExternalOutput").ap()
    d_x2 = nc.dram_tensor("x2_s", [n_batch, S, D], F32, kind="Internal").ap()
    d_xq = nc.dram_tensor("xq_s", [n_batch, S, D], BF16, kind="Internal").ap()
    d_wgt = nc.dram_tensor("wgt_s", [128, KT * DI], BF16, kind="Internal").ap()
    d_wut = nc.dram_tensor("wut_s", [128, KT * DI], BF16, kind="Internal").ap()
    d_wdt = nc.dram_tensor("wdt_s", [128, NT * D], BF16, kind="Internal").ap()

    es = contextlib.ExitStack()
    with es:
        def sbuf(name, shape, dt):
            return es.enter_context(nc.sbuf_tensor(name, shape, dt))

        cin = Ctr(nc, es, "cin")
        cxl = [Ctr(nc, es, f"cxl{i}") for i in range(2)]
        c3l = [Ctr(nc, es, f"c3l{i}") for i in range(2)]
        cfl = [Ctr(nc, es, f"cfl{i}") for i in range(2)]
        cpl = [Ctr(nc, es, f"cpl{i}") for i in range(2)]
        cso = [Ctr(nc, es, f"cso{i}") for i in range(2)]
        cqo = [Ctr(nc, es, f"cqo{i}") for i in range(2)]
        cql = [Ctr(nc, es, f"cql{i}") for i in range(2)]
        cfo = [Ctr(nc, es, f"cfo{i}") for i in range(2)]
        cwl = Ctr(nc, es, "cwl")
        cst = Ctr(nc, es, "cst")
        ct = Ctr(nc, es, "ct")
        cv = Ctr(nc, es, "cv")
        cs = Ctr(nc, es, "cs")

        blk = es.enter_context(nc.Block())

        bo_bc = sbuf("bo_bc", [128, D], F32)
        idb = sbuf("idb", [128, 128], BF16)
        ones64 = sbuf("ones64", [128, 64], BF16)
        cbias = sbuf("cbias", [128, 2], F32)

        def _(g):
            g.dma_start(out=idb[:], in_=d_id).then_inc(cwl.sem, 16)
            g.dma_start(out=bo_bc[:], in_=d_bo).then_inc(cwl.sem, 16)
        blk.gpsimd(_)
        cwl.n += 32
        n_ld0 = cwl.n

        def _(v):
            v.wait_ge(cwl.sem, n_ld0)
            v.memset(ones64[:], 1.0)
            v.memset(cbias[:, 0:1], EPS)
            cv.inc(v.memset(cbias[:, 1:2], -3.0))
        blk.vector(_)
        n_consts = cv.n

        last_qd = [0]

        def emit_quant(src_ap, dst_ap, F, sq, ta, tsg, bmax, scl, rscl,
                       rscl2, sclb, wait_v=(), wait_s=(), ext_prep=None,
                       marks=None):
            G = F // 16

            if ext_prep is None:
                def _(s):
                    for ctr, n in wait_s:
                        if n > 0:
                            s.wait_ge(ctr.sem, n)
                    if last_qd[0] > 0:
                        s.wait_ge(cv.sem, last_qd[0])
                    load_tbl(s, SET_LNEXP)
                    s.activation(tsg[:, 0:F], src_ap, AF.Sign)
                    cs.inc(s.activation(sq[:, 0:F], src_ap, AF.Square))
                blk.scalar(_)
                n_prep = cs.n
            else:
                n_prep = ext_prep

            def _(v):
                for ctr, n in wait_v:
                    if n > 0:
                        v.wait_ge(ctr.sem, n)
                src3 = src_ap.rearrange("p (g w) -> p g w", w=16)
                n_red = cv.inc(v.tensor_reduce(
                    bmax[:, 0:G].rearrange("p (g o) -> p g o", o=1), src3,
                    axis=AX.X, op=ALU.max, apply_absolute_value=True))
                if marks is not None:
                    marks["bmax"] = n_red
                v.wait_ge(cv.sem, n_red)
                v.tensor_scalar(scl[:, 0:G], bmax[:, 0:G], 1.0 / 6.0, 1e-12,
                                op0=ALU.mult, op1=ALU.max)
                n_rc = cv.inc(v.reciprocal(rscl[:, 0:G], scl[:, 0:G]))
                v.wait_ge(cv.sem, n_rc)
                v.tensor_tensor(rscl2[:, 0:G], rscl[:, 0:G], rscl[:, 0:G],
                                op=ALU.mult)
                v.tensor_copy(sclb[:, 0:G], scl[:, 0:G])
                v.wait_ge(cs.sem, n_prep)
                r2b = rscl2[:, 0:G].rearrange("p (g o) -> p g o", o=1) \
                    .broadcast_to([128, G, 16])
                sqf = sq[:, 0:F]
                v.tensor_tensor(sqf.rearrange("p (g w) -> p g w", w=16),
                                sqf.rearrange("p (g w) -> p g w", w=16),
                                r2b, op=ALU.mult)
                v.tensor_copy(dst_ap, sqf)
                v.tensor_scalar(ta[:, 0:F], dst_ap, 0.0625, None,
                                op0=ALU.is_gt)
                for t in (0.5625, 1.5625, 3.0625):
                    v.scalar_tensor_tensor(ta[:, 0:F], dst_ap, t, ta[:, 0:F],
                                           op0=ALU.is_gt, op1=ALU.add)
                v.tensor_scalar(ta[:, 0:F], ta[:, 0:F], 0.5, None,
                                op0=ALU.mult)
                for t in (6.25, 12.25):
                    v.scalar_tensor_tensor(ta[:, 0:F], sqf, t, ta[:, 0:F],
                                           op0=ALU.is_gt, op1=ALU.add)
                v.tensor_scalar(dst_ap, sqf, 25.0, 2.0,
                                op0=ALU.is_gt, op1=ALU.mult)
                v.tensor_tensor(ta[:, 0:F], ta[:, 0:F], dst_ap, op=ALU.add)
                sb3 = sclb[:, 0:G].rearrange("p (g o) -> p g o", o=1) \
                    .broadcast_to([128, G, 16])
                v.tensor_tensor(tsg[:, 0:F].rearrange("p (g w) -> p g w", w=16),
                                tsg[:, 0:F].rearrange("p (g w) -> p g w", w=16),
                                sb3, op=ALU.mult)
                cv.inc(v.tensor_tensor(dst_ap, ta[:, 0:F], tsg[:, 0:F],
                                       op=ALU.mult))
            blk.vector(_)
            last_qd[0] = cv.n
            return cv.n

        def emit_ln(src_ap, scratch_ap, dst_ap, stat_t, c0, wait_load,
                    col_guard=0, sq_waits=(), norm_waits=(), sign_dst=None):
            xs, mn, nmn, ssr, nm2, nm2e, var_, lnv, rs, nb = (
                stat_t[:, c0 + i:c0 + i + 1] for i in range(10))

            def _(v):
                for ctr, n in wait_load:
                    v.wait_ge(ctr.sem, n)
                if col_guard > 0:
                    v.wait_ge(cs.sem, col_guard)
                n_red = cv.inc(v.reduce_sum(xs, src_ap, axis=AX.X))
                v.wait_ge(cv.sem, n_red)
                v.tensor_scalar(mn, xs, 1.0 / D, None, op0=ALU.mult)
                cv.inc(v.tensor_scalar(nmn, xs, -1.0 / D, None, op0=ALU.mult))
            blk.vector(_)
            n1 = cv.n

            def _(s):
                for ctr, n in wait_load:
                    s.wait_ge(ctr.sem, n)
                for ctr, n in sq_waits:
                    s.wait_ge(ctr.sem, n)
                if sign_dst is not None and last_qd[0] > 0:
                    s.wait_ge(cv.sem, last_qd[0])
                load_tbl(s, SET_LNEXP)
                s.activation(scratch_ap, src_ap, AF.Square, accum_out=ssr)
                s.wait_ge(cv.sem, n1)
                s.activation(nm2, nmn, AF.Identity, scale=mn, bias=0.0)
                n_a = cs.inc(s.activation(nm2e, nm2, AF.Identity,
                                          bias=cbias[:, 0:1]))
                s.wait_ge(cs.sem, n_a)
                s.activation(var_, ssr, AF.Identity, scale=1.0 / D,
                             bias=nm2e)
                s.activation(lnv, var_, AF.Ln)
                n_b = cs.inc(s.activation(rs, lnv, AF.Exp, scale=-0.5))
                s.wait_ge(cs.sem, n_b)
                n_c = cs.inc(s.activation(nb, nmn, AF.Identity, scale=rs,
                                          bias=0.0))
                s.wait_ge(cs.sem, n_c)
                for ctr, n in norm_waits:
                    s.wait_ge(ctr.sem, n)
                cs.inc(s.activation(dst_ap, src_ap, AF.Identity,
                                    scale=rs, bias=nb))
                if sign_dst is not None:
                    sgn_ap, sq_ap = sign_dst
                    s.activation(sgn_ap, dst_ap, AF.Sign)
                    cs.inc(s.activation(sq_ap, dst_ap, AF.Square))
            blk.scalar(_)
            return cs.n

        p0 = contextlib.ExitStack()
        with p0:
            def p0_sbuf(name, shape, dt):
                return p0.enter_context(nc.sbuf_tensor(name, shape, dt))

            p0_wtile = [p0_sbuf(f"p0w{i}", [128, DI], F32) for i in range(2)]
            p0_q = [p0_sbuf(f"p0q{i}", [128, DI], BF16) for i in range(2)]
            p0_tv = p0_sbuf("p0tv", [128, DI], F32)
            p0_ta = p0_sbuf("p0ta", [128, DI], BF16)
            p0_sg = p0_sbuf("p0sg", [128, DI], BF16)
            p0_bm = p0_sbuf("p0bm", [128, DI // 16], F32)
            p0_r2 = p0_sbuf("p0r2", [128, DI // 16], F32)
            p0_sb = p0_sbuf("p0sb", [128, DI // 16], BF16)
            p0_sc = p0_sbuf("p0sc", [128, DI // 16], F32)
            p0_rs = p0_sbuf("p0rs", [128, DI // 16], F32)
            p0_stage = p0_sbuf("p0_stage", [128, KT * DI], BF16)
            p0_tp = p0.enter_context(
                nc.psum_tensor("p0_tp", [128, 1024], BF16))

            def prep_weight(d_src, d_dst, n_tiles, tile_f, tp_rows):
                nblk = tile_f // 128
                quant_done = {}
                evac_done = [0]
                tp_done = [0, 0]
                entry_st = cst.n
                for t in range(n_tiles):
                    r = t % 2

                    def _(g, t=t, r=r):
                        if t >= 2:
                            g.wait_ge(cv.sem, quant_done[t - 2])
                        g.dma_start(out=p0_wtile[r][:, 0:tile_f],
                                    in_=d_src[t * 128:(t + 1) * 128, 0:tile_f]
                                    ).then_inc(cpl[r].sem, 16)
                    blk.gpsimd(_)
                    cpl[r].n += 16
                    n_ld = cpl[r].n
                    qd = emit_quant(
                        p0_wtile[r][:, 0:tile_f], p0_q[r][:, 0:tile_f],
                        tile_f,
                        p0_tv, p0_ta, p0_sg, p0_bm, p0_sc, p0_rs,
                        p0_r2, p0_sb,
                        wait_v=[(cpl[r], n_ld), (ct, tp_done[1])],
                        wait_s=[(cpl[r], n_ld)])
                    quant_done[t] = qd
                    for g0 in range(0, nblk, 8):
                        ng = min(8, nblk - g0)

                        def _(te, g0=g0, ng=ng, qd=qd, t=t):
                            te.wait_ge(cv.sem, max(qd, n_consts))
                            if evac_done[0] > 0:
                                te.wait_ge(cv.sem, evac_done[0])
                            last = None
                            for j in range(ng):
                                last = te.transpose(
                                    p0_tp[:, j * 128:(j + 1) * 128],
                                    p0_q[t % 2][:, (g0 + j) * 128:
                                                 (g0 + j + 1) * 128],
                                    idb[:])
                            ct.inc(last)
                        blk.tensor(_)
                        n_tp = ct.n

                        def _(v, g0=g0, ng=ng, t=t, n_tp=n_tp):
                            v.wait_ge(ct.sem, n_tp)
                            if t == 0 and g0 == 0 and entry_st > 0:
                                v.wait_ge(cst.sem, entry_st)
                            src = p0_tp[:, 0:ng * 128].rearrange(
                                "p (j c) -> p j c", c=128)
                            dst = p0_stage[:, 0:nblk * tp_rows].rearrange(
                                "p (j n) -> p j n", n=tp_rows)[
                                :, g0:g0 + ng, t * 128:(t + 1) * 128]
                            cv.inc(v.tensor_copy(dst, src))
                        blk.vector(_)
                        evac_done[0] = cv.n
                    tp_done[1] = tp_done[0]
                    tp_done[0] = ct.n

                def _(g):
                    g.wait_ge(cv.sem, evac_done[0])
                    g.dma_start(out=d_dst, in_=p0_stage[:, 0:nblk * tp_rows]
                                ).then_inc(cst.sem, 16)
                blk.gpsimd(_)
                cst.n += 16
                return cst.n

            prep_weight(d_wg, d_wgt, NT, D, DI)
            prep_weight(d_wu, d_wut, NT, D, DI)
            st_prep = prep_weight(d_wd, d_wdt, KT, DI, D)

        pa = contextlib.ExitStack()
        with pa:
            def pas(name, shape, dt):
                return pa.enter_context(nc.sbuf_tensor(name, shape, dt))

            WW = pas("WW", [128, KT * 2048], BF16)
            WoB = pas("WoB", [128, KT * 1024], BF16)
            LNXT = pas("LNXT", [128, KT * 2048], BF16)
            QT = pas("QT", [128, KT * 2048], BF16)
            KTb = pas("KTb", [128, KT * 2048], BF16)
            V = pas("V", [128, SB * 1024], BF16)
            EK = [pas(f"EK{i}", [128, 1024], BF16) for i in range(2)]
            RC = pas("RC", [128, 512], F32)
            xbuf = [pas(f"xbuf{i}", [128, D], F32) for i in range(2)]
            aq_bm = pas("aq_bm", [128, D // 16], F32)
            aq_sc = pas("aq_sc", [128, D // 16], F32)
            aq_rs = pas("aq_rs", [128, D // 16], F32)
            aq_r2 = pas("aq_r2", [128, D // 16], F32)
            aq_sb = pas("aq_sb", [128, D // 16], BF16)
            x3buf = [pas(f"x3buf{i}", [128, D], F32) for i in range(2)]
            lnx = [pas(f"lnx{i}", [128, D], BF16) for i in range(2)]
            stat = pas("stat", [128, 40], F32)

            def _(g):
                g.wait_ge(cst.sem, st_prep)
                for k in range(KT):
                    g.dma_start(out=WoB[:, k * 1024:(k + 1) * 1024],
                                in_=d_wo[k * 128:(k + 1) * 128, :]
                                ).then_inc(cwl.sem, 16)
            blk.gpsimd(_)
            cwl.n += 16 * 8

            dbg_gate = [0]
            prev_a2_cs = [0]
            prev_lnx_guard = [0]
            prev_ln2_end = [0]
            prev_xq_store = [(0, 0)]
            x2_store_all = {}
            xq_store_all = {}
            prev_t_end = 0
            prev_v_end = 0
            prev_qkv_t = 0
            prev_store_last = (0, 0)

            for b in range(n_batch):
                bps = contextlib.ExitStack()
                with bps:
                    pq = [bps.enter_context(nc.psum_tensor(
                        f"pq{b}_{i}", [128, 1024], F32)) for i in range(3)]
                    tp2 = [bps.enter_context(nc.psum_tensor(
                        f"tp{b}_{i}", [128, 1024], BF16)) for i in range(2)]

                    def _(g, b=b):
                        if b > 0:
                            g.wait_ge(ct.sem, prev_qkv_t)
                        for k in range(KT):
                            g.dma_start(out=WW[:, k * 2048:k * 2048 + 1024],
                                        in_=d_wq[k * 128:(k + 1) * 128, :]
                                        ).then_inc(cwl.sem, 16)
                    blk.gpsimd(_)
                    cwl.n += 16 * KT
                    n_wq = cwl.n

                    norm_done = {}
                    tpos_done = {}
                    lnxt_evac = {}
                    for sb in range(SB):
                        r = sb % 2

                        def _(sy, b=b, sb=sb, r=r):
                            if sb >= 2:
                                sy.wait_ge(cs.sem, norm_done[sb - 2])
                            if b == 0 and sb == 0:
                                sy.wait_ge(cst.sem, st_prep)
                            if b > 0 and sb < 2:
                                sy.wait_ge(cs.sem, prev_ln2_end[0])
                                sy.wait_ge(cv.sem, prev_lnx_guard[0])
                            sy.dma_start(
                                out=xbuf[r][:],
                                in_=d_x[b, sb * 128:(sb + 1) * 128, :]
                            ).then_inc(cxl[r].sem, 16)
                        blk.sync(_)
                        cxl[r].n += 16
                        n_ldx = cxl[r].n

                        sq_w = []
                        if b > 0 and sb < 2:
                            sq_w.append((cso[0], prev_store_last[0]))
                            sq_w.append((cso[1], prev_store_last[1]))
                        nrm_w = []
                        if sb >= 2:
                            nrm_w.append((ct, tpos_done[sb - 2]))
                        if b > 0 and sb < 2:
                            nrm_w.append((cv, prev_lnx_guard[0]))
                        norm_done[sb] = emit_ln(
                            xbuf[r][:], x3buf[r][:], lnx[r][:], stat, r * 10,
                            wait_load=[(cxl[r], n_ldx)],
                            col_guard=(norm_done[sb - 2] if sb >= 2 else 0),
                            sq_waits=sq_w, norm_waits=nrm_w)

                        def _(te, r=r, sb=sb, b=b):
                            te.wait_ge(cs.sem, norm_done[sb])
                            te.wait_ge(cv.sem, n_consts)
                            if sb >= 2:
                                te.wait_ge(cv.sem, lnxt_evac[sb - 2])
                            if b > 0 and sb == 0:
                                te.wait_ge(cv.sem, prev_v_end)
                            last = None
                            for j in range(KT):
                                last = te.transpose(
                                    tp2[sb % 2][:, j * 128:(j + 1) * 128],
                                    lnx[r][:, j * 128:(j + 1) * 128], idb[:])
                            ct.inc(last)
                        blk.tensor(_)
                        tpos_done[sb] = ct.n

                        def _(v, sb=sb, b=b):
                            v.wait_ge(ct.sem, tpos_done[sb])
                            src = tp2[sb % 2][:].rearrange(
                                "p (j c) -> p j c", c=128)
                            dst = LNXT[:].rearrange(
                                "p (j q) -> p j q", q=2048)[
                                :, :, sb * 128:(sb + 1) * 128]
                            cv.inc(v.tensor_copy(dst, src))
                        blk.vector(_)
                        lnxt_evac[sb] = cv.n
                    n_lnxt_all = cv.n

                    pq_evac = []
                    uidx = [0]

                    def qkv_unit(stat_fn, rhs_fn, out_ap):
                        u = uidx[0] % 3
                        myu = uidx[0]
                        my_attw = n_attw

                        def _(te):
                            te.wait_ge(cv.sem, n_lnxt_all)
                            te.wait_ge(cwl.sem, my_attw)
                            if myu >= 3:
                                te.wait_ge(cv.sem, pq_evac[myu - 3])
                            last = None
                            for k in range(KT):
                                lhsT = stat_fn(k)
                                for c in range(2):
                                    last = te.matmul(
                                        pq[u][:, c * 512:(c + 1) * 512],
                                        lhsT, rhs_fn(k, c),
                                        start=(k == 0), stop=(k == KT - 1))
                            ct.inc(last)
                        blk.tensor(_)
                        n_mm = ct.n

                        def _(v, u=u, out_ap=out_ap, n_mm=n_mm):
                            v.wait_ge(ct.sem, n_mm)
                            cv.inc(v.tensor_copy(out_ap, pq[u][:]))
                        blk.vector(_)
                        pq_evac.append(cv.n)
                        uidx[0] += 1

                    n_attw = n_wq
                    for j in range(KT):
                        for half in range(2):
                            qkv_unit(
                                (lambda k, j=j:
                                 WW[:, k * 2048 + j * 128:
                                    k * 2048 + (j + 1) * 128]),
                                (lambda k, c, half=half:
                                 LNXT[:, k * 2048 + half * 1024 + c * 512:
                                      k * 2048 + half * 1024 + (c + 1) * 512]),
                                QT[:, j * 2048 + half * 1024:
                                   j * 2048 + (half + 1) * 1024])
                    n_q_mm = ct.n

                    def _(g, n_q_mm=n_q_mm):
                        g.wait_ge(ct.sem, n_q_mm)
                        for k in range(KT):
                            g.dma_start(out=WW[:, k * 2048:(k + 1) * 2048],
                                        in_=d_wkv[k * 128:(k + 1) * 128, :]
                                        ).then_inc(cwl.sem, 16)
                    blk.gpsimd(_)
                    cwl.n += 16 * KT
                    n_attw = cwl.n

                    for j in range(KT):
                        for half in range(2):
                            qkv_unit(
                                (lambda k, j=j:
                                 WW[:, k * 2048 + j * 128:
                                    k * 2048 + (j + 1) * 128]),
                                (lambda k, c, half=half:
                                 LNXT[:, k * 2048 + half * 1024 + c * 512:
                                      k * 2048 + half * 1024 + (c + 1) * 512]),
                                KTb[:, j * 2048 + half * 1024:
                                    j * 2048 + (half + 1) * 1024])
                    for t in range(SB):
                        qkv_unit(
                            (lambda k, t=t:
                             LNXT[:, k * 2048 + t * 128:
                                  k * 2048 + (t + 1) * 128]),
                            (lambda k, c:
                             WW[:, k * 2048 + 1024 + c * 512:
                                k * 2048 + 1024 + (c + 1) * 512]),
                            V[:, t * 1024:(t + 1) * 1024])
                    n_qkv_t = ct.n
                    n_qkv_v = cv.n

                    if debug and b == 0:
                        def _(g):
                            g.wait_ge(cv.sem, n_qkv_v)
                            g.dma_start(out=d_dbg["lnxt"], in_=LNXT[:]
                                        ).then_inc(cst.sem, 16)
                            g.dma_start(out=d_dbg["qt"], in_=QT[:]
                                        ).then_inc(cst.sem, 16)
                            g.dma_start(out=d_dbg["ktb"], in_=KTb[:]
                                        ).then_inc(cst.sem, 16)
                            g.dma_start(out=d_dbg["v"], in_=V[:]
                                        ).then_inc(cst.sem, 16)
                        blk.gpsimd(_)
                        cst.n += 64
                        dbg_gate[0] = cst.n

                ba2 = contextlib.ExitStack()
                with ba2:
                    sps = [ba2.enter_context(nc.psum_tensor(
                        f"sps{b}_{i}", [128, 1024], F32)) for i in range(3)]
                    pvo = ba2.enter_context(
                        nc.psum_tensor(f"pvo{b}", [128, 512], F32))
                    pvd = ba2.enter_context(
                        nc.psum_tensor(f"pvd{b}", [128, 512], F32))

                    s_done = {}
                    exp_done = {}
                    pv_done = {}
                    ot_write = [0]
                    for p in range(NPAIR):
                        key0 = p * 64
                        cs_pre = cs.n
                        for i in range(64):
                            exp_done[key0 + i] = cs_pre + i + 1
                        for qc in range(4):
                            def _(te, p=p, qc=qc):
                                def emit_s(kb):
                                    key = p * 64 + qc * SB + kb
                                    sp = sps[kb % 3]
                                    if key == 0:
                                        te.wait_ge(ct.sem, n_qkv_t)
                                        te.wait_ge(cv.sem, n_qkv_v)
                                        if prev_a2_cs[0] > 0:
                                            te.wait_ge(cs.sem, prev_a2_cs[0])
                                    if key >= 3:
                                        te.wait_ge(cs.sem, exp_done[key - 3])
                                    la = KTb[0:64, p * 2048 + kb * 128:
                                             p * 2048 + (kb + 1) * 128]
                                    ra = QT[0:64, p * 2048 + qc * 512:
                                            p * 2048 + (qc + 1) * 512]
                                    lb = KTb[64:128, p * 2048 + kb * 128:
                                             p * 2048 + (kb + 1) * 128]
                                    rb = QT[64:128, p * 2048 + qc * 512:
                                            p * 2048 + (qc + 1) * 512]
                                    te.matmul(sp[:, 0:512], la, ra,
                                              start=True, stop=True,
                                              tile_position=(0, 0))
                                    ct.inc(te.matmul(sp[:, 512:1024], lb, rb,
                                                     start=True, stop=True,
                                                     tile_position=(64, 0)))
                                    s_done[key] = ct.n

                                def emit_pv(kb):
                                    key = p * 64 + qc * SB + kb
                                    te.wait_ge(cs.sem, exp_done[key])
                                    if kb == 0:
                                        te.wait_ge(cv.sem, ot_write[0])
                                    v3 = V[:].rearrange(
                                        "p (t c) -> p t c", c=1024)
                                    va = v3[:, kb, (2 * p) * 64:
                                            (2 * p) * 64 + 64]
                                    vb = v3[:, kb, (2 * p + 1) * 64:
                                            (2 * p + 1) * 64 + 64]
                                    ea = EK[kb % 2][:, 0:512]
                                    eb = EK[kb % 2][:, 512:1024]
                                    st_, sp_ = (kb == 0), (kb == SB - 1)
                                    te.matmul(pvo[0:64, :], va, ea,
                                              start=st_, stop=sp_,
                                              tile_position=(0, 0),
                                              skip_group_check=True)
                                    te.matmul(pvo[64:128, :], vb, eb,
                                              start=st_, stop=sp_,
                                              tile_position=(0, 64),
                                              skip_group_check=True)
                                    te.matmul(pvd[0:64, :], ones64[:], ea,
                                              start=st_, stop=sp_,
                                              tile_position=(0, 0),
                                              skip_group_check=True)
                                    ct.inc(te.matmul(pvd[64:128, :],
                                                     ones64[:], eb,
                                                     start=st_, stop=sp_,
                                                     tile_position=(0, 64),
                                                     skip_group_check=True))
                                    pv_done[key] = ct.n

                                for kb in range(SB):
                                    emit_s(kb)
                                    if kb >= 2:
                                        emit_pv(kb - 2)
                                emit_pv(SB - 2)
                                emit_pv(SB - 1)
                            blk.tensor(_)
                            n_pv = ct.n

                            def _(v, p=p, qc=qc, n_pv=n_pv):
                                v.wait_ge(ct.sem, n_pv)
                                if dbg_gate[0]:
                                    v.wait_ge(cst.sem, dbg_gate[0])
                                    dbg_gate[0] = 0
                                n_rc = cv.inc(v.reciprocal(RC[:], pvd[:]))
                                v.wait_ge(cv.sem, n_rc)
                                base = p * 2048 + qc * 512
                                v.tensor_tensor(LNXT[0:64, base:base + 512],
                                                pvo[0:64, :], RC[0:64, :],
                                                op=ALU.mult)
                                cv.inc(v.tensor_tensor(
                                    LNXT[64:128, base:base + 512],
                                    pvo[64:128, :], RC[64:128, :],
                                    op=ALU.mult))
                            blk.vector(_)
                            ot_write[0] = cv.n

                        def _(s, p=p, cs_pre=cs_pre):
                            load_tbl(s, SET_LNEXP)
                            for qc in range(4):
                                for kb in range(SB):
                                    key = p * 64 + qc * SB + kb
                                    s.wait_ge(ct.sem, s_done[key])
                                    if key < 2 and prev_xq_store[0][kb] > 0:
                                        s.wait_ge(cqo[kb].sem,
                                                  prev_xq_store[0][kb])
                                    if key >= 2:
                                        s.wait_ge(ct.sem, pv_done[key - 2])
                                    cs.inc(s.activation(
                                        EK[kb % 2][:], sps[kb % 3][:],
                                        AF.Exp, bias=cbias[:, 1:2],
                                        scale=0.125))
                                    assert cs.n == exp_done[key], \
                                        (cs.n, exp_done[key])
                        blk.scalar(_)
                    n_ot_all = cv.n

                if debug and b == 0:
                    def _(g):
                        g.wait_ge(cv.sem, n_ot_all)
                        g.dma_start(out=d_dbg["ot"], in_=LNXT[:]
                                    ).then_inc(cst.sem, 16)
                    blk.gpsimd(_)
                    cst.n += 16
                    dbg_a3 = [True]

                prev_a2_cs[0] = cs.n

                ba3 = contextlib.ExitStack()
                with ba3:
                    wop = [ba3.enter_context(nc.psum_tensor(
                        f"wop{b}_{i}", [128, 1024], F32)) for i in range(2)]
                    x2_evac = {}
                    x2_store = {}
                    xq_store = {}
                    ln2a_done = {}
                    ln2a_qd = {}
                    for sb in range(SB):
                        r = sb % 2

                        def _(sy, b=b, sb=sb, r=r):
                            if sb >= 2:
                                sy.wait_ge(cso[r].sem, x2_store[sb - 2])
                                sy.wait_ge(cv.sem, ln2a_qd[sb - 2])
                            else:
                                sy.wait_ge(cs.sem, norm_done[SB - 1])
                            sy.dma_start(
                                out=x3buf[r][:],
                                in_=d_x[b, sb * 128:(sb + 1) * 128, :]
                            ).then_inc(c3l[r].sem, 16)
                        blk.sync(_)
                        c3l[r].n += 16
                        n_ldx3 = c3l[r].n

                        def _(te, sb=sb, r=r):
                            te.wait_ge(cv.sem, n_ot_all)
                            if sb >= 2:
                                te.wait_ge(cv.sem, x2_evac[sb - 2])
                            last = None
                            for j in range(KT):
                                statp = LNXT[:, j * 2048 + sb * 128:
                                             j * 2048 + (sb + 1) * 128]
                                for c in range(2):
                                    last = te.matmul(
                                        wop[r][:, c * 512:(c + 1) * 512],
                                        statp,
                                        WoB[:, j * 1024 + c * 512:
                                            j * 1024 + (c + 1) * 512],
                                        start=(j == 0), stop=(j == KT - 1))
                            ct.inc(last)
                        blk.tensor(_)
                        n_wo = ct.n

                        def _(v, sb=sb, r=r, n_wo=n_wo, n_ldx3=n_ldx3):
                            v.wait_ge(ct.sem, n_wo)
                            v.wait_ge(c3l[r].sem, n_ldx3)
                            v.tensor_tensor(x3buf[r][:], x3buf[r][:],
                                            wop[r][:], op=ALU.add)
                            cv.inc(v.tensor_tensor(x3buf[r][:], x3buf[r][:],
                                                   bo_bc[:], op=ALU.add))
                        blk.vector(_)
                        x2_evac[sb] = cv.n

                        def _(g, b=b, sb=sb, r=r):
                            g.wait_ge(cv.sem, x2_evac[sb])
                            g.dma_start(
                                out=d_x2[b, sb * 128:(sb + 1) * 128, :],
                                in_=x3buf[r][:]).then_inc(cso[r].sem, 16)
                        blk.gpsimd(_)
                        cso[r].n += 16
                        x2_store[sb] = cso[r].n

                        nrm2_w = []
                        if b > 0 and sb < 2:
                            nrm2_w.append((cv, prev_lnx_guard[0]))
                        ln2a_done[sb] = emit_ln(
                            x3buf[r][:], xbuf[0][:], xbuf[1][:], stat, 20,
                            wait_load=[(cv, x2_evac[sb])],
                            col_guard=(ln2a_done[sb - 1] if sb >= 1 else 0),
                            sq_waits=([(cv, ln2a_qd[sb - 1])] if sb >= 1
                                      else []),
                            norm_waits=nrm2_w,
                            sign_dst=(lnx[1][:], xbuf[0][:]))
                        qd = emit_quant(
                            xbuf[1][:], EK[r][:], D,
                            xbuf[0], lnx[0], lnx[1],
                            aq_bm, aq_sc, aq_rs, aq_r2, aq_sb,
                            wait_v=[(cs, ln2a_done[sb])] +
                                   ([] if sb < 2 else
                                    [(cqo[r], xq_store[sb - 2])]),
                            ext_prep=ln2a_done[sb])
                        ln2a_qd[sb] = qd

                        def _(g, b=b, sb=sb, r=r, qd=qd):
                            g.wait_ge(cv.sem, qd)
                            g.dma_start(
                                out=d_xq[b, sb * 128:(sb + 1) * 128, :],
                                in_=EK[r][:]).then_inc(cqo[r].sem, 16)
                        blk.gpsimd(_)
                        cqo[r].n += 16
                        xq_store[sb] = cqo[r].n
                    if debug and b == 0:
                        def _(g):
                            g.wait_ge(cv.sem, ln2a_qd[SB - 1])
                            g.dma_start(out=d_dbg["aqbm"], in_=aq_bm[:]
                                        ).then_inc(cst.sem, 16)
                            g.dma_start(out=d_dbg["axln"], in_=xbuf[1][:]
                                        ).then_inc(cst.sem, 16)
                            g.dma_start(out=d_dbg["asq"], in_=xbuf[0][:]
                                        ).then_inc(cst.sem, 16)
                            g.dma_start(out=d_dbg["ata"], in_=lnx[0][:]
                                        ).then_inc(cst.sem, 16)
                            g.dma_start(out=d_dbg["atsg"], in_=lnx[1][:]
                                        ).then_inc(cst.sem, 16)
                        blk.gpsimd(_)
                        cst.n += 80
                    prev_lnx_guard[0] = ln2a_qd[SB - 1]
                    prev_ln2_end[0] = ln2a_done[SB - 1]
                    prev_xq_store[0] = (cqo[0].n, cqo[1].n)
                    prev_t_end = ct.n
                    prev_v_end = cv.n
                    prev_qkv_t = n_qkv_t
                    prev_store_last = (cso[0].n, cso[1].n)
                    for _sb in range(SB):
                        x2_store_all[b * SB + _sb] = (_sb % 2, x2_store[_sb])
                        xq_store_all[b * SB + _sb] = (_sb % 2, xq_store[_sb])

        pa_t_end, pa_v_end, pa_s_end, pa_st_end = ct.n, cv.n, cs.n, cst.n
        pa_so = (cso[0].n, cso[1].n)
        pa_qo = (cqo[0].n, cqo[1].n)

        pf = contextlib.ExitStack()
        with pf:
            def pfs(name, shape, dt):
                return pf.enter_context(nc.sbuf_tensor(name, shape, dt))

            WGT = pfs("WGT", [128, KT * DI], BF16)
            WUT = pfs("WUT", [128, KT * DI], BF16)
            WDT = pfs("WDT", [128, NT * D], BF16)
            x2b = [pfs(f"x2b{i}", [128, D], F32) for i in range(2)]
            xqb = [pfs(f"xqb{i}", [128, D], BF16) for i in range(2)]
            XqT = [pfs(f"XqT{i}", [128, D], BF16) for i in range(2)]
            hbuf = pfs("hbuf", [128, DI], F32)
            Hq = pfs("Hq", [128, DI], BF16)
            HqT = [pfs(f"HqT{i}", [128, DI], BF16) for i in range(2)]
            q_tv = pfs("q_tv", [128, DI], F32)
            q_ta = pfs("q_ta", [128, DI], BF16)
            q_sg = pfs("q_sg", [128, DI], BF16)
            q_bm = pfs("q_bm", [128, DI // 16], F32)
            q_r2 = pfs("q_r2", [128, DI // 16], F32)
            q_slb = pfs("q_slb", [128, DI // 16], BF16)
            q_sc = pfs("q_sc", [128, DI // 16], F32)
            q_rs = pfs("q_rs", [128, DI // 16], F32)
            stat2 = pfs("stat2", [128, 24], F32)

            gps = [pf.enter_context(nc.psum_tensor(
                f"gps{i}", [128, 512], F32)) for i in range(2)]
            ups = [pf.enter_context(nc.psum_tensor(
                f"ups{i}", [128, 512], F32)) for i in range(2)]
            tps = [pf.enter_context(nc.psum_tensor(
                f"tps{i}", [128, 1024], BF16)) for i in range(2)]
            dps = pf.enter_context(nc.psum_tensor("dps", [128, 1024], F32))

            def _(g):
                g.wait_ge(ct.sem, pa_t_end)
                g.wait_ge(cv.sem, pa_v_end)
                g.wait_ge(cs.sem, pa_s_end)
                g.wait_ge(cst.sem, pa_st_end)
                g.wait_ge(cso[0].sem, pa_so[0])
                g.wait_ge(cso[1].sem, pa_so[1])
                g.wait_ge(cqo[0].sem, pa_qo[0])
                g.wait_ge(cqo[1].sem, pa_qo[1])
                g.dma_start(out=WGT[:], in_=d_wgt).then_inc(cwl.sem, 16)
                g.dma_start(out=WUT[:], in_=d_wut).then_inc(cwl.sem, 16)
                g.dma_start(out=WDT[:], in_=d_wdt).then_inc(cwl.sem, 16)
            blk.gpsimd(_)
            cwl.n += 48
            n_ffnw = cwl.n

            out_store = {}
            xq_tp_done = {}
            xqt_evac = {}
            hq_tp_done = {}
            hqt_last = {}
            gu_last = {}
            guk_done = {}
            h_done = {}
            qd_h_hist = {}
            down_done = {}
            x2_used = {}

            hbuf_free = {}
            gu_rec_all = {}
            nxqt_all = {}
            nxqtp_all = {}
            nhall_all = {}
            ldx2_all = {}
            hqt_g1_evac = {}
            NCH = len(N_CHUNKS)

            def emit_gu_chunk(gi, r, ci):
                nc0, nc1 = N_CHUNKS[ci]
                cw = nc1 - nc0
                pr = ci % 2
                kk = gi * NCH + ci
                n_xqt = nxqt_all[gi]

                def _(te, r=r, nc0=nc0, nc1=nc1, cw=cw, pr=pr, kk=kk,
                      n_xqt=n_xqt):
                    te.wait_ge(cv.sem, n_xqt)
                    if kk >= 2:
                        te.wait_ge(cv.sem, h_done[kk - 2])
                    for k in range(KT):
                        te.matmul(gps[pr][:, 0:cw],
                                  XqT[r][:, k * 128:(k + 1) * 128],
                                  WGT[:, k * DI + nc0:k * DI + nc1],
                                  start=(k == 0), stop=(k == KT - 1))
                    last = None
                    for k in range(KT):
                        last = te.matmul(
                            ups[pr][:, 0:cw],
                            XqT[r][:, k * 128:(k + 1) * 128],
                            WUT[:, k * DI + nc0:k * DI + nc1],
                            start=(k == 0), stop=(k == KT - 1))
                    ct.inc(last)
                blk.tensor(_)
                guk_done[kk] = ct.n
                gu_rec_all[gi][ci] = ct.n

            def stage_a1(b, sb, gi, r):
                def _(sy, b=b, sb=sb, gi=gi):
                    slot = gi % 2
                    if gi >= 2:
                        sy.wait_ge(ct.sem, xq_tp_done[gi - 2])
                    qslot, qval = xq_store_all[gi]
                    sy.wait_ge(cqo[qslot].sem, qval)
                    sy.dma_start(
                        out=xqb[slot][:],
                        in_=d_xq[b, sb * 128:(sb + 1) * 128, :]
                    ).then_inc(cql[slot].sem, 16)
                blk.sync(_)
                cql[gi % 2].n += 16
                n_ldxq = cql[gi % 2].n

                def _(te, gi=gi, n_ldxq=n_ldxq):
                    te.wait_ge(cql[gi % 2].sem, n_ldxq)
                    if gi == 0:
                        te.wait_ge(cwl.sem, n_ffnw)
                    if gi >= 1:
                        te.wait_ge(cv.sem, xqt_evac[gi - 1])
                    if gi >= 2:
                        te.wait_ge(cv.sem, hqt_g1_evac[gi - 2])
                    last = None
                    for j in range(KT):
                        last = te.transpose(
                            tps[0][:, j * 128:(j + 1) * 128],
                            xqb[gi % 2][:, j * 128:(j + 1) * 128], idb[:])
                    ct.inc(last)
                blk.tensor(_)
                xq_tp_done[gi] = ct.n
                nxqtp_all[gi] = ct.n

            def stage_a2(gi, r):
                n_xqtp = nxqtp_all[gi]

                def _(v, gi=gi, r=r, n_xqtp=n_xqtp):
                    v.wait_ge(ct.sem, n_xqtp)
                    if gi >= 2:
                        v.wait_ge(ct.sem, gu_last[gi - 2])
                    cv.inc(v.tensor_copy(XqT[r][:], tps[0][:]))
                blk.vector(_)
                xqt_evac[gi] = cv.n
                nxqt_all[gi] = cv.n

                gu_rec_all[gi] = {}
                for ci in (0, 1):
                    emit_gu_chunk(gi, r, ci)

            def stage_b1a(gi, r):
                qd_h = qd_h_hist[gi]
                hqt_evacs = []
                for idx, g0 in enumerate(range(0, NT, 8)):
                    ng = min(8, NT - g0)
                    tp = tps[1 - idx % 2]

                    def _(te, g0=g0, ng=ng, qd_h=qd_h, gi=gi, idx=idx,
                          tp=tp, hqt_evacs=tuple(hqt_evacs)):
                        te.wait_ge(cv.sem, qd_h)
                        if idx == 0:
                            if gi >= 1:
                                te.wait_ge(cv.sem, hqt_last[gi - 1])
                        elif idx == 1:
                            if gi + 1 < n_gi:
                                te.wait_ge(cv.sem, xqt_evac[gi + 1])
                        else:
                            te.wait_ge(cv.sem, hqt_evacs[idx - 2])
                        last = None
                        for j in range(ng):
                            last = te.transpose(
                                tp[:, j * 128:(j + 1) * 128],
                                Hq[:, (g0 + j) * 128:(g0 + j + 1) * 128],
                                idb[:])
                        ct.inc(last)
                    blk.tensor(_)
                    n_tp = ct.n

                    def _(v, g0=g0, ng=ng, r=r, n_tp=n_tp, gi=gi, tp=tp):
                        v.wait_ge(ct.sem, n_tp)
                        if g0 == 0 and gi >= 2:
                            v.wait_ge(ct.sem, down_done[gi - 2])
                        cv.inc(v.tensor_copy(
                            HqT[r][:, g0 * 128:(g0 + ng) * 128],
                            tp[:, 0:ng * 128]))
                    blk.vector(_)
                    hqt_evacs.append(cv.n)
                hq_tp_done[gi] = ct.n
                hqt_last[gi] = cv.n
                hqt_g1_evac[gi] = hqt_evacs[1]

            def stage_b1b(gi, r):
                n_hqt = hqt_last[gi]

                def _(te, r=r, n_hqt=n_hqt, gi=gi):
                    te.wait_ge(cv.sem, n_hqt)
                    if gi >= 1:
                        te.wait_ge(cv.sem, x2_used[gi - 1])
                    last = None
                    for t in range(NT):
                        for c in range(2):
                            last = te.matmul(
                                dps[:, c * 512:(c + 1) * 512],
                                HqT[r][:, t * 128:(t + 1) * 128],
                                WDT[:, t * D + c * 512:
                                    t * D + (c + 1) * 512],
                                start=(t == 0), stop=(t == NT - 1))
                    ct.inc(last)
                blk.tensor(_)
                down_done[gi] = ct.n

            def emit_x2_load(gi):
                b, sb = gi // SB, gi % SB
                r = gi % 2

                def _(g, b=b, sb=sb, r=r, gi=gi):
                    if gi >= 2:
                        g.wait_ge(cfo[r].sem, out_store[gi - 2])
                    slot, val = x2_store_all[gi]
                    g.wait_ge(cso[slot].sem, val)
                    g.dma_start(
                        out=x2b[r][:],
                        in_=d_x2[b, sb * 128:(sb + 1) * 128, :]
                    ).then_inc(cfl[r].sem, 16)
                blk.gpsimd(_)
                cfl[r].n += 16
                ldx2_all[gi] = cfl[r].n

            def stage_b2(gi, r):
                b, sb = gi // SB, gi % SB
                n_down = down_done[gi]
                n_ldx2 = ldx2_all[gi]

                def _(v, r=r, n_down=n_down, n_ldx2=n_ldx2):
                    v.wait_ge(ct.sem, n_down)
                    v.wait_ge(cfl[r].sem, n_ldx2)
                    cv.inc(v.tensor_tensor(x2b[r][:], x2b[r][:], dps[:],
                                           op=ALU.add))
                blk.vector(_)
                x2_used[gi] = cv.n
                n_out = cv.n

                def _(g, b=b, sb=sb, r=r, n_out=n_out):
                    g.wait_ge(cv.sem, n_out)
                    g.dma_start(
                        out=d_out[b, sb * 128:(sb + 1) * 128, :],
                        in_=x2b[r][:]).then_inc(cfo[r].sem, 16)
                blk.gpsimd(_)
                cfo[r].n += 16
                out_store[gi] = cfo[r].n

                if gi + 2 < n_gi:
                    emit_x2_load(gi + 2)

            def stage_c1(b, sb, gi, r):
                gu_rec = gu_rec_all[gi]
                cv_pre_tt = cv.n
                for ci in range(NCH):
                    h_done[gi * NCH + ci] = cv_pre_tt + ci + 1
                for ci in range(2, NCH):
                    emit_gu_chunk(gi, r, ci)
                gu_last[gi] = ct.n

                cs_pre_silu = cs.n

                def _(s, gi=gi, gu_rec=gu_rec, cv_pre_tt=cv_pre_tt):
                    load_tbl(s, SET_SILU)
                    if gi >= 1:
                        s.wait_ge(cv.sem, hbuf_free[gi - 1])
                    for ci in range(NCH):
                        nc0, nc1 = N_CHUNKS[ci]
                        cw = nc1 - nc0
                        pr = ci % 2
                        s.wait_ge(ct.sem, gu_rec[ci])
                        if ci >= 2:
                            s.wait_ge(cv.sem, cv_pre_tt + ci - 1)
                        cs.inc(s.activation(hbuf[:, nc0:nc0 + cw],
                                            gps[pr][:, 0:cw], AF.Silu))
                blk.scalar(_)

                for ci, (nc0, nc1) in enumerate(N_CHUNKS):
                    cw = nc1 - nc0
                    pr = ci % 2

                    def _(v, nc0=nc0, cw=cw, pr=pr, ci=ci,
                          cs_pre_silu=cs_pre_silu):
                        v.wait_ge(cs.sem, cs_pre_silu + ci + 1)
                        cv.inc(v.tensor_tensor(hbuf[:, nc0:nc0 + cw],
                                               hbuf[:, nc0:nc0 + cw],
                                               ups[pr][:, 0:cw],
                                               op=ALU.mult))
                    blk.vector(_)
                assert cv.n == cv_pre_tt + NCH
                nhall_all[gi] = (cv_pre_tt, cv.n)

            def stage_c2(b, sb, gi, r):
                cv_pre_tt, n_h_all = nhall_all[gi]

                def _(s, cv_pre_tt=cv_pre_tt):
                    load_tbl(s, SET_SILU)
                    if last_qd[0] > 0:
                        s.wait_ge(cv.sem, last_qd[0])
                    for ci in range(NCH):
                        nc0, nc1 = N_CHUNKS[ci]
                        cw = nc1 - nc0
                        s.wait_ge(cv.sem, cv_pre_tt + ci + 1)
                        s.activation(q_sg[:, nc0:nc0 + cw],
                                     hbuf[:, nc0:nc0 + cw], AF.Sign)
                        cs.inc(s.activation(q_tv[:, nc0:nc0 + cw],
                                            hbuf[:, nc0:nc0 + cw],
                                            AF.Square))
                blk.scalar(_)
                n_sgn_h = cs.n

                marks = {}
                qd_h = emit_quant(
                    hbuf[:], Hq[:], DI,
                    q_tv, q_ta, q_sg, q_bm, q_sc, q_rs,
                    q_r2, q_slb,
                    wait_v=[(cv, n_h_all),
                            (ct, hq_tp_done.get(gi - 1, 0))],
                    ext_prep=n_sgn_h, marks=marks)
                qd_h_hist[gi] = qd_h
                hbuf_free[gi] = marks["bmax"]

            n_gi = n_batch * SB
            emit_x2_load(0)
            emit_x2_load(1)
            stage_a1(0, 0, 0, 0)
            stage_a2(0, 0)
            for b in range(n_batch):
                for sb in range(SB):
                    gi = b * SB + sb
                    r = sb % 2
                    if gi + 1 < n_gi:
                        stage_a1((gi + 1) // SB, (gi + 1) % SB, gi + 1,
                                 (gi + 1) % 2)
                    stage_c1(b, sb, gi, r)
                    if gi >= 1:
                        stage_b1b(gi - 1, (gi - 1) % 2)
                    if gi + 1 < n_gi:
                        stage_a2(gi + 1, (gi + 1) % 2)
                    stage_c2(b, sb, gi, r)
                    stage_b1a(gi, r)
                    if gi >= 1:
                        stage_b2(gi - 1, (gi - 1) % 2)
            stage_b1b(n_gi - 1, (n_gi - 1) % 2)
            stage_b2(n_gi - 1, (n_gi - 1) % 2)

            if debug:
                def _(g):
                    g.wait_ge(cwl.sem, n_ffnw)
                    g.dma_start(out=d_dbg["wgt"], in_=WGT[:]
                                ).then_inc(cst.sem, 16)
                    g.dma_start(out=d_dbg["wut"], in_=WUT[:]
                                ).then_inc(cst.sem, 16)
                    g.dma_start(out=d_dbg["wdt"], in_=WDT[:]
                                ).then_inc(cst.sem, 16)
                    g.wait_ge(cv.sem, cv.n)
                    g.wait_ge(cs.sem, cs.n)
                    last_r = (SB - 1) % 2
                    g.dma_start(out=d_dbg["xq"], in_=xqb[(SB - 1) % 2][:]
                                ).then_inc(cst.sem, 16)
                    g.dma_start(out=d_dbg["hq"], in_=Hq[:]
                                ).then_inc(cst.sem, 16)
                    g.dma_start(out=d_dbg["hbuf"], in_=hbuf[:]
                                ).then_inc(cst.sem, 16)
                    g.dma_start(out=d_dbg["xln2"], in_=x2b[last_r][:]
                                ).then_inc(cst.sem, 16)
                    g.dma_start(out=d_dbg["xqt"], in_=XqT[last_r][:]
                                ).then_inc(cst.sem, 16)
                    g.dma_start(out=d_dbg["hqt"], in_=HqT[last_r][:]
                                ).then_inc(cst.sem, 16)
                    g.dma_start(out=d_dbg["x2"],
                                in_=d_x2.rearrange("b s d -> (b s) d")
                                ).then_inc(cst.sem, 16)
                blk.gpsimd(_)
                cst.n += 160

            def _(g):
                g.wait_ge(cst.sem, cst.n)
                g.wait_ge(cso[0].sem, cso[0].n)
                g.wait_ge(cso[1].sem, cso[1].n)
                g.wait_ge(cfo[0].sem, cfo[0].n)
                g.wait_ge(cfo[1].sem, cfo[1].n)
                g.wait_ge(cqo[0].sem, cqo[0].n)
                g.wait_ge(cqo[1].sem, cqo[1].n)
            blk.gpsimd(_)

    return nc


def kernel(L0_attn_hidden_states, L0_attn_Wq, L0_attn_Wkv, L0_attn_Wo,
           L0_attn_bo, L0_ffn_w_gate, L0_ffn_w_up, L0_ffn_w_down):
    x = np.ascontiguousarray(L0_attn_hidden_states, dtype=np.float32)
    wq = np.ascontiguousarray(L0_attn_Wq, dtype=np.float32)
    wkv = np.ascontiguousarray(L0_attn_Wkv, dtype=np.float32)
    wo = np.ascontiguousarray(L0_attn_Wo, dtype=np.float32)
    bo = np.ascontiguousarray(L0_attn_bo, dtype=np.float32)
    wg = np.ascontiguousarray(L0_ffn_w_gate, dtype=np.float32)
    wu = np.ascontiguousarray(L0_ffn_w_up, dtype=np.float32)
    wd = np.ascontiguousarray(L0_ffn_w_down, dtype=np.float32)

    bo_t = np.ascontiguousarray(np.broadcast_to(bo[None, :], (128, D)))
    ident = np.eye(128, dtype=np.float32)

    nc = build()
    in_maps = []
    for c in range(N_CORES):
        in_maps.append({
            "x": np.ascontiguousarray(x[c * N_BATCH:(c + 1) * N_BATCH]),
            "wq": wq, "wkv": wkv, "wo": wo, "bo_t": bo_t,
            "wg": wg, "wu": wu, "wd": wd, "ident": ident,
        })
    res = run_bass_kernel_spmd(nc, in_maps, core_ids=list(range(N_CORES)))
    return np.concatenate([r["out"] for r in res.results], axis=0)


# version 14
# speedup vs baseline: 1.1482x; 1.0125x over previous
import contextlib
import numpy as np

import concourse.bass as bass
import concourse.mybir as mybir
from concourse.bass_utils import run_bass_kernel_spmd
from concourse.hw_specs import get_activation_tables

F32 = mybir.dt.float32
BF16 = mybir.dt.bfloat16
AF = mybir.ActivationFunctionType
ALU = mybir.AluOpType
AX = mybir.AxisListType

N_CORES = 8
N_BATCH = 4
S = 2048
D = 1024
NH = 16
DH = 64
DI = 2816
SB = S // 128
KT = D // 128
NT = DI // 128
NPAIR = NH // 2
EPS = 1e-5
N_CHUNKS = [(c, min(c + 512, DI)) for c in range(0, DI, 512)]


class Ctr:
    def __init__(self, nc, es, name):
        self.sem = es.enter_context(nc.semaphore(name))
        self.n = 0

    def inc(self, instr, k=1):
        instr.then_inc(self.sem, k)
        self.n += k
        return self.n


def build(n_batch=N_BATCH, debug=False):
    nc = bass.Bass("TRN2", target_bir_lowering=False,
                   detect_race_conditions=False)
    tbl_ids = {name: i for i, name in
               enumerate(get_activation_tables(nc.m.arch))}
    SET_LNEXP = tbl_ids["natural_log_exp_and_others"]
    SET_SILU = tbl_ids["silu_and_others"]

    def load_tbl(eng, set_id):
        eng.add_instruction(mybir.InstLoadActFuncSet(
            name=nc.get_next_instruction_name(), act_func_set_id=set_id,
            ins=[], outs=[]))

    d_x = nc.dram_tensor("x", [n_batch, S, D], F32, kind="ExternalInput").ap()
    d_wq = nc.dram_tensor("wq", [D, D], F32, kind="ExternalInput").ap()
    d_wkv = nc.dram_tensor("wkv", [D, 2 * D], F32, kind="ExternalInput").ap()
    d_wo = nc.dram_tensor("wo", [D, D], F32, kind="ExternalInput").ap()
    d_bo = nc.dram_tensor("bo_t", [128, D], F32, kind="ExternalInput").ap()
    d_wg = nc.dram_tensor("wg", [DI, D], F32, kind="ExternalInput").ap()
    d_wu = nc.dram_tensor("wu", [DI, D], F32, kind="ExternalInput").ap()
    d_wd = nc.dram_tensor("wd", [D, DI], F32, kind="ExternalInput").ap()
    d_id = nc.dram_tensor("ident", [128, 128], F32, kind="ExternalInput").ap()
    d_out = nc.dram_tensor("out", [n_batch, S, D], F32,
                           kind="ExternalOutput").ap()

    if debug:
        d_dbg = {}
        for nm, shape in (
            ("lnxt", [128, KT * 2048]), ("qt", [128, KT * 2048]),
            ("ktb", [128, KT * 2048]), ("v", [128, SB * 1024]),
            ("ot", [128, KT * 2048]), ("xq", [128, D]), ("hq", [128, DI]),
            ("hbuf", [128, DI]), ("xln2", [128, D]), ("xqt", [128, D]),
            ("aqbm", [128, D // 16]), ("axln", [128, D]), ("asq", [128, D]),
            ("ata", [128, D]), ("atsg", [128, D]),
            ("hqt", [128, DI]), ("wgt", [128, KT * DI]),
            ("wut", [128, KT * DI]), ("wdt", [128, NT * D]),
            ("x2", [n_batch * S, D]),
        ):
            d_dbg[nm] = nc.dram_tensor("dbg_" + nm, shape, F32,
                                       kind="ExternalOutput").ap()
    d_x2 = nc.dram_tensor("x2_s", [n_batch, S, D], F32, kind="Internal").ap()
    d_xq = nc.dram_tensor("xq_s", [n_batch, S, D], BF16, kind="Internal").ap()
    d_wgt = nc.dram_tensor("wgt_s", [128, KT * DI], BF16, kind="Internal").ap()
    d_wut = nc.dram_tensor("wut_s", [128, KT * DI], BF16, kind="Internal").ap()
    d_wdt = nc.dram_tensor("wdt_s", [128, NT * D], BF16, kind="Internal").ap()

    es = contextlib.ExitStack()
    with es:
        def sbuf(name, shape, dt):
            return es.enter_context(nc.sbuf_tensor(name, shape, dt))

        cin = Ctr(nc, es, "cin")
        cxl = [Ctr(nc, es, f"cxl{i}") for i in range(2)]
        c3l = [Ctr(nc, es, f"c3l{i}") for i in range(2)]
        cfl = [Ctr(nc, es, f"cfl{i}") for i in range(2)]
        cpl = [Ctr(nc, es, f"cpl{i}") for i in range(2)]
        cso = [Ctr(nc, es, f"cso{i}") for i in range(2)]
        cqo = [Ctr(nc, es, f"cqo{i}") for i in range(2)]
        cql = [Ctr(nc, es, f"cql{i}") for i in range(2)]
        cfo = [Ctr(nc, es, f"cfo{i}") for i in range(2)]
        cwl = Ctr(nc, es, "cwl")
        cst = Ctr(nc, es, "cst")
        ct = Ctr(nc, es, "ct")
        cv = Ctr(nc, es, "cv")
        cs = Ctr(nc, es, "cs")

        blk = es.enter_context(nc.Block())

        bo_bc = sbuf("bo_bc", [128, D], F32)
        idb = sbuf("idb", [128, 128], BF16)
        ones64 = sbuf("ones64", [128, 64], BF16)
        cbias = sbuf("cbias", [128, 2], F32)

        def _(g):
            g.dma_start(out=idb[:], in_=d_id).then_inc(cwl.sem, 16)
            g.dma_start(out=bo_bc[:], in_=d_bo).then_inc(cwl.sem, 16)
        blk.gpsimd(_)
        cwl.n += 32
        n_ld0 = cwl.n

        def _(v):
            v.wait_ge(cwl.sem, n_ld0)
            v.memset(ones64[:], 1.0)
            v.memset(cbias[:, 0:1], EPS)
            cv.inc(v.memset(cbias[:, 1:2], -3.0))
        blk.vector(_)
        n_consts = cv.n

        last_qd = [0]

        def emit_quant(src_ap, dst_ap, F, sq, ta, tsg, bmax, scl, rscl,
                       rscl2, sclb, wait_v=(), wait_s=(), ext_prep=None,
                       marks=None):
            G = F // 16

            if ext_prep is None:
                def _(s):
                    for ctr, n in wait_s:
                        if n > 0:
                            s.wait_ge(ctr.sem, n)
                    if last_qd[0] > 0:
                        s.wait_ge(cv.sem, last_qd[0])
                    load_tbl(s, SET_LNEXP)
                    s.activation(tsg[:, 0:F], src_ap, AF.Sign)
                    cs.inc(s.activation(sq[:, 0:F], src_ap, AF.Square))
                blk.scalar(_)
                n_prep = cs.n
            else:
                n_prep = ext_prep

            def _(v):
                for ctr, n in wait_v:
                    if n > 0:
                        v.wait_ge(ctr.sem, n)
                src3 = src_ap.rearrange("p (g w) -> p g w", w=16)
                n_red = cv.inc(v.tensor_reduce(
                    bmax[:, 0:G].rearrange("p (g o) -> p g o", o=1), src3,
                    axis=AX.X, op=ALU.max, apply_absolute_value=True))
                if marks is not None:
                    marks["bmax"] = n_red
                v.wait_ge(cv.sem, n_red)
                v.tensor_scalar(scl[:, 0:G], bmax[:, 0:G], 1.0 / 6.0, 1e-12,
                                op0=ALU.mult, op1=ALU.max)
                n_rc = cv.inc(v.reciprocal(rscl[:, 0:G], scl[:, 0:G]))
                v.wait_ge(cv.sem, n_rc)
                v.tensor_tensor(rscl2[:, 0:G], rscl[:, 0:G], rscl[:, 0:G],
                                op=ALU.mult)
                v.tensor_copy(sclb[:, 0:G], scl[:, 0:G])
                v.wait_ge(cs.sem, n_prep)
                r2b = rscl2[:, 0:G].rearrange("p (g o) -> p g o", o=1) \
                    .broadcast_to([128, G, 16])
                sqf = sq[:, 0:F]
                v.tensor_tensor(sqf.rearrange("p (g w) -> p g w", w=16),
                                sqf.rearrange("p (g w) -> p g w", w=16),
                                r2b, op=ALU.mult)
                v.tensor_copy(dst_ap, sqf)
                v.tensor_scalar(ta[:, 0:F], dst_ap, 0.0625, None,
                                op0=ALU.is_gt)
                for t in (0.5625, 1.5625, 3.0625):
                    v.scalar_tensor_tensor(ta[:, 0:F], dst_ap, t, ta[:, 0:F],
                                           op0=ALU.is_gt, op1=ALU.add)
                v.tensor_scalar(ta[:, 0:F], ta[:, 0:F], 0.5, None,
                                op0=ALU.mult)
                for t in (6.25, 12.25):
                    v.scalar_tensor_tensor(ta[:, 0:F], sqf, t, ta[:, 0:F],
                                           op0=ALU.is_gt, op1=ALU.add)
                v.tensor_scalar(dst_ap, sqf, 25.0, 2.0,
                                op0=ALU.is_gt, op1=ALU.mult)
                v.tensor_tensor(ta[:, 0:F], ta[:, 0:F], dst_ap, op=ALU.add)
                sb3 = sclb[:, 0:G].rearrange("p (g o) -> p g o", o=1) \
                    .broadcast_to([128, G, 16])
                v.tensor_tensor(tsg[:, 0:F].rearrange("p (g w) -> p g w", w=16),
                                tsg[:, 0:F].rearrange("p (g w) -> p g w", w=16),
                                sb3, op=ALU.mult)
                cv.inc(v.tensor_tensor(dst_ap, ta[:, 0:F], tsg[:, 0:F],
                                       op=ALU.mult))
            blk.vector(_)
            last_qd[0] = cv.n
            return cv.n

        def emit_ln(src_ap, scratch_ap, dst_ap, stat_t, c0, wait_load,
                    col_guard=0, sq_waits=(), norm_waits=(), sign_dst=None):
            xs, mn, nmn, ssr, nm2, nm2e, var_, lnv, rs, nb = (
                stat_t[:, c0 + i:c0 + i + 1] for i in range(10))

            def _(v):
                for ctr, n in wait_load:
                    v.wait_ge(ctr.sem, n)
                if col_guard > 0:
                    v.wait_ge(cs.sem, col_guard)
                n_red = cv.inc(v.reduce_sum(xs, src_ap, axis=AX.X))
                v.wait_ge(cv.sem, n_red)
                v.tensor_scalar(mn, xs, 1.0 / D, None, op0=ALU.mult)
                cv.inc(v.tensor_scalar(nmn, xs, -1.0 / D, None, op0=ALU.mult))
            blk.vector(_)
            n1 = cv.n

            def _(s):
                for ctr, n in wait_load:
                    s.wait_ge(ctr.sem, n)
                for ctr, n in sq_waits:
                    s.wait_ge(ctr.sem, n)
                if sign_dst is not None and last_qd[0] > 0:
                    s.wait_ge(cv.sem, last_qd[0])
                load_tbl(s, SET_LNEXP)
                s.activation(scratch_ap, src_ap, AF.Square, accum_out=ssr)
                s.wait_ge(cv.sem, n1)
                s.activation(nm2, nmn, AF.Identity, scale=mn, bias=0.0)
                n_a = cs.inc(s.activation(nm2e, nm2, AF.Identity,
                                          bias=cbias[:, 0:1]))
                s.wait_ge(cs.sem, n_a)
                s.activation(var_, ssr, AF.Identity, scale=1.0 / D,
                             bias=nm2e)
                s.activation(lnv, var_, AF.Ln)
                n_b = cs.inc(s.activation(rs, lnv, AF.Exp, scale=-0.5))
                s.wait_ge(cs.sem, n_b)
                n_c = cs.inc(s.activation(nb, nmn, AF.Identity, scale=rs,
                                          bias=0.0))
                s.wait_ge(cs.sem, n_c)
                for ctr, n in norm_waits:
                    s.wait_ge(ctr.sem, n)
                cs.inc(s.activation(dst_ap, src_ap, AF.Identity,
                                    scale=rs, bias=nb))
                if sign_dst is not None:
                    sgn_ap, sq_ap = sign_dst
                    s.activation(sgn_ap, dst_ap, AF.Sign)
                    cs.inc(s.activation(sq_ap, dst_ap, AF.Square))
            blk.scalar(_)
            return cs.n

        p0 = contextlib.ExitStack()
        with p0:
            def p0_sbuf(name, shape, dt):
                return p0.enter_context(nc.sbuf_tensor(name, shape, dt))

            p0_wtile = [p0_sbuf(f"p0w{i}", [128, DI], F32) for i in range(2)]
            p0_q = [p0_sbuf(f"p0q{i}", [128, DI], BF16) for i in range(2)]
            p0_tv = p0_sbuf("p0tv", [128, DI], F32)
            p0_ta = p0_sbuf("p0ta", [128, DI], BF16)
            p0_sg = p0_sbuf("p0sg", [128, DI], BF16)
            p0_bm = p0_sbuf("p0bm", [128, DI // 16], F32)
            p0_r2 = p0_sbuf("p0r2", [128, DI // 16], F32)
            p0_sb = p0_sbuf("p0sb", [128, DI // 16], BF16)
            p0_sc = p0_sbuf("p0sc", [128, DI // 16], F32)
            p0_rs = p0_sbuf("p0rs", [128, DI // 16], F32)
            p0_stage = p0_sbuf("p0_stage", [128, KT * DI], BF16)
            p0_tp = p0.enter_context(
                nc.psum_tensor("p0_tp", [128, 1024], BF16))

            def prep_weight(d_src, d_dst, n_tiles, tile_f, tp_rows):
                nblk = tile_f // 128
                quant_done = {}
                evac_done = [0]
                tp_done = [0, 0]
                entry_st = cst.n
                for t in range(n_tiles):
                    r = t % 2

                    def _(g, t=t, r=r):
                        if t >= 2:
                            g.wait_ge(cv.sem, quant_done[t - 2])
                        g.dma_start(out=p0_wtile[r][:, 0:tile_f],
                                    in_=d_src[t * 128:(t + 1) * 128, 0:tile_f]
                                    ).then_inc(cpl[r].sem, 16)
                    blk.gpsimd(_)
                    cpl[r].n += 16
                    n_ld = cpl[r].n
                    qd = emit_quant(
                        p0_wtile[r][:, 0:tile_f], p0_q[r][:, 0:tile_f],
                        tile_f,
                        p0_tv, p0_ta, p0_sg, p0_bm, p0_sc, p0_rs,
                        p0_r2, p0_sb,
                        wait_v=[(cpl[r], n_ld), (ct, tp_done[1])],
                        wait_s=[(cpl[r], n_ld)])
                    quant_done[t] = qd
                    for g0 in range(0, nblk, 8):
                        ng = min(8, nblk - g0)

                        def _(te, g0=g0, ng=ng, qd=qd, t=t):
                            te.wait_ge(cv.sem, max(qd, n_consts))
                            if evac_done[0] > 0:
                                te.wait_ge(cv.sem, evac_done[0])
                            last = None
                            for j in range(ng):
                                last = te.transpose(
                                    p0_tp[:, j * 128:(j + 1) * 128],
                                    p0_q[t % 2][:, (g0 + j) * 128:
                                                 (g0 + j + 1) * 128],
                                    idb[:])
                            ct.inc(last)
                        blk.tensor(_)
                        n_tp = ct.n

                        def _(v, g0=g0, ng=ng, t=t, n_tp=n_tp):
                            v.wait_ge(ct.sem, n_tp)
                            if t == 0 and g0 == 0 and entry_st > 0:
                                v.wait_ge(cst.sem, entry_st)
                            src = p0_tp[:, 0:ng * 128].rearrange(
                                "p (j c) -> p j c", c=128)
                            dst = p0_stage[:, 0:nblk * tp_rows].rearrange(
                                "p (j n) -> p j n", n=tp_rows)[
                                :, g0:g0 + ng, t * 128:(t + 1) * 128]
                            cv.inc(v.tensor_copy(dst, src))
                        blk.vector(_)
                        evac_done[0] = cv.n
                    tp_done[1] = tp_done[0]
                    tp_done[0] = ct.n

                def _(g):
                    g.wait_ge(cv.sem, evac_done[0])
                    g.dma_start(out=d_dst, in_=p0_stage[:, 0:nblk * tp_rows]
                                ).then_inc(cst.sem, 16)
                blk.gpsimd(_)
                cst.n += 16
                return cst.n

            prep_weight(d_wg, d_wgt, NT, D, DI)
            prep_weight(d_wu, d_wut, NT, D, DI)
            st_prep = prep_weight(d_wd, d_wdt, KT, DI, D)

        pa = contextlib.ExitStack()
        with pa:
            def pas(name, shape, dt):
                return pa.enter_context(nc.sbuf_tensor(name, shape, dt))

            WW = pas("WW", [128, KT * 2048], BF16)
            WoB = pas("WoB", [128, KT * 1024], BF16)
            LNXT = pas("LNXT", [128, KT * 2048], BF16)
            QT = pas("QT", [128, KT * 2048], BF16)
            KTb = pas("KTb", [128, KT * 2048], BF16)
            V = pas("V", [128, SB * 1024], BF16)
            EK = [pas(f"EK{i}", [128, 1024], BF16) for i in range(2)]
            RC = pas("RC", [128, 512], F32)
            xbuf = [pas(f"xbuf{i}", [128, D], F32) for i in range(2)]
            aq_bm = pas("aq_bm", [128, D // 16], F32)
            aq_sc = pas("aq_sc", [128, D // 16], F32)
            aq_rs = pas("aq_rs", [128, D // 16], F32)
            aq_r2 = pas("aq_r2", [128, D // 16], F32)
            aq_sb = pas("aq_sb", [128, D // 16], BF16)
            x3buf = [pas(f"x3buf{i}", [128, D], F32) for i in range(2)]
            lnx = [pas(f"lnx{i}", [128, D], BF16) for i in range(2)]
            stat = pas("stat", [128, 40], F32)

            def _(g):
                g.wait_ge(cst.sem, st_prep)
                for k in range(KT):
                    g.dma_start(out=WoB[:, k * 1024:(k + 1) * 1024],
                                in_=d_wo[k * 128:(k + 1) * 128, :]
                                ).then_inc(cwl.sem, 16)
            blk.gpsimd(_)
            cwl.n += 16 * 8

            dbg_gate = [0]
            prev_a2_cs = [0]
            prev_lnx_guard = [0]
            prev_ln2_end = [0]
            prev_xq_store = [(0, 0)]
            x2_store_all = {}
            xq_store_all = {}
            prev_t_end = 0
            prev_v_end = 0
            prev_qkv_t = 0
            prev_store_last = (0, 0)

            for b in range(n_batch):
                bps = contextlib.ExitStack()
                with bps:
                    pq = [bps.enter_context(nc.psum_tensor(
                        f"pq{b}_{i}", [128, 1024], F32)) for i in range(3)]
                    tp2 = [bps.enter_context(nc.psum_tensor(
                        f"tp{b}_{i}", [128, 1024], BF16)) for i in range(2)]

                    def _(g, b=b):
                        if b > 0:
                            g.wait_ge(ct.sem, prev_qkv_t)
                        for k in range(KT):
                            g.dma_start(out=WW[:, k * 2048:k * 2048 + 1024],
                                        in_=d_wq[k * 128:(k + 1) * 128, :]
                                        ).then_inc(cwl.sem, 16)
                    blk.gpsimd(_)
                    cwl.n += 16 * KT
                    n_wq = cwl.n

                    norm_done = {}
                    tpos_done = {}
                    lnxt_evac = {}
                    for sb in range(SB):
                        r = sb % 2

                        def _(sy, b=b, sb=sb, r=r):
                            if sb >= 2:
                                sy.wait_ge(cs.sem, norm_done[sb - 2])
                            if b == 0 and sb == 0:
                                sy.wait_ge(cst.sem, st_prep)
                            if b > 0 and sb < 2:
                                sy.wait_ge(cs.sem, prev_ln2_end[0])
                                sy.wait_ge(cv.sem, prev_lnx_guard[0])
                            sy.dma_start(
                                out=xbuf[r][:],
                                in_=d_x[b, sb * 128:(sb + 1) * 128, :]
                            ).then_inc(cxl[r].sem, 16)
                        blk.sync(_)
                        cxl[r].n += 16
                        n_ldx = cxl[r].n

                        sq_w = []
                        if b > 0 and sb < 2:
                            sq_w.append((cso[0], prev_store_last[0]))
                            sq_w.append((cso[1], prev_store_last[1]))
                        nrm_w = []
                        if sb >= 2:
                            nrm_w.append((ct, tpos_done[sb - 2]))
                        if b > 0 and sb < 2:
                            nrm_w.append((cv, prev_lnx_guard[0]))
                        norm_done[sb] = emit_ln(
                            xbuf[r][:], x3buf[r][:], lnx[r][:], stat, r * 10,
                            wait_load=[(cxl[r], n_ldx)],
                            col_guard=(norm_done[sb - 2] if sb >= 2 else 0),
                            sq_waits=sq_w, norm_waits=nrm_w)

                        def _(te, r=r, sb=sb, b=b):
                            te.wait_ge(cs.sem, norm_done[sb])
                            te.wait_ge(cv.sem, n_consts)
                            if sb >= 2:
                                te.wait_ge(cv.sem, lnxt_evac[sb - 2])
                            if b > 0 and sb == 0:
                                te.wait_ge(cv.sem, prev_v_end)
                            last = None
                            for j in range(KT):
                                last = te.transpose(
                                    tp2[sb % 2][:, j * 128:(j + 1) * 128],
                                    lnx[r][:, j * 128:(j + 1) * 128], idb[:])
                            ct.inc(last)
                        blk.tensor(_)
                        tpos_done[sb] = ct.n

                        def _(v, sb=sb, b=b):
                            v.wait_ge(ct.sem, tpos_done[sb])
                            src = tp2[sb % 2][:].rearrange(
                                "p (j c) -> p j c", c=128)
                            dst = LNXT[:].rearrange(
                                "p (j q) -> p j q", q=2048)[
                                :, :, sb * 128:(sb + 1) * 128]
                            cv.inc(v.tensor_copy(dst, src))
                        blk.vector(_)
                        lnxt_evac[sb] = cv.n
                    n_lnxt_all = cv.n

                    pq_evac = []
                    uidx = [0]

                    def qkv_unit(stat_fn, rhs_fn, out_ap):
                        u = uidx[0] % 3
                        myu = uidx[0]
                        my_attw = n_attw

                        def _(te):
                            te.wait_ge(cv.sem, n_lnxt_all)
                            te.wait_ge(cwl.sem, my_attw)
                            if myu >= 3:
                                te.wait_ge(cv.sem, pq_evac[myu - 3])
                            last = None
                            for k in range(KT):
                                lhsT = stat_fn(k)
                                for c in range(2):
                                    last = te.matmul(
                                        pq[u][:, c * 512:(c + 1) * 512],
                                        lhsT, rhs_fn(k, c),
                                        start=(k == 0), stop=(k == KT - 1))
                            ct.inc(last)
                        blk.tensor(_)
                        n_mm = ct.n

                        def _(v, u=u, out_ap=out_ap, n_mm=n_mm):
                            v.wait_ge(ct.sem, n_mm)
                            cv.inc(v.tensor_copy(out_ap, pq[u][:]))
                        blk.vector(_)
                        pq_evac.append(cv.n)
                        uidx[0] += 1

                    n_attw = n_wq
                    for j in range(KT):
                        for half in range(2):
                            qkv_unit(
                                (lambda k, j=j:
                                 WW[:, k * 2048 + j * 128:
                                    k * 2048 + (j + 1) * 128]),
                                (lambda k, c, half=half:
                                 LNXT[:, k * 2048 + half * 1024 + c * 512:
                                      k * 2048 + half * 1024 + (c + 1) * 512]),
                                QT[:, j * 2048 + half * 1024:
                                   j * 2048 + (half + 1) * 1024])
                    n_q_mm = ct.n

                    def _(g, n_q_mm=n_q_mm):
                        g.wait_ge(ct.sem, n_q_mm)
                        for k in range(KT):
                            g.dma_start(out=WW[:, k * 2048:(k + 1) * 2048],
                                        in_=d_wkv[k * 128:(k + 1) * 128, :]
                                        ).then_inc(cwl.sem, 16)
                    blk.gpsimd(_)
                    cwl.n += 16 * KT
                    n_attw = cwl.n

                    for j in range(KT):
                        for half in range(2):
                            qkv_unit(
                                (lambda k, j=j:
                                 WW[:, k * 2048 + j * 128:
                                    k * 2048 + (j + 1) * 128]),
                                (lambda k, c, half=half:
                                 LNXT[:, k * 2048 + half * 1024 + c * 512:
                                      k * 2048 + half * 1024 + (c + 1) * 512]),
                                KTb[:, j * 2048 + half * 1024:
                                    j * 2048 + (half + 1) * 1024])
                    for t in range(SB):
                        qkv_unit(
                            (lambda k, t=t:
                             LNXT[:, k * 2048 + t * 128:
                                  k * 2048 + (t + 1) * 128]),
                            (lambda k, c:
                             WW[:, k * 2048 + 1024 + c * 512:
                                k * 2048 + 1024 + (c + 1) * 512]),
                            V[:, t * 1024:(t + 1) * 1024])
                    n_qkv_t = ct.n
                    n_qkv_v = cv.n

                    if debug and b == 0:
                        def _(g):
                            g.wait_ge(cv.sem, n_qkv_v)
                            g.dma_start(out=d_dbg["lnxt"], in_=LNXT[:]
                                        ).then_inc(cst.sem, 16)
                            g.dma_start(out=d_dbg["qt"], in_=QT[:]
                                        ).then_inc(cst.sem, 16)
                            g.dma_start(out=d_dbg["ktb"], in_=KTb[:]
                                        ).then_inc(cst.sem, 16)
                            g.dma_start(out=d_dbg["v"], in_=V[:]
                                        ).then_inc(cst.sem, 16)
                        blk.gpsimd(_)
                        cst.n += 64
                        dbg_gate[0] = cst.n

                ba2 = contextlib.ExitStack()
                with ba2:
                    sps = [ba2.enter_context(nc.psum_tensor(
                        f"sps{b}_{i}", [128, 1024], F32)) for i in range(3)]
                    pvo = ba2.enter_context(
                        nc.psum_tensor(f"pvo{b}", [128, 512], F32))
                    pvd = ba2.enter_context(
                        nc.psum_tensor(f"pvd{b}", [128, 512], F32))

                    s_done = {}
                    exp_done = {}
                    pv_done = {}
                    ot_write = [0]
                    for p in range(NPAIR):
                        key0 = p * 64
                        cs_pre = cs.n
                        for i in range(64):
                            exp_done[key0 + i] = cs_pre + i + 1
                        for qc in range(4):
                            def _(te, p=p, qc=qc):
                                def emit_s(kb):
                                    key = p * 64 + qc * SB + kb
                                    sp = sps[kb % 3]
                                    if key == 0:
                                        te.wait_ge(ct.sem, n_qkv_t)
                                        te.wait_ge(cv.sem, n_qkv_v)
                                        if prev_a2_cs[0] > 0:
                                            te.wait_ge(cs.sem, prev_a2_cs[0])
                                    if key >= 3:
                                        te.wait_ge(cs.sem, exp_done[key - 3])
                                    la = KTb[0:64, p * 2048 + kb * 128:
                                             p * 2048 + (kb + 1) * 128]
                                    ra = QT[0:64, p * 2048 + qc * 512:
                                            p * 2048 + (qc + 1) * 512]
                                    lb = KTb[64:128, p * 2048 + kb * 128:
                                             p * 2048 + (kb + 1) * 128]
                                    rb = QT[64:128, p * 2048 + qc * 512:
                                            p * 2048 + (qc + 1) * 512]
                                    te.matmul(sp[:, 0:512], la, ra,
                                              start=True, stop=True,
                                              tile_position=(0, 0))
                                    ct.inc(te.matmul(sp[:, 512:1024], lb, rb,
                                                     start=True, stop=True,
                                                     tile_position=(64, 0)))
                                    s_done[key] = ct.n

                                def emit_pv(kb):
                                    key = p * 64 + qc * SB + kb
                                    te.wait_ge(cs.sem, exp_done[key])
                                    if kb == 0:
                                        te.wait_ge(cv.sem, ot_write[0])
                                    v3 = V[:].rearrange(
                                        "p (t c) -> p t c", c=1024)
                                    va = v3[:, kb, (2 * p) * 64:
                                            (2 * p) * 64 + 64]
                                    vb = v3[:, kb, (2 * p + 1) * 64:
                                            (2 * p + 1) * 64 + 64]
                                    ea = EK[kb % 2][:, 0:512]
                                    eb = EK[kb % 2][:, 512:1024]
                                    st_, sp_ = (kb == 0), (kb == SB - 1)
                                    te.matmul(pvo[0:64, :], va, ea,
                                              start=st_, stop=sp_,
                                              tile_position=(0, 0),
                                              skip_group_check=True)
                                    te.matmul(pvo[64:128, :], vb, eb,
                                              start=st_, stop=sp_,
                                              tile_position=(0, 64),
                                              skip_group_check=True)
                                    te.matmul(pvd[0:64, :], ones64[:], ea,
                                              start=st_, stop=sp_,
                                              tile_position=(0, 0),
                                              skip_group_check=True)
                                    ct.inc(te.matmul(pvd[64:128, :],
                                                     ones64[:], eb,
                                                     start=st_, stop=sp_,
                                                     tile_position=(0, 64),
                                                     skip_group_check=True))
                                    pv_done[key] = ct.n

                                for kb in range(SB):
                                    emit_s(kb)
                                    if kb >= 2:
                                        emit_pv(kb - 2)
                                emit_pv(SB - 2)
                                emit_pv(SB - 1)
                            blk.tensor(_)
                            n_pv = ct.n

                            def _(v, p=p, qc=qc, n_pv=n_pv):
                                v.wait_ge(ct.sem, n_pv)
                                if dbg_gate[0]:
                                    v.wait_ge(cst.sem, dbg_gate[0])
                                    dbg_gate[0] = 0
                                n_rc = cv.inc(v.reciprocal(RC[:], pvd[:]))
                                v.wait_ge(cv.sem, n_rc)
                                base = p * 2048 + qc * 512
                                v.tensor_tensor(LNXT[0:64, base:base + 512],
                                                pvo[0:64, :], RC[0:64, :],
                                                op=ALU.mult)
                                cv.inc(v.tensor_tensor(
                                    LNXT[64:128, base:base + 512],
                                    pvo[64:128, :], RC[64:128, :],
                                    op=ALU.mult))
                            blk.vector(_)
                            ot_write[0] = cv.n

                        def _(s, p=p, cs_pre=cs_pre):
                            load_tbl(s, SET_LNEXP)
                            for qc in range(4):
                                for kb in range(SB):
                                    key = p * 64 + qc * SB + kb
                                    s.wait_ge(ct.sem, s_done[key])
                                    if key < 2 and prev_xq_store[0][kb] > 0:
                                        s.wait_ge(cqo[kb].sem,
                                                  prev_xq_store[0][kb])
                                    if key >= 2:
                                        s.wait_ge(ct.sem, pv_done[key - 2])
                                    cs.inc(s.activation(
                                        EK[kb % 2][:], sps[kb % 3][:],
                                        AF.Exp, bias=cbias[:, 1:2],
                                        scale=0.125))
                                    assert cs.n == exp_done[key], \
                                        (cs.n, exp_done[key])
                        blk.scalar(_)
                    n_ot_all = cv.n

                if debug and b == 0:
                    def _(g):
                        g.wait_ge(cv.sem, n_ot_all)
                        g.dma_start(out=d_dbg["ot"], in_=LNXT[:]
                                    ).then_inc(cst.sem, 16)
                    blk.gpsimd(_)
                    cst.n += 16
                    dbg_a3 = [True]

                prev_a2_cs[0] = cs.n

                ba3 = contextlib.ExitStack()
                with ba3:
                    wop = [ba3.enter_context(nc.psum_tensor(
                        f"wop{b}_{i}", [128, 1024], F32)) for i in range(2)]
                    x2_evac = {}
                    x2_store = {}
                    xq_store = {}
                    ln2a_done = {}
                    ln2a_qd = {}
                    for sb in range(SB):
                        r = sb % 2

                        def _(sy, b=b, sb=sb, r=r):
                            if sb >= 2:
                                sy.wait_ge(cso[r].sem, x2_store[sb - 2])
                                sy.wait_ge(cv.sem, ln2a_qd[sb - 2])
                            else:
                                sy.wait_ge(cs.sem, norm_done[SB - 1])
                            sy.dma_start(
                                out=x3buf[r][:],
                                in_=d_x[b, sb * 128:(sb + 1) * 128, :]
                            ).then_inc(c3l[r].sem, 16)
                        blk.sync(_)
                        c3l[r].n += 16
                        n_ldx3 = c3l[r].n

                        def _(te, sb=sb, r=r):
                            te.wait_ge(cv.sem, n_ot_all)
                            if sb >= 2:
                                te.wait_ge(cv.sem, x2_evac[sb - 2])
                            last = None
                            for j in range(KT):
                                statp = LNXT[:, j * 2048 + sb * 128:
                                             j * 2048 + (sb + 1) * 128]
                                for c in range(2):
                                    last = te.matmul(
                                        wop[r][:, c * 512:(c + 1) * 512],
                                        statp,
                                        WoB[:, j * 1024 + c * 512:
                                            j * 1024 + (c + 1) * 512],
                                        start=(j == 0), stop=(j == KT - 1))
                            ct.inc(last)
                        blk.tensor(_)
                        n_wo = ct.n

                        def _(v, sb=sb, r=r, n_wo=n_wo, n_ldx3=n_ldx3):
                            v.wait_ge(ct.sem, n_wo)
                            v.wait_ge(c3l[r].sem, n_ldx3)
                            v.tensor_tensor(x3buf[r][:], x3buf[r][:],
                                            wop[r][:], op=ALU.add)
                            cv.inc(v.tensor_tensor(x3buf[r][:], x3buf[r][:],
                                                   bo_bc[:], op=ALU.add))
                        blk.vector(_)
                        x2_evac[sb] = cv.n

                        def _(g, b=b, sb=sb, r=r):
                            g.wait_ge(cv.sem, x2_evac[sb])
                            g.dma_start(
                                out=d_x2[b, sb * 128:(sb + 1) * 128, :],
                                in_=x3buf[r][:]).then_inc(cso[r].sem, 16)
                        blk.gpsimd(_)
                        cso[r].n += 16
                        x2_store[sb] = cso[r].n

                        nrm2_w = []
                        if b > 0 and sb < 2:
                            nrm2_w.append((cv, prev_lnx_guard[0]))
                        ln2a_done[sb] = emit_ln(
                            x3buf[r][:], xbuf[0][:], xbuf[1][:], stat, 20,
                            wait_load=[(cv, x2_evac[sb])],
                            col_guard=(ln2a_done[sb - 1] if sb >= 1 else 0),
                            sq_waits=([(cv, ln2a_qd[sb - 1])] if sb >= 1
                                      else []),
                            norm_waits=nrm2_w,
                            sign_dst=(lnx[1][:], xbuf[0][:]))
                        qd = emit_quant(
                            xbuf[1][:], EK[r][:], D,
                            xbuf[0], lnx[0], lnx[1],
                            aq_bm, aq_sc, aq_rs, aq_r2, aq_sb,
                            wait_v=[(cs, ln2a_done[sb])] +
                                   ([] if sb < 2 else
                                    [(cqo[r], xq_store[sb - 2])]),
                            ext_prep=ln2a_done[sb])
                        ln2a_qd[sb] = qd

                        def _(g, b=b, sb=sb, r=r, qd=qd):
                            g.wait_ge(cv.sem, qd)
                            g.dma_start(
                                out=d_xq[b, sb * 128:(sb + 1) * 128, :],
                                in_=EK[r][:]).then_inc(cqo[r].sem, 16)
                        blk.gpsimd(_)
                        cqo[r].n += 16
                        xq_store[sb] = cqo[r].n
                    if debug and b == 0:
                        def _(g):
                            g.wait_ge(cv.sem, ln2a_qd[SB - 1])
                            g.dma_start(out=d_dbg["aqbm"], in_=aq_bm[:]
                                        ).then_inc(cst.sem, 16)
                            g.dma_start(out=d_dbg["axln"], in_=xbuf[1][:]
                                        ).then_inc(cst.sem, 16)
                            g.dma_start(out=d_dbg["asq"], in_=xbuf[0][:]
                                        ).then_inc(cst.sem, 16)
                            g.dma_start(out=d_dbg["ata"], in_=lnx[0][:]
                                        ).then_inc(cst.sem, 16)
                            g.dma_start(out=d_dbg["atsg"], in_=lnx[1][:]
                                        ).then_inc(cst.sem, 16)
                        blk.gpsimd(_)
                        cst.n += 80
                    prev_lnx_guard[0] = ln2a_qd[SB - 1]
                    prev_ln2_end[0] = ln2a_done[SB - 1]
                    prev_xq_store[0] = (cqo[0].n, cqo[1].n)
                    prev_t_end = ct.n
                    prev_v_end = cv.n
                    prev_qkv_t = n_qkv_t
                    prev_store_last = (cso[0].n, cso[1].n)
                    for _sb in range(SB):
                        x2_store_all[b * SB + _sb] = (_sb % 2, x2_store[_sb])
                        xq_store_all[b * SB + _sb] = (_sb % 2, xq_store[_sb])

        pa_t_end, pa_v_end, pa_s_end, pa_st_end = ct.n, cv.n, cs.n, cst.n
        pa_so = (cso[0].n, cso[1].n)
        pa_qo = (cqo[0].n, cqo[1].n)

        pf = contextlib.ExitStack()
        with pf:
            def pfs(name, shape, dt):
                return pf.enter_context(nc.sbuf_tensor(name, shape, dt))

            WGT = pfs("WGT", [128, KT * DI], BF16)
            WUT = pfs("WUT", [128, KT * DI], BF16)
            WDT = pfs("WDT", [128, NT * D], BF16)
            x2b = [pfs(f"x2b{i}", [128, D], F32) for i in range(2)]
            xqb = [pfs(f"xqb{i}", [128, D], BF16) for i in range(2)]
            XqT = [pfs(f"XqT{i}", [128, D], BF16) for i in range(2)]
            hbuf = pfs("hbuf", [128, DI], F32)
            Hq = pfs("Hq", [128, DI], BF16)
            HqT = [pfs(f"HqT{i}", [128, DI], BF16) for i in range(2)]
            q_tv = pfs("q_tv", [128, DI], F32)
            q_ta = pfs("q_ta", [128, DI], BF16)
            q_sg = pfs("q_sg", [128, DI], BF16)
            q_bm = pfs("q_bm", [128, DI // 16], F32)
            q_r2 = pfs("q_r2", [128, DI // 16], F32)
            q_slb = pfs("q_slb", [128, DI // 16], BF16)
            q_sc = pfs("q_sc", [128, DI // 16], F32)
            q_rs = pfs("q_rs", [128, DI // 16], F32)
            stat2 = pfs("stat2", [128, 24], F32)

            gps = [pf.enter_context(nc.psum_tensor(
                f"gps{i}", [128, 512], F32)) for i in range(2)]
            ups = [pf.enter_context(nc.psum_tensor(
                f"ups{i}", [128, 512], F32)) for i in range(2)]
            tps = [pf.enter_context(nc.psum_tensor(
                f"tps{i}", [128, 1024], BF16)) for i in range(2)]
            dps = pf.enter_context(nc.psum_tensor("dps", [128, 1024], F32))

            def _(g):
                g.wait_ge(ct.sem, pa_t_end)
                g.wait_ge(cv.sem, pa_v_end)
                g.wait_ge(cs.sem, pa_s_end)
                g.wait_ge(cst.sem, pa_st_end)
                g.wait_ge(cso[0].sem, pa_so[0])
                g.wait_ge(cso[1].sem, pa_so[1])
                g.wait_ge(cqo[0].sem, pa_qo[0])
                g.wait_ge(cqo[1].sem, pa_qo[1])
                g.dma_start(out=WGT[:], in_=d_wgt).then_inc(cwl.sem, 16)
                g.dma_start(out=WUT[:], in_=d_wut).then_inc(cwl.sem, 16)
                g.dma_start(out=WDT[:], in_=d_wdt).then_inc(cwl.sem, 16)
            blk.gpsimd(_)
            cwl.n += 48
            n_ffnw = cwl.n

            out_store = {}
            xq_tp_done = {}
            xqt_evac = {}
            hq_tp_done = {}
            hqt_last = {}
            gu_last = {}
            guk_done = {}
            h_done = {}
            qd_h_hist = {}
            down_done = {}
            x2_used = {}

            hbuf_free = {}
            gu_rec_all = {}
            nxqt_all = {}
            nxqtp_all = {}
            nhall_all = {}
            ldx2_all = {}
            hqt_g1_evac = {}
            NCH = len(N_CHUNKS)

            def emit_gu_chunk(gi, r, ci):
                nc0, nc1 = N_CHUNKS[ci]
                cw = nc1 - nc0
                pr = ci % 2
                kk = gi * NCH + ci
                n_xqt = nxqt_all[gi]

                def _(te, r=r, nc0=nc0, nc1=nc1, cw=cw, pr=pr, kk=kk,
                      n_xqt=n_xqt):
                    te.wait_ge(cv.sem, n_xqt)
                    if kk >= 2:
                        te.wait_ge(cv.sem, h_done[kk - 2])
                    for k in range(KT):
                        te.matmul(gps[pr][:, 0:cw],
                                  XqT[r][:, k * 128:(k + 1) * 128],
                                  WGT[:, k * DI + nc0:k * DI + nc1],
                                  start=(k == 0), stop=(k == KT - 1))
                    last = None
                    for k in range(KT):
                        last = te.matmul(
                            ups[pr][:, 0:cw],
                            XqT[r][:, k * 128:(k + 1) * 128],
                            WUT[:, k * DI + nc0:k * DI + nc1],
                            start=(k == 0), stop=(k == KT - 1))
                    ct.inc(last)
                blk.tensor(_)
                guk_done[kk] = ct.n
                gu_rec_all[gi][ci] = ct.n

            def stage_a1(b, sb, gi, r):
                def _(sy, b=b, sb=sb, gi=gi):
                    slot = gi % 2
                    if gi >= 2:
                        sy.wait_ge(ct.sem, xq_tp_done[gi - 2])
                    qslot, qval = xq_store_all[gi]
                    sy.wait_ge(cqo[qslot].sem, qval)
                    sy.dma_start(
                        out=xqb[slot][:],
                        in_=d_xq[b, sb * 128:(sb + 1) * 128, :]
                    ).then_inc(cql[slot].sem, 16)
                blk.sync(_)
                cql[gi % 2].n += 16
                n_ldxq = cql[gi % 2].n

                def _(te, gi=gi, n_ldxq=n_ldxq):
                    te.wait_ge(cql[gi % 2].sem, n_ldxq)
                    if gi == 0:
                        te.wait_ge(cwl.sem, n_ffnw)
                    if gi >= 1:
                        te.wait_ge(cv.sem, xqt_evac[gi - 1])
                    if gi >= 2:
                        te.wait_ge(cv.sem, hqt_g1_evac[gi - 2])
                    last = None
                    for j in range(KT):
                        last = te.transpose(
                            tps[0][:, j * 128:(j + 1) * 128],
                            xqb[gi % 2][:, j * 128:(j + 1) * 128], idb[:])
                    ct.inc(last)
                blk.tensor(_)
                xq_tp_done[gi] = ct.n
                nxqtp_all[gi] = ct.n

            def stage_a2(gi, r):
                n_xqtp = nxqtp_all[gi]

                def _(v, gi=gi, r=r, n_xqtp=n_xqtp):
                    v.wait_ge(ct.sem, n_xqtp)
                    if gi >= 2:
                        v.wait_ge(ct.sem, gu_last[gi - 2])
                    cv.inc(v.tensor_copy(XqT[r][:], tps[0][:]))
                blk.vector(_)
                xqt_evac[gi] = cv.n
                nxqt_all[gi] = cv.n

                gu_rec_all[gi] = {}
                for ci in (0, 1):
                    emit_gu_chunk(gi, r, ci)

            def stage_b1a(gi, r):
                qd_h = qd_h_hist[gi]
                hqt_evacs = []
                for idx, g0 in enumerate(range(0, NT, 8)):
                    ng = min(8, NT - g0)
                    tp = tps[1 - idx % 2]

                    def _(te, g0=g0, ng=ng, qd_h=qd_h, gi=gi, idx=idx,
                          tp=tp, hqt_evacs=tuple(hqt_evacs)):
                        te.wait_ge(cv.sem, qd_h)
                        if idx == 0:
                            if gi >= 1:
                                te.wait_ge(cv.sem, hqt_last[gi - 1])
                        elif idx == 1:
                            if gi + 1 < n_gi:
                                te.wait_ge(cv.sem, xqt_evac[gi + 1])
                        else:
                            te.wait_ge(cv.sem, hqt_evacs[idx - 2])
                        last = None
                        for j in range(ng):
                            last = te.transpose(
                                tp[:, j * 128:(j + 1) * 128],
                                Hq[:, (g0 + j) * 128:(g0 + j + 1) * 128],
                                idb[:])
                        ct.inc(last)
                    blk.tensor(_)
                    n_tp = ct.n

                    def _(v, g0=g0, ng=ng, r=r, n_tp=n_tp, gi=gi, tp=tp):
                        v.wait_ge(ct.sem, n_tp)
                        if g0 == 0 and gi >= 2:
                            v.wait_ge(ct.sem, down_done[gi - 2])
                        cv.inc(v.tensor_copy(
                            HqT[r][:, g0 * 128:(g0 + ng) * 128],
                            tp[:, 0:ng * 128]))
                    blk.vector(_)
                    hqt_evacs.append(cv.n)
                hq_tp_done[gi] = ct.n
                hqt_last[gi] = cv.n
                hqt_g1_evac[gi] = hqt_evacs[1]

            def stage_b1b(gi, r):
                n_hqt = hqt_last[gi]

                def _(te, r=r, n_hqt=n_hqt, gi=gi):
                    te.wait_ge(cv.sem, n_hqt)
                    if gi >= 1:
                        te.wait_ge(cv.sem, x2_used[gi - 1])
                    last = None
                    for t in range(NT):
                        for c in range(2):
                            last = te.matmul(
                                dps[:, c * 512:(c + 1) * 512],
                                HqT[r][:, t * 128:(t + 1) * 128],
                                WDT[:, t * D + c * 512:
                                    t * D + (c + 1) * 512],
                                start=(t == 0), stop=(t == NT - 1))
                    ct.inc(last)
                blk.tensor(_)
                down_done[gi] = ct.n

            def emit_x2_load(gi):
                b, sb = gi // SB, gi % SB
                r = gi % 2

                def _(g, b=b, sb=sb, r=r, gi=gi):
                    if gi >= 2:
                        g.wait_ge(cfo[r].sem, out_store[gi - 2])
                    slot, val = x2_store_all[gi]
                    g.wait_ge(cso[slot].sem, val)
                    g.dma_start(
                        out=x2b[r][:],
                        in_=d_x2[b, sb * 128:(sb + 1) * 128, :]
                    ).then_inc(cfl[r].sem, 16)
                blk.gpsimd(_)
                cfl[r].n += 16
                ldx2_all[gi] = cfl[r].n

            def stage_b2(gi, r):
                b, sb = gi // SB, gi % SB
                n_down = down_done[gi]
                n_ldx2 = ldx2_all[gi]

                def _(v, r=r, n_down=n_down, n_ldx2=n_ldx2):
                    v.wait_ge(ct.sem, n_down)
                    v.wait_ge(cfl[r].sem, n_ldx2)
                    cv.inc(v.tensor_tensor(x2b[r][:], x2b[r][:], dps[:],
                                           op=ALU.add))
                blk.vector(_)
                x2_used[gi] = cv.n
                n_out = cv.n

                def _(g, b=b, sb=sb, r=r, n_out=n_out):
                    g.wait_ge(cv.sem, n_out)
                    g.dma_start(
                        out=d_out[b, sb * 128:(sb + 1) * 128, :],
                        in_=x2b[r][:]).then_inc(cfo[r].sem, 16)
                blk.gpsimd(_)
                cfo[r].n += 16
                out_store[gi] = cfo[r].n

                if gi + 2 < n_gi:
                    emit_x2_load(gi + 2)

            def stage_c1(b, sb, gi, r):
                gu_rec = gu_rec_all[gi]
                cv_pre_tt = cv.n
                for ci in range(NCH):
                    h_done[gi * NCH + ci] = cv_pre_tt + ci + 1
                for ci in range(2, NCH):
                    emit_gu_chunk(gi, r, ci)
                gu_last[gi] = ct.n

                cs_pre_silu = cs.n

                def _(s, gi=gi, gu_rec=gu_rec, cv_pre_tt=cv_pre_tt):
                    def ss(ci, inc):
                        nc0, nc1 = N_CHUNKS[ci]
                        cw = nc1 - nc0
                        s.wait_ge(cv.sem, cv_pre_tt + ci + 1)
                        s.activation(q_sg[:, nc0:nc0 + cw],
                                     hbuf[:, nc0:nc0 + cw], AF.Sign)
                        a = s.activation(q_tv[:, nc0:nc0 + cw],
                                         hbuf[:, nc0:nc0 + cw], AF.Square)
                        if inc:
                            cs.inc(a)

                    load_tbl(s, SET_SILU)
                    if gi >= 1:
                        s.wait_ge(cv.sem, hbuf_free[gi - 1])
                    for ci in range(NCH):
                        nc0, nc1 = N_CHUNKS[ci]
                        cw = nc1 - nc0
                        pr = ci % 2
                        s.wait_ge(ct.sem, gu_rec[ci])
                        if ci >= 2:
                            s.wait_ge(cv.sem, cv_pre_tt + ci - 1)
                        cs.inc(s.activation(hbuf[:, nc0:nc0 + cw],
                                            gps[pr][:, 0:cw], AF.Silu))
                        if ci == 1 and last_qd[0] > 0:
                            s.wait_ge(cv.sem, last_qd[0])
                        if ci >= 1:
                            ss(ci - 1, False)
                    ss(NCH - 1, True)
                blk.scalar(_)

                for ci, (nc0, nc1) in enumerate(N_CHUNKS):
                    cw = nc1 - nc0
                    pr = ci % 2

                    def _(v, nc0=nc0, cw=cw, pr=pr, ci=ci,
                          cs_pre_silu=cs_pre_silu):
                        v.wait_ge(cs.sem, cs_pre_silu + ci + 1)
                        cv.inc(v.tensor_tensor(hbuf[:, nc0:nc0 + cw],
                                               hbuf[:, nc0:nc0 + cw],
                                               ups[pr][:, 0:cw],
                                               op=ALU.mult))
                    blk.vector(_)
                assert cv.n == cv_pre_tt + NCH
                nhall_all[gi] = (cv.n, cs.n)

            def stage_c2(b, sb, gi, r):
                n_h_all, n_sgn_h = nhall_all[gi]

                marks = {}
                qd_h = emit_quant(
                    hbuf[:], Hq[:], DI,
                    q_tv, q_ta, q_sg, q_bm, q_sc, q_rs,
                    q_r2, q_slb,
                    wait_v=[(cv, n_h_all),
                            (ct, hq_tp_done.get(gi - 1, 0))],
                    ext_prep=n_sgn_h, marks=marks)
                qd_h_hist[gi] = qd_h
                hbuf_free[gi] = marks["bmax"]

            n_gi = n_batch * SB
            emit_x2_load(0)
            emit_x2_load(1)
            stage_a1(0, 0, 0, 0)
            stage_a2(0, 0)
            for b in range(n_batch):
                for sb in range(SB):
                    gi = b * SB + sb
                    r = sb % 2
                    if gi + 1 < n_gi:
                        stage_a1((gi + 1) // SB, (gi + 1) % SB, gi + 1,
                                 (gi + 1) % 2)
                    stage_c1(b, sb, gi, r)
                    if gi >= 1:
                        stage_b1b(gi - 1, (gi - 1) % 2)
                    if gi + 1 < n_gi:
                        stage_a2(gi + 1, (gi + 1) % 2)
                    stage_c2(b, sb, gi, r)
                    stage_b1a(gi, r)
                    if gi >= 1:
                        stage_b2(gi - 1, (gi - 1) % 2)
            stage_b1b(n_gi - 1, (n_gi - 1) % 2)
            stage_b2(n_gi - 1, (n_gi - 1) % 2)

            if debug:
                def _(g):
                    g.wait_ge(cwl.sem, n_ffnw)
                    g.dma_start(out=d_dbg["wgt"], in_=WGT[:]
                                ).then_inc(cst.sem, 16)
                    g.dma_start(out=d_dbg["wut"], in_=WUT[:]
                                ).then_inc(cst.sem, 16)
                    g.dma_start(out=d_dbg["wdt"], in_=WDT[:]
                                ).then_inc(cst.sem, 16)
                    g.wait_ge(cv.sem, cv.n)
                    g.wait_ge(cs.sem, cs.n)
                    last_r = (SB - 1) % 2
                    g.dma_start(out=d_dbg["xq"], in_=xqb[(SB - 1) % 2][:]
                                ).then_inc(cst.sem, 16)
                    g.dma_start(out=d_dbg["hq"], in_=Hq[:]
                                ).then_inc(cst.sem, 16)
                    g.dma_start(out=d_dbg["hbuf"], in_=hbuf[:]
                                ).then_inc(cst.sem, 16)
                    g.dma_start(out=d_dbg["xln2"], in_=x2b[last_r][:]
                                ).then_inc(cst.sem, 16)
                    g.dma_start(out=d_dbg["xqt"], in_=XqT[last_r][:]
                                ).then_inc(cst.sem, 16)
                    g.dma_start(out=d_dbg["hqt"], in_=HqT[last_r][:]
                                ).then_inc(cst.sem, 16)
                    g.dma_start(out=d_dbg["x2"],
                                in_=d_x2.rearrange("b s d -> (b s) d")
                                ).then_inc(cst.sem, 16)
                blk.gpsimd(_)
                cst.n += 160

            def _(g):
                g.wait_ge(cst.sem, cst.n)
                g.wait_ge(cso[0].sem, cso[0].n)
                g.wait_ge(cso[1].sem, cso[1].n)
                g.wait_ge(cfo[0].sem, cfo[0].n)
                g.wait_ge(cfo[1].sem, cfo[1].n)
                g.wait_ge(cqo[0].sem, cqo[0].n)
                g.wait_ge(cqo[1].sem, cqo[1].n)
            blk.gpsimd(_)

    return nc


def kernel(L0_attn_hidden_states, L0_attn_Wq, L0_attn_Wkv, L0_attn_Wo,
           L0_attn_bo, L0_ffn_w_gate, L0_ffn_w_up, L0_ffn_w_down):
    x = np.ascontiguousarray(L0_attn_hidden_states, dtype=np.float32)
    wq = np.ascontiguousarray(L0_attn_Wq, dtype=np.float32)
    wkv = np.ascontiguousarray(L0_attn_Wkv, dtype=np.float32)
    wo = np.ascontiguousarray(L0_attn_Wo, dtype=np.float32)
    bo = np.ascontiguousarray(L0_attn_bo, dtype=np.float32)
    wg = np.ascontiguousarray(L0_ffn_w_gate, dtype=np.float32)
    wu = np.ascontiguousarray(L0_ffn_w_up, dtype=np.float32)
    wd = np.ascontiguousarray(L0_ffn_w_down, dtype=np.float32)

    bo_t = np.ascontiguousarray(np.broadcast_to(bo[None, :], (128, D)))
    ident = np.eye(128, dtype=np.float32)

    nc = build()
    in_maps = []
    for c in range(N_CORES):
        in_maps.append({
            "x": np.ascontiguousarray(x[c * N_BATCH:(c + 1) * N_BATCH]),
            "wq": wq, "wkv": wkv, "wo": wo, "bo_t": bo_t,
            "wg": wg, "wu": wu, "wd": wd, "ident": ident,
        })
    res = run_bass_kernel_spmd(nc, in_maps, core_ids=list(range(N_CORES)))
    return np.concatenate([r["out"] for r in res.results], axis=0)


# version 15 (final)
# speedup vs baseline: 1.1767x; 1.0248x over previous
import contextlib
import numpy as np

import concourse.bass as bass
import concourse.mybir as mybir
from concourse.bass_utils import run_bass_kernel_spmd
from concourse.hw_specs import get_activation_tables

F32 = mybir.dt.float32
BF16 = mybir.dt.bfloat16
AF = mybir.ActivationFunctionType
ALU = mybir.AluOpType
AX = mybir.AxisListType

N_CORES = 8
N_BATCH = 4
S = 2048
D = 1024
NH = 16
DH = 64
DI = 2816
SB = S // 128
KT = D // 128
NT = DI // 128
NPAIR = NH // 2
EPS = 1e-5
N_CHUNKS = [(c, min(c + 512, DI)) for c in range(0, DI, 512)]


class Ctr:
    def __init__(self, nc, es, name):
        self.sem = es.enter_context(nc.semaphore(name))
        self.n = 0

    def inc(self, instr, k=1):
        instr.then_inc(self.sem, k)
        self.n += k
        return self.n


def build(n_batch=N_BATCH, debug=False):
    nc = bass.Bass("TRN2", target_bir_lowering=False,
                   detect_race_conditions=False)
    tbl_ids = {name: i for i, name in
               enumerate(get_activation_tables(nc.m.arch))}
    SET_LNEXP = tbl_ids["natural_log_exp_and_others"]
    SET_SILU = tbl_ids["silu_and_others"]

    def load_tbl(eng, set_id):
        eng.add_instruction(mybir.InstLoadActFuncSet(
            name=nc.get_next_instruction_name(), act_func_set_id=set_id,
            ins=[], outs=[]))

    d_x = nc.dram_tensor("x", [n_batch, S, D], F32, kind="ExternalInput").ap()
    d_wq = nc.dram_tensor("wq", [D, D], F32, kind="ExternalInput").ap()
    d_wkv = nc.dram_tensor("wkv", [D, 2 * D], F32, kind="ExternalInput").ap()
    d_wo = nc.dram_tensor("wo", [D, D], F32, kind="ExternalInput").ap()
    d_bo = nc.dram_tensor("bo_t", [128, D], F32, kind="ExternalInput").ap()
    d_wg = nc.dram_tensor("wg", [DI, D], F32, kind="ExternalInput").ap()
    d_wu = nc.dram_tensor("wu", [DI, D], F32, kind="ExternalInput").ap()
    d_wd = nc.dram_tensor("wd", [D, DI], F32, kind="ExternalInput").ap()
    d_id = nc.dram_tensor("ident", [128, 128], F32, kind="ExternalInput").ap()
    d_out = nc.dram_tensor("out", [n_batch, S, D], F32,
                           kind="ExternalOutput").ap()

    if debug:
        d_dbg = {}
        for nm, shape in (
            ("lnxt", [128, KT * 2048]), ("qt", [128, KT * 2048]),
            ("ktb", [128, KT * 2048]), ("v", [128, SB * 1024]),
            ("ot", [128, KT * 2048]), ("xq", [128, D]), ("hq", [128, DI]),
            ("hbuf", [128, DI]), ("xln2", [128, D]), ("xqt", [128, D]),
            ("aqbm", [128, D // 16]), ("axln", [128, D]), ("asq", [128, D]),
            ("ata", [128, D]), ("atsg", [128, D]),
            ("hqt", [128, DI]), ("wgt", [128, KT * DI]),
            ("wut", [128, KT * DI]), ("wdt", [128, NT * D]),
            ("x2", [n_batch * S, D]),
        ):
            d_dbg[nm] = nc.dram_tensor("dbg_" + nm, shape, F32,
                                       kind="ExternalOutput").ap()
    d_x2 = nc.dram_tensor("x2_s", [n_batch, S, D], F32, kind="Internal").ap()
    d_xq = nc.dram_tensor("xq_s", [n_batch, S, D], BF16, kind="Internal").ap()
    d_wgt = nc.dram_tensor("wgt_s", [128, KT * DI], BF16, kind="Internal").ap()
    d_wut = nc.dram_tensor("wut_s", [128, KT * DI], BF16, kind="Internal").ap()
    d_wdt = nc.dram_tensor("wdt_s", [128, NT * D], BF16, kind="Internal").ap()

    es = contextlib.ExitStack()
    with es:
        def sbuf(name, shape, dt):
            return es.enter_context(nc.sbuf_tensor(name, shape, dt))

        cin = Ctr(nc, es, "cin")
        cxl = [Ctr(nc, es, f"cxl{i}") for i in range(2)]
        c3l = [Ctr(nc, es, f"c3l{i}") for i in range(2)]
        cfl = [Ctr(nc, es, f"cfl{i}") for i in range(2)]
        cpl = [Ctr(nc, es, f"cpl{i}") for i in range(2)]
        cso = [Ctr(nc, es, f"cso{i}") for i in range(2)]
        cqo = [Ctr(nc, es, f"cqo{i}") for i in range(2)]
        cql = [Ctr(nc, es, f"cql{i}") for i in range(2)]
        cfo = [Ctr(nc, es, f"cfo{i}") for i in range(2)]
        cwl = Ctr(nc, es, "cwl")
        cst = Ctr(nc, es, "cst")
        ct = Ctr(nc, es, "ct")
        cv = Ctr(nc, es, "cv")
        cs = Ctr(nc, es, "cs")

        blk = es.enter_context(nc.Block())

        bo_bc = sbuf("bo_bc", [128, D], F32)
        idb = sbuf("idb", [128, 128], BF16)
        ones64 = sbuf("ones64", [128, 64], BF16)
        cbias = sbuf("cbias", [128, 2], F32)

        def _(g):
            g.dma_start(out=idb[:], in_=d_id).then_inc(cwl.sem, 16)
            g.dma_start(out=bo_bc[:], in_=d_bo).then_inc(cwl.sem, 16)
        blk.gpsimd(_)
        cwl.n += 32
        n_ld0 = cwl.n

        def _(v):
            v.wait_ge(cwl.sem, n_ld0)
            v.memset(ones64[:], 1.0)
            v.memset(cbias[:, 0:1], EPS)
            cv.inc(v.memset(cbias[:, 1:2], -3.0))
        blk.vector(_)
        n_consts = cv.n

        last_qd = [0]

        def emit_quant(src_ap, dst_ap, F, sq, ta, tsg, bmax, scl, rscl,
                       rscl2, sclb, wait_v=(), wait_s=(), ext_prep=None,
                       marks=None, pre_red=None):
            G = F // 16

            if ext_prep is None:
                def _(s):
                    for ctr, n in wait_s:
                        if n > 0:
                            s.wait_ge(ctr.sem, n)
                    if last_qd[0] > 0:
                        s.wait_ge(cv.sem, last_qd[0])
                    load_tbl(s, SET_LNEXP)
                    s.activation(tsg[:, 0:F], src_ap, AF.Sign)
                    cs.inc(s.activation(sq[:, 0:F], src_ap, AF.Square))
                blk.scalar(_)
                n_prep = cs.n
            else:
                n_prep = ext_prep

            def _(v):
                for ctr, n in wait_v:
                    if n > 0:
                        v.wait_ge(ctr.sem, n)
                if pre_red is None:
                    src3 = src_ap.rearrange("p (g w) -> p g w", w=16)
                    n_red = cv.inc(v.tensor_reduce(
                        bmax[:, 0:G].rearrange("p (g o) -> p g o", o=1),
                        src3,
                        axis=AX.X, op=ALU.max, apply_absolute_value=True))
                else:
                    n_red = pre_red
                if marks is not None:
                    marks["bmax"] = n_red
                v.wait_ge(cv.sem, n_red)
                v.tensor_scalar(scl[:, 0:G], bmax[:, 0:G], 1.0 / 6.0, 1e-12,
                                op0=ALU.mult, op1=ALU.max)
                n_rc = cv.inc(v.reciprocal(rscl[:, 0:G], scl[:, 0:G]))
                v.wait_ge(cv.sem, n_rc)
                v.tensor_tensor(rscl2[:, 0:G], rscl[:, 0:G], rscl[:, 0:G],
                                op=ALU.mult)
                v.tensor_copy(sclb[:, 0:G], scl[:, 0:G])
                v.wait_ge(cs.sem, n_prep)
                r2b = rscl2[:, 0:G].rearrange("p (g o) -> p g o", o=1) \
                    .broadcast_to([128, G, 16])
                sqf = sq[:, 0:F]
                v.tensor_tensor(sqf.rearrange("p (g w) -> p g w", w=16),
                                sqf.rearrange("p (g w) -> p g w", w=16),
                                r2b, op=ALU.mult)
                v.tensor_copy(dst_ap, sqf)
                v.tensor_scalar(ta[:, 0:F], dst_ap, 0.0625, None,
                                op0=ALU.is_gt)
                for t in (0.5625, 1.5625, 3.0625):
                    v.scalar_tensor_tensor(ta[:, 0:F], dst_ap, t, ta[:, 0:F],
                                           op0=ALU.is_gt, op1=ALU.add)
                v.tensor_scalar(ta[:, 0:F], ta[:, 0:F], 0.5, None,
                                op0=ALU.mult)
                for t in (6.25, 12.25):
                    v.scalar_tensor_tensor(ta[:, 0:F], sqf, t, ta[:, 0:F],
                                           op0=ALU.is_gt, op1=ALU.add)
                v.tensor_scalar(dst_ap, sqf, 25.0, 2.0,
                                op0=ALU.is_gt, op1=ALU.mult)
                v.tensor_tensor(ta[:, 0:F], ta[:, 0:F], dst_ap, op=ALU.add)
                sb3 = sclb[:, 0:G].rearrange("p (g o) -> p g o", o=1) \
                    .broadcast_to([128, G, 16])
                v.tensor_tensor(tsg[:, 0:F].rearrange("p (g w) -> p g w", w=16),
                                tsg[:, 0:F].rearrange("p (g w) -> p g w", w=16),
                                sb3, op=ALU.mult)
                cv.inc(v.tensor_tensor(dst_ap, ta[:, 0:F], tsg[:, 0:F],
                                       op=ALU.mult))
            blk.vector(_)
            last_qd[0] = cv.n
            return cv.n

        def emit_ln(src_ap, scratch_ap, dst_ap, stat_t, c0, wait_load,
                    col_guard=0, sq_waits=(), norm_waits=(), sign_dst=None):
            xs, mn, nmn, ssr, nm2, nm2e, var_, lnv, rs, nb = (
                stat_t[:, c0 + i:c0 + i + 1] for i in range(10))

            def _(v):
                for ctr, n in wait_load:
                    v.wait_ge(ctr.sem, n)
                if col_guard > 0:
                    v.wait_ge(cs.sem, col_guard)
                n_red = cv.inc(v.reduce_sum(xs, src_ap, axis=AX.X))
                v.wait_ge(cv.sem, n_red)
                v.tensor_scalar(mn, xs, 1.0 / D, None, op0=ALU.mult)
                cv.inc(v.tensor_scalar(nmn, xs, -1.0 / D, None, op0=ALU.mult))
            blk.vector(_)
            n1 = cv.n

            def _(s):
                for ctr, n in wait_load:
                    s.wait_ge(ctr.sem, n)
                for ctr, n in sq_waits:
                    s.wait_ge(ctr.sem, n)
                if sign_dst is not None and last_qd[0] > 0:
                    s.wait_ge(cv.sem, last_qd[0])
                load_tbl(s, SET_LNEXP)
                s.activation(scratch_ap, src_ap, AF.Square, accum_out=ssr)
                s.wait_ge(cv.sem, n1)
                s.activation(nm2, nmn, AF.Identity, scale=mn, bias=0.0)
                n_a = cs.inc(s.activation(nm2e, nm2, AF.Identity,
                                          bias=cbias[:, 0:1]))
                s.wait_ge(cs.sem, n_a)
                s.activation(var_, ssr, AF.Identity, scale=1.0 / D,
                             bias=nm2e)
                s.activation(lnv, var_, AF.Ln)
                n_b = cs.inc(s.activation(rs, lnv, AF.Exp, scale=-0.5))
                s.wait_ge(cs.sem, n_b)
                n_c = cs.inc(s.activation(nb, nmn, AF.Identity, scale=rs,
                                          bias=0.0))
                s.wait_ge(cs.sem, n_c)
                for ctr, n in norm_waits:
                    s.wait_ge(ctr.sem, n)
                cs.inc(s.activation(dst_ap, src_ap, AF.Identity,
                                    scale=rs, bias=nb))
                if sign_dst is not None:
                    sgn_ap, sq_ap = sign_dst
                    s.activation(sgn_ap, dst_ap, AF.Sign)
                    cs.inc(s.activation(sq_ap, dst_ap, AF.Square))
            blk.scalar(_)
            return cs.n

        p0 = contextlib.ExitStack()
        with p0:
            def p0_sbuf(name, shape, dt):
                return p0.enter_context(nc.sbuf_tensor(name, shape, dt))

            p0_wtile = [p0_sbuf(f"p0w{i}", [128, DI], F32) for i in range(2)]
            p0_q = [p0_sbuf(f"p0q{i}", [128, DI], BF16) for i in range(2)]
            p0_tv = p0_sbuf("p0tv", [128, DI], F32)
            p0_ta = p0_sbuf("p0ta", [128, DI], BF16)
            p0_sg = p0_sbuf("p0sg", [128, DI], BF16)
            p0_bm = p0_sbuf("p0bm", [128, DI // 16], F32)
            p0_r2 = p0_sbuf("p0r2", [128, DI // 16], F32)
            p0_sb = p0_sbuf("p0sb", [128, DI // 16], BF16)
            p0_sc = p0_sbuf("p0sc", [128, DI // 16], F32)
            p0_rs = p0_sbuf("p0rs", [128, DI // 16], F32)
            p0_stage = p0_sbuf("p0_stage", [128, KT * DI], BF16)
            p0_tp = p0.enter_context(
                nc.psum_tensor("p0_tp", [128, 1024], BF16))

            def prep_weight(d_src, d_dst, n_tiles, tile_f, tp_rows):
                nblk = tile_f // 128
                quant_done = {}
                evac_done = [0]
                tp_done = [0, 0]
                entry_st = cst.n
                for t in range(n_tiles):
                    r = t % 2

                    def _(g, t=t, r=r):
                        if t >= 2:
                            g.wait_ge(cv.sem, quant_done[t - 2])
                        g.dma_start(out=p0_wtile[r][:, 0:tile_f],
                                    in_=d_src[t * 128:(t + 1) * 128, 0:tile_f]
                                    ).then_inc(cpl[r].sem, 16)
                    blk.gpsimd(_)
                    cpl[r].n += 16
                    n_ld = cpl[r].n
                    qd = emit_quant(
                        p0_wtile[r][:, 0:tile_f], p0_q[r][:, 0:tile_f],
                        tile_f,
                        p0_tv, p0_ta, p0_sg, p0_bm, p0_sc, p0_rs,
                        p0_r2, p0_sb,
                        wait_v=[(cpl[r], n_ld), (ct, tp_done[1])],
                        wait_s=[(cpl[r], n_ld)])
                    quant_done[t] = qd
                    for g0 in range(0, nblk, 8):
                        ng = min(8, nblk - g0)

                        def _(te, g0=g0, ng=ng, qd=qd, t=t):
                            te.wait_ge(cv.sem, max(qd, n_consts))
                            if evac_done[0] > 0:
                                te.wait_ge(cv.sem, evac_done[0])
                            last = None
                            for j in range(ng):
                                last = te.transpose(
                                    p0_tp[:, j * 128:(j + 1) * 128],
                                    p0_q[t % 2][:, (g0 + j) * 128:
                                                 (g0 + j + 1) * 128],
                                    idb[:])
                            ct.inc(last)
                        blk.tensor(_)
                        n_tp = ct.n

                        def _(v, g0=g0, ng=ng, t=t, n_tp=n_tp):
                            v.wait_ge(ct.sem, n_tp)
                            if t == 0 and g0 == 0 and entry_st > 0:
                                v.wait_ge(cst.sem, entry_st)
                            src = p0_tp[:, 0:ng * 128].rearrange(
                                "p (j c) -> p j c", c=128)
                            dst = p0_stage[:, 0:nblk * tp_rows].rearrange(
                                "p (j n) -> p j n", n=tp_rows)[
                                :, g0:g0 + ng, t * 128:(t + 1) * 128]
                            cv.inc(v.tensor_copy(dst, src))
                        blk.vector(_)
                        evac_done[0] = cv.n
                    tp_done[1] = tp_done[0]
                    tp_done[0] = ct.n

                def _(g):
                    g.wait_ge(cv.sem, evac_done[0])
                    g.dma_start(out=d_dst, in_=p0_stage[:, 0:nblk * tp_rows]
                                ).then_inc(cst.sem, 16)
                blk.gpsimd(_)
                cst.n += 16
                return cst.n

            prep_weight(d_wg, d_wgt, NT, D, DI)
            prep_weight(d_wu, d_wut, NT, D, DI)
            st_prep = prep_weight(d_wd, d_wdt, KT, DI, D)

        pa = contextlib.ExitStack()
        with pa:
            def pas(name, shape, dt):
                return pa.enter_context(nc.sbuf_tensor(name, shape, dt))

            WW = pas("WW", [128, KT * 2048], BF16)
            WoB = pas("WoB", [128, KT * 1024], BF16)
            LNXT = pas("LNXT", [128, KT * 2048], BF16)
            QT = pas("QT", [128, KT * 2048], BF16)
            KTb = pas("KTb", [128, KT * 2048], BF16)
            V = pas("V", [128, SB * 1024], BF16)
            EK = [pas(f"EK{i}", [128, 1024], BF16) for i in range(2)]
            RC = pas("RC", [128, 512], F32)
            xbuf = [pas(f"xbuf{i}", [128, D], F32) for i in range(2)]
            aq_bm = pas("aq_bm", [128, D // 16], F32)
            aq_sc = pas("aq_sc", [128, D // 16], F32)
            aq_rs = pas("aq_rs", [128, D // 16], F32)
            aq_r2 = pas("aq_r2", [128, D // 16], F32)
            aq_sb = pas("aq_sb", [128, D // 16], BF16)
            x3buf = [pas(f"x3buf{i}", [128, D], F32) for i in range(2)]
            lnx = [pas(f"lnx{i}", [128, D], BF16) for i in range(2)]
            stat = pas("stat", [128, 40], F32)

            def _(g):
                g.wait_ge(cst.sem, st_prep)
                for k in range(KT):
                    g.dma_start(out=WoB[:, k * 1024:(k + 1) * 1024],
                                in_=d_wo[k * 128:(k + 1) * 128, :]
                                ).then_inc(cwl.sem, 16)
            blk.gpsimd(_)
            cwl.n += 16 * 8

            dbg_gate = [0]
            prev_a2_cs = [0]
            prev_lnx_guard = [0]
            prev_ln2_end = [0]
            prev_xq_store = [(0, 0)]
            x2_store_all = {}
            xq_store_all = {}
            prev_t_end = 0
            prev_v_end = 0
            prev_qkv_t = 0
            prev_store_last = (0, 0)

            for b in range(n_batch):
                bps = contextlib.ExitStack()
                with bps:
                    pq = [bps.enter_context(nc.psum_tensor(
                        f"pq{b}_{i}", [128, 1024], F32)) for i in range(3)]
                    tp2 = [bps.enter_context(nc.psum_tensor(
                        f"tp{b}_{i}", [128, 1024], BF16)) for i in range(2)]

                    def _(g, b=b):
                        if b > 0:
                            g.wait_ge(ct.sem, prev_qkv_t)
                        for k in range(KT):
                            g.dma_start(out=WW[:, k * 2048:k * 2048 + 1024],
                                        in_=d_wq[k * 128:(k + 1) * 128, :]
                                        ).then_inc(cwl.sem, 16)
                    blk.gpsimd(_)
                    cwl.n += 16 * KT
                    n_wq = cwl.n

                    norm_done = {}
                    tpos_done = {}
                    lnxt_evac = {}
                    for sb in range(SB):
                        r = sb % 2

                        def _(sy, b=b, sb=sb, r=r):
                            if sb >= 2:
                                sy.wait_ge(cs.sem, norm_done[sb - 2])
                            if b == 0 and sb == 0:
                                sy.wait_ge(cst.sem, st_prep)
                            if b > 0 and sb < 2:
                                sy.wait_ge(cs.sem, prev_ln2_end[0])
                                sy.wait_ge(cv.sem, prev_lnx_guard[0])
                            sy.dma_start(
                                out=xbuf[r][:],
                                in_=d_x[b, sb * 128:(sb + 1) * 128, :]
                            ).then_inc(cxl[r].sem, 16)
                        blk.sync(_)
                        cxl[r].n += 16
                        n_ldx = cxl[r].n

                        sq_w = []
                        if b > 0 and sb < 2:
                            sq_w.append((cso[0], prev_store_last[0]))
                            sq_w.append((cso[1], prev_store_last[1]))
                        nrm_w = []
                        if sb >= 2:
                            nrm_w.append((ct, tpos_done[sb - 2]))
                        if b > 0 and sb < 2:
                            nrm_w.append((cv, prev_lnx_guard[0]))
                        norm_done[sb] = emit_ln(
                            xbuf[r][:], x3buf[r][:], lnx[r][:], stat, r * 10,
                            wait_load=[(cxl[r], n_ldx)],
                            col_guard=(norm_done[sb - 2] if sb >= 2 else 0),
                            sq_waits=sq_w, norm_waits=nrm_w)

                        def _(te, r=r, sb=sb, b=b):
                            te.wait_ge(cs.sem, norm_done[sb])
                            te.wait_ge(cv.sem, n_consts)
                            if sb >= 2:
                                te.wait_ge(cv.sem, lnxt_evac[sb - 2])
                            if b > 0 and sb == 0:
                                te.wait_ge(cv.sem, prev_v_end)
                            last = None
                            for j in range(KT):
                                last = te.transpose(
                                    tp2[sb % 2][:, j * 128:(j + 1) * 128],
                                    lnx[r][:, j * 128:(j + 1) * 128], idb[:])
                            ct.inc(last)
                        blk.tensor(_)
                        tpos_done[sb] = ct.n

                        def _(v, sb=sb, b=b):
                            v.wait_ge(ct.sem, tpos_done[sb])
                            src = tp2[sb % 2][:].rearrange(
                                "p (j c) -> p j c", c=128)
                            dst = LNXT[:].rearrange(
                                "p (j q) -> p j q", q=2048)[
                                :, :, sb * 128:(sb + 1) * 128]
                            cv.inc(v.tensor_copy(dst, src))
                        blk.vector(_)
                        lnxt_evac[sb] = cv.n
                    n_lnxt_all = cv.n

                    pq_evac = []
                    uidx = [0]

                    def qkv_unit(stat_fn, rhs_fn, out_ap):
                        u = uidx[0] % 3
                        myu = uidx[0]
                        my_attw = n_attw

                        def _(te):
                            te.wait_ge(cv.sem, n_lnxt_all)
                            te.wait_ge(cwl.sem, my_attw)
                            if myu >= 3:
                                te.wait_ge(cv.sem, pq_evac[myu - 3])
                            last = None
                            for k in range(KT):
                                lhsT = stat_fn(k)
                                for c in range(2):
                                    last = te.matmul(
                                        pq[u][:, c * 512:(c + 1) * 512],
                                        lhsT, rhs_fn(k, c),
                                        start=(k == 0), stop=(k == KT - 1))
                            ct.inc(last)
                        blk.tensor(_)
                        n_mm = ct.n

                        def _(v, u=u, out_ap=out_ap, n_mm=n_mm):
                            v.wait_ge(ct.sem, n_mm)
                            cv.inc(v.tensor_copy(out_ap, pq[u][:]))
                        blk.vector(_)
                        pq_evac.append(cv.n)
                        uidx[0] += 1

                    n_attw = n_wq
                    for j in range(KT):
                        for half in range(2):
                            qkv_unit(
                                (lambda k, j=j:
                                 WW[:, k * 2048 + j * 128:
                                    k * 2048 + (j + 1) * 128]),
                                (lambda k, c, half=half:
                                 LNXT[:, k * 2048 + half * 1024 + c * 512:
                                      k * 2048 + half * 1024 + (c + 1) * 512]),
                                QT[:, j * 2048 + half * 1024:
                                   j * 2048 + (half + 1) * 1024])
                    n_q_mm = ct.n

                    def _(g, n_q_mm=n_q_mm):
                        g.wait_ge(ct.sem, n_q_mm)
                        for k in range(KT):
                            g.dma_start(out=WW[:, k * 2048:(k + 1) * 2048],
                                        in_=d_wkv[k * 128:(k + 1) * 128, :]
                                        ).then_inc(cwl.sem, 16)
                    blk.gpsimd(_)
                    cwl.n += 16 * KT
                    n_attw = cwl.n

                    for j in range(KT):
                        for half in range(2):
                            qkv_unit(
                                (lambda k, j=j:
                                 WW[:, k * 2048 + j * 128:
                                    k * 2048 + (j + 1) * 128]),
                                (lambda k, c, half=half:
                                 LNXT[:, k * 2048 + half * 1024 + c * 512:
                                      k * 2048 + half * 1024 + (c + 1) * 512]),
                                KTb[:, j * 2048 + half * 1024:
                                    j * 2048 + (half + 1) * 1024])
                    for t in range(SB):
                        qkv_unit(
                            (lambda k, t=t:
                             LNXT[:, k * 2048 + t * 128:
                                  k * 2048 + (t + 1) * 128]),
                            (lambda k, c:
                             WW[:, k * 2048 + 1024 + c * 512:
                                k * 2048 + 1024 + (c + 1) * 512]),
                            V[:, t * 1024:(t + 1) * 1024])
                    n_qkv_t = ct.n
                    n_qkv_v = cv.n

                    if debug and b == 0:
                        def _(g):
                            g.wait_ge(cv.sem, n_qkv_v)
                            g.dma_start(out=d_dbg["lnxt"], in_=LNXT[:]
                                        ).then_inc(cst.sem, 16)
                            g.dma_start(out=d_dbg["qt"], in_=QT[:]
                                        ).then_inc(cst.sem, 16)
                            g.dma_start(out=d_dbg["ktb"], in_=KTb[:]
                                        ).then_inc(cst.sem, 16)
                            g.dma_start(out=d_dbg["v"], in_=V[:]
                                        ).then_inc(cst.sem, 16)
                        blk.gpsimd(_)
                        cst.n += 64
                        dbg_gate[0] = cst.n

                ba2 = contextlib.ExitStack()
                with ba2:
                    sps = [ba2.enter_context(nc.psum_tensor(
                        f"sps{b}_{i}", [128, 1024], F32)) for i in range(3)]
                    pvo = ba2.enter_context(
                        nc.psum_tensor(f"pvo{b}", [128, 512], F32))
                    pvd = ba2.enter_context(
                        nc.psum_tensor(f"pvd{b}", [128, 512], F32))

                    s_done = {}
                    exp_done = {}
                    pv_done = {}
                    ot_write = [0]
                    for p in range(NPAIR):
                        key0 = p * 64
                        cs_pre = cs.n
                        for i in range(64):
                            exp_done[key0 + i] = cs_pre + i + 1
                        for qc in range(4):
                            def _(te, p=p, qc=qc):
                                def emit_s(kb):
                                    key = p * 64 + qc * SB + kb
                                    sp = sps[kb % 3]
                                    if key == 0:
                                        te.wait_ge(ct.sem, n_qkv_t)
                                        te.wait_ge(cv.sem, n_qkv_v)
                                        if prev_a2_cs[0] > 0:
                                            te.wait_ge(cs.sem, prev_a2_cs[0])
                                    if key >= 3:
                                        te.wait_ge(cs.sem, exp_done[key - 3])
                                    la = KTb[0:64, p * 2048 + kb * 128:
                                             p * 2048 + (kb + 1) * 128]
                                    ra = QT[0:64, p * 2048 + qc * 512:
                                            p * 2048 + (qc + 1) * 512]
                                    lb = KTb[64:128, p * 2048 + kb * 128:
                                             p * 2048 + (kb + 1) * 128]
                                    rb = QT[64:128, p * 2048 + qc * 512:
                                            p * 2048 + (qc + 1) * 512]
                                    te.matmul(sp[:, 0:512], la, ra,
                                              start=True, stop=True,
                                              tile_position=(0, 0))
                                    ct.inc(te.matmul(sp[:, 512:1024], lb, rb,
                                                     start=True, stop=True,
                                                     tile_position=(64, 0)))
                                    s_done[key] = ct.n

                                def emit_pv(kb):
                                    key = p * 64 + qc * SB + kb
                                    te.wait_ge(cs.sem, exp_done[key])
                                    if kb == 0:
                                        te.wait_ge(cv.sem, ot_write[0])
                                    v3 = V[:].rearrange(
                                        "p (t c) -> p t c", c=1024)
                                    va = v3[:, kb, (2 * p) * 64:
                                            (2 * p) * 64 + 64]
                                    vb = v3[:, kb, (2 * p + 1) * 64:
                                            (2 * p + 1) * 64 + 64]
                                    ea = EK[kb % 2][:, 0:512]
                                    eb = EK[kb % 2][:, 512:1024]
                                    st_, sp_ = (kb == 0), (kb == SB - 1)
                                    te.matmul(pvo[0:64, :], va, ea,
                                              start=st_, stop=sp_,
                                              tile_position=(0, 0),
                                              skip_group_check=True)
                                    te.matmul(pvo[64:128, :], vb, eb,
                                              start=st_, stop=sp_,
                                              tile_position=(0, 64),
                                              skip_group_check=True)
                                    te.matmul(pvd[0:64, :], ones64[:], ea,
                                              start=st_, stop=sp_,
                                              tile_position=(0, 0),
                                              skip_group_check=True)
                                    ct.inc(te.matmul(pvd[64:128, :],
                                                     ones64[:], eb,
                                                     start=st_, stop=sp_,
                                                     tile_position=(0, 64),
                                                     skip_group_check=True))
                                    pv_done[key] = ct.n

                                for kb in range(SB):
                                    emit_s(kb)
                                    if kb >= 2:
                                        emit_pv(kb - 2)
                                emit_pv(SB - 2)
                                emit_pv(SB - 1)
                            blk.tensor(_)
                            n_pv = ct.n

                            def _(v, p=p, qc=qc, n_pv=n_pv):
                                v.wait_ge(ct.sem, n_pv)
                                if dbg_gate[0]:
                                    v.wait_ge(cst.sem, dbg_gate[0])
                                    dbg_gate[0] = 0
                                n_rc = cv.inc(v.reciprocal(RC[:], pvd[:]))
                                v.wait_ge(cv.sem, n_rc)
                                base = p * 2048 + qc * 512
                                v.tensor_tensor(LNXT[0:64, base:base + 512],
                                                pvo[0:64, :], RC[0:64, :],
                                                op=ALU.mult)
                                cv.inc(v.tensor_tensor(
                                    LNXT[64:128, base:base + 512],
                                    pvo[64:128, :], RC[64:128, :],
                                    op=ALU.mult))
                            blk.vector(_)
                            ot_write[0] = cv.n

                        def _(s, p=p, cs_pre=cs_pre):
                            load_tbl(s, SET_LNEXP)
                            for qc in range(4):
                                for kb in range(SB):
                                    key = p * 64 + qc * SB + kb
                                    s.wait_ge(ct.sem, s_done[key])
                                    if key < 2 and prev_xq_store[0][kb] > 0:
                                        s.wait_ge(cqo[kb].sem,
                                                  prev_xq_store[0][kb])
                                    if key >= 2:
                                        s.wait_ge(ct.sem, pv_done[key - 2])
                                    cs.inc(s.activation(
                                        EK[kb % 2][:], sps[kb % 3][:],
                                        AF.Exp, bias=cbias[:, 1:2],
                                        scale=0.125))
                                    assert cs.n == exp_done[key], \
                                        (cs.n, exp_done[key])
                        blk.scalar(_)
                    n_ot_all = cv.n

                if debug and b == 0:
                    def _(g):
                        g.wait_ge(cv.sem, n_ot_all)
                        g.dma_start(out=d_dbg["ot"], in_=LNXT[:]
                                    ).then_inc(cst.sem, 16)
                    blk.gpsimd(_)
                    cst.n += 16
                    dbg_a3 = [True]

                prev_a2_cs[0] = cs.n

                ba3 = contextlib.ExitStack()
                with ba3:
                    wop = [ba3.enter_context(nc.psum_tensor(
                        f"wop{b}_{i}", [128, 1024], F32)) for i in range(2)]
                    x2_evac = {}
                    x2_store = {}
                    xq_store = {}
                    ln2a_done = {}
                    ln2a_qd = {}
                    for sb in range(SB):
                        r = sb % 2

                        def _(sy, b=b, sb=sb, r=r):
                            if sb >= 2:
                                sy.wait_ge(cso[r].sem, x2_store[sb - 2])
                                sy.wait_ge(cv.sem, ln2a_qd[sb - 2])
                            else:
                                sy.wait_ge(cs.sem, norm_done[SB - 1])
                            sy.dma_start(
                                out=x3buf[r][:],
                                in_=d_x[b, sb * 128:(sb + 1) * 128, :]
                            ).then_inc(c3l[r].sem, 16)
                        blk.sync(_)
                        c3l[r].n += 16
                        n_ldx3 = c3l[r].n

                        def _(te, sb=sb, r=r):
                            te.wait_ge(cv.sem, n_ot_all)
                            if sb >= 2:
                                te.wait_ge(cv.sem, x2_evac[sb - 2])
                            last = None
                            for j in range(KT):
                                statp = LNXT[:, j * 2048 + sb * 128:
                                             j * 2048 + (sb + 1) * 128]
                                for c in range(2):
                                    last = te.matmul(
                                        wop[r][:, c * 512:(c + 1) * 512],
                                        statp,
                                        WoB[:, j * 1024 + c * 512:
                                            j * 1024 + (c + 1) * 512],
                                        start=(j == 0), stop=(j == KT - 1))
                            ct.inc(last)
                        blk.tensor(_)
                        n_wo = ct.n

                        def _(v, sb=sb, r=r, n_wo=n_wo, n_ldx3=n_ldx3):
                            v.wait_ge(ct.sem, n_wo)
                            v.wait_ge(c3l[r].sem, n_ldx3)
                            v.tensor_tensor(x3buf[r][:], x3buf[r][:],
                                            wop[r][:], op=ALU.add)
                            cv.inc(v.tensor_tensor(x3buf[r][:], x3buf[r][:],
                                                   bo_bc[:], op=ALU.add))
                        blk.vector(_)
                        x2_evac[sb] = cv.n

                        def _(g, b=b, sb=sb, r=r):
                            g.wait_ge(cv.sem, x2_evac[sb])
                            g.dma_start(
                                out=d_x2[b, sb * 128:(sb + 1) * 128, :],
                                in_=x3buf[r][:]).then_inc(cso[r].sem, 16)
                        blk.gpsimd(_)
                        cso[r].n += 16
                        x2_store[sb] = cso[r].n

                        nrm2_w = []
                        if b > 0 and sb < 2:
                            nrm2_w.append((cv, prev_lnx_guard[0]))
                        ln2a_done[sb] = emit_ln(
                            x3buf[r][:], xbuf[0][:], xbuf[1][:], stat, 20,
                            wait_load=[(cv, x2_evac[sb])],
                            col_guard=(ln2a_done[sb - 1] if sb >= 1 else 0),
                            sq_waits=([(cv, ln2a_qd[sb - 1])] if sb >= 1
                                      else []),
                            norm_waits=nrm2_w,
                            sign_dst=(lnx[1][:], xbuf[0][:]))
                        qd = emit_quant(
                            xbuf[1][:], EK[r][:], D,
                            xbuf[0], lnx[0], lnx[1],
                            aq_bm, aq_sc, aq_rs, aq_r2, aq_sb,
                            wait_v=[(cs, ln2a_done[sb])] +
                                   ([] if sb < 2 else
                                    [(cqo[r], xq_store[sb - 2])]),
                            ext_prep=ln2a_done[sb])
                        ln2a_qd[sb] = qd

                        def _(g, b=b, sb=sb, r=r, qd=qd):
                            g.wait_ge(cv.sem, qd)
                            g.dma_start(
                                out=d_xq[b, sb * 128:(sb + 1) * 128, :],
                                in_=EK[r][:]).then_inc(cqo[r].sem, 16)
                        blk.gpsimd(_)
                        cqo[r].n += 16
                        xq_store[sb] = cqo[r].n
                    if debug and b == 0:
                        def _(g):
                            g.wait_ge(cv.sem, ln2a_qd[SB - 1])
                            g.dma_start(out=d_dbg["aqbm"], in_=aq_bm[:]
                                        ).then_inc(cst.sem, 16)
                            g.dma_start(out=d_dbg["axln"], in_=xbuf[1][:]
                                        ).then_inc(cst.sem, 16)
                            g.dma_start(out=d_dbg["asq"], in_=xbuf[0][:]
                                        ).then_inc(cst.sem, 16)
                            g.dma_start(out=d_dbg["ata"], in_=lnx[0][:]
                                        ).then_inc(cst.sem, 16)
                            g.dma_start(out=d_dbg["atsg"], in_=lnx[1][:]
                                        ).then_inc(cst.sem, 16)
                        blk.gpsimd(_)
                        cst.n += 80
                    prev_lnx_guard[0] = ln2a_qd[SB - 1]
                    prev_ln2_end[0] = ln2a_done[SB - 1]
                    prev_xq_store[0] = (cqo[0].n, cqo[1].n)
                    prev_t_end = ct.n
                    prev_v_end = cv.n
                    prev_qkv_t = n_qkv_t
                    prev_store_last = (cso[0].n, cso[1].n)
                    for _sb in range(SB):
                        x2_store_all[b * SB + _sb] = (_sb % 2, x2_store[_sb])
                        xq_store_all[b * SB + _sb] = (_sb % 2, xq_store[_sb])

        pa_t_end, pa_v_end, pa_s_end, pa_st_end = ct.n, cv.n, cs.n, cst.n
        pa_so = (cso[0].n, cso[1].n)
        pa_qo = (cqo[0].n, cqo[1].n)

        pf = contextlib.ExitStack()
        with pf:
            def pfs(name, shape, dt):
                return pf.enter_context(nc.sbuf_tensor(name, shape, dt))

            WGT = pfs("WGT", [128, KT * DI], BF16)
            WUT = pfs("WUT", [128, KT * DI], BF16)
            WDT = pfs("WDT", [128, NT * D], BF16)
            x2b = [pfs(f"x2b{i}", [128, D], F32) for i in range(2)]
            xqb = [pfs(f"xqb{i}", [128, D], BF16) for i in range(2)]
            XqT = [pfs(f"XqT{i}", [128, D], BF16) for i in range(2)]
            hbuf = pfs("hbuf", [128, DI], F32)
            Hq = pfs("Hq", [128, DI], BF16)
            HqT = [pfs(f"HqT{i}", [128, DI], BF16) for i in range(2)]
            q_tv = pfs("q_tv", [128, DI], F32)
            q_ta = pfs("q_ta", [128, DI], BF16)
            q_sg = pfs("q_sg", [128, DI], BF16)
            q_bm = pfs("q_bm", [128, DI // 16], F32)
            q_r2 = pfs("q_r2", [128, DI // 16], F32)
            q_slb = pfs("q_slb", [128, DI // 16], BF16)
            q_sc = pfs("q_sc", [128, DI // 16], F32)
            q_rs = pfs("q_rs", [128, DI // 16], F32)
            stat2 = pfs("stat2", [128, 24], F32)

            gps = [pf.enter_context(nc.psum_tensor(
                f"gps{i}", [128, 512], F32)) for i in range(2)]
            ups = [pf.enter_context(nc.psum_tensor(
                f"ups{i}", [128, 512], F32)) for i in range(2)]
            tps = [pf.enter_context(nc.psum_tensor(
                f"tps{i}", [128, 1024], BF16)) for i in range(2)]
            dps = pf.enter_context(nc.psum_tensor("dps", [128, 1024], F32))

            def _(g):
                g.wait_ge(ct.sem, pa_t_end)
                g.wait_ge(cv.sem, pa_v_end)
                g.wait_ge(cs.sem, pa_s_end)
                g.wait_ge(cst.sem, pa_st_end)
                g.wait_ge(cso[0].sem, pa_so[0])
                g.wait_ge(cso[1].sem, pa_so[1])
                g.wait_ge(cqo[0].sem, pa_qo[0])
                g.wait_ge(cqo[1].sem, pa_qo[1])
                g.dma_start(out=WGT[:], in_=d_wgt).then_inc(cwl.sem, 16)
                g.dma_start(out=WUT[:], in_=d_wut).then_inc(cwl.sem, 16)
                g.dma_start(out=WDT[:], in_=d_wdt).then_inc(cwl.sem, 16)
            blk.gpsimd(_)
            cwl.n += 48
            n_ffnw = cwl.n

            out_store = {}
            xq_tp_done = {}
            xqt_evac = {}
            hq_tp_done = {}
            hqt_last = {}
            gu_last = {}
            guk_done = {}
            h_done = {}
            qd_h_hist = {}
            down_done = {}
            x2_used = {}

            hbuf_free = {}
            gu_rec_all = {}
            nxqt_all = {}
            nxqtp_all = {}
            nhall_all = {}
            ldx2_all = {}
            hqt_g1_evac = {}
            NCH = len(N_CHUNKS)

            def emit_gu_chunk(gi, r, ci):
                nc0, nc1 = N_CHUNKS[ci]
                cw = nc1 - nc0
                pr = ci % 2
                kk = gi * NCH + ci
                n_xqt = nxqt_all[gi]

                def _(te, r=r, nc0=nc0, nc1=nc1, cw=cw, pr=pr, kk=kk,
                      n_xqt=n_xqt):
                    te.wait_ge(cv.sem, n_xqt)
                    if kk >= 2:
                        te.wait_ge(cv.sem, h_done[kk - 2])
                    for k in range(KT):
                        te.matmul(gps[pr][:, 0:cw],
                                  XqT[r][:, k * 128:(k + 1) * 128],
                                  WGT[:, k * DI + nc0:k * DI + nc1],
                                  start=(k == 0), stop=(k == KT - 1))
                    last = None
                    for k in range(KT):
                        last = te.matmul(
                            ups[pr][:, 0:cw],
                            XqT[r][:, k * 128:(k + 1) * 128],
                            WUT[:, k * DI + nc0:k * DI + nc1],
                            start=(k == 0), stop=(k == KT - 1))
                    ct.inc(last)
                blk.tensor(_)
                guk_done[kk] = ct.n
                gu_rec_all[gi][ci] = ct.n

            def stage_a1(b, sb, gi, r):
                def _(sy, b=b, sb=sb, gi=gi):
                    slot = gi % 2
                    if gi >= 2:
                        sy.wait_ge(ct.sem, xq_tp_done[gi - 2])
                    qslot, qval = xq_store_all[gi]
                    sy.wait_ge(cqo[qslot].sem, qval)
                    sy.dma_start(
                        out=xqb[slot][:],
                        in_=d_xq[b, sb * 128:(sb + 1) * 128, :]
                    ).then_inc(cql[slot].sem, 16)
                blk.sync(_)
                cql[gi % 2].n += 16
                n_ldxq = cql[gi % 2].n

                def _(te, gi=gi, n_ldxq=n_ldxq):
                    te.wait_ge(cql[gi % 2].sem, n_ldxq)
                    if gi == 0:
                        te.wait_ge(cwl.sem, n_ffnw)
                    if gi >= 1:
                        te.wait_ge(cv.sem, xqt_evac[gi - 1])
                    if gi >= 2:
                        te.wait_ge(cv.sem, hqt_g1_evac[gi - 2])
                    last = None
                    for j in range(KT):
                        last = te.transpose(
                            tps[0][:, j * 128:(j + 1) * 128],
                            xqb[gi % 2][:, j * 128:(j + 1) * 128], idb[:])
                    ct.inc(last)
                blk.tensor(_)
                xq_tp_done[gi] = ct.n
                nxqtp_all[gi] = ct.n

            def stage_a2(gi, r):
                n_xqtp = nxqtp_all[gi]

                def _(v, gi=gi, r=r, n_xqtp=n_xqtp):
                    v.wait_ge(ct.sem, n_xqtp)
                    if gi >= 2:
                        v.wait_ge(ct.sem, gu_last[gi - 2])
                    cv.inc(v.tensor_copy(XqT[r][:], tps[0][:]))
                blk.vector(_)
                xqt_evac[gi] = cv.n
                nxqt_all[gi] = cv.n

                gu_rec_all[gi] = {}
                for ci in (0, 1):
                    emit_gu_chunk(gi, r, ci)

            def stage_b1a(gi, r):
                qd_h = qd_h_hist[gi]
                hqt_evacs = []
                for idx, g0 in enumerate(range(0, NT, 8)):
                    ng = min(8, NT - g0)
                    tp = tps[1 - idx % 2]

                    def _(te, g0=g0, ng=ng, qd_h=qd_h, gi=gi, idx=idx,
                          tp=tp, hqt_evacs=tuple(hqt_evacs)):
                        te.wait_ge(cv.sem, qd_h)
                        if idx == 0:
                            if gi >= 1:
                                te.wait_ge(cv.sem, hqt_last[gi - 1])
                        elif idx == 1:
                            if gi + 1 < n_gi:
                                te.wait_ge(cv.sem, xqt_evac[gi + 1])
                        else:
                            te.wait_ge(cv.sem, hqt_evacs[idx - 2])
                        last = None
                        for j in range(ng):
                            last = te.transpose(
                                tp[:, j * 128:(j + 1) * 128],
                                Hq[:, (g0 + j) * 128:(g0 + j + 1) * 128],
                                idb[:])
                        ct.inc(last)
                    blk.tensor(_)
                    n_tp = ct.n

                    def _(v, g0=g0, ng=ng, r=r, n_tp=n_tp, gi=gi, tp=tp):
                        v.wait_ge(ct.sem, n_tp)
                        if g0 == 0 and gi >= 2:
                            v.wait_ge(ct.sem, down_done[gi - 2])
                        cv.inc(v.tensor_copy(
                            HqT[r][:, g0 * 128:(g0 + ng) * 128],
                            tp[:, 0:ng * 128]))
                    blk.vector(_)
                    hqt_evacs.append(cv.n)
                hq_tp_done[gi] = ct.n
                hqt_last[gi] = cv.n
                hqt_g1_evac[gi] = hqt_evacs[1]

            def stage_b1b(gi, r):
                n_hqt = hqt_last[gi]

                def _(te, r=r, n_hqt=n_hqt, gi=gi):
                    te.wait_ge(cv.sem, n_hqt)
                    if gi >= 1:
                        te.wait_ge(cv.sem, x2_used[gi - 1])
                    last = None
                    for t in range(NT):
                        for c in range(2):
                            last = te.matmul(
                                dps[:, c * 512:(c + 1) * 512],
                                HqT[r][:, t * 128:(t + 1) * 128],
                                WDT[:, t * D + c * 512:
                                    t * D + (c + 1) * 512],
                                start=(t == 0), stop=(t == NT - 1))
                    ct.inc(last)
                blk.tensor(_)
                down_done[gi] = ct.n

            def emit_x2_load(gi):
                b, sb = gi // SB, gi % SB
                r = gi % 2

                def _(g, b=b, sb=sb, r=r, gi=gi):
                    if gi >= 2:
                        g.wait_ge(cfo[r].sem, out_store[gi - 2])
                    slot, val = x2_store_all[gi]
                    g.wait_ge(cso[slot].sem, val)
                    g.dma_start(
                        out=x2b[r][:],
                        in_=d_x2[b, sb * 128:(sb + 1) * 128, :]
                    ).then_inc(cfl[r].sem, 16)
                blk.gpsimd(_)
                cfl[r].n += 16
                ldx2_all[gi] = cfl[r].n

            def stage_b2(gi, r):
                b, sb = gi // SB, gi % SB
                n_down = down_done[gi]
                n_ldx2 = ldx2_all[gi]

                def _(v, r=r, n_down=n_down, n_ldx2=n_ldx2):
                    v.wait_ge(ct.sem, n_down)
                    v.wait_ge(cfl[r].sem, n_ldx2)
                    cv.inc(v.tensor_tensor(x2b[r][:], x2b[r][:], dps[:],
                                           op=ALU.add))
                blk.vector(_)
                x2_used[gi] = cv.n
                n_out = cv.n

                def _(g, b=b, sb=sb, r=r, n_out=n_out):
                    g.wait_ge(cv.sem, n_out)
                    g.dma_start(
                        out=d_out[b, sb * 128:(sb + 1) * 128, :],
                        in_=x2b[r][:]).then_inc(cfo[r].sem, 16)
                blk.gpsimd(_)
                cfo[r].n += 16
                out_store[gi] = cfo[r].n

                if gi + 2 < n_gi:
                    emit_x2_load(gi + 2)

            def stage_c1(b, sb, gi, r):
                gu_rec = gu_rec_all[gi]
                cv_pre_tt = cv.n
                for ci in range(NCH):
                    h_done[gi * NCH + ci] = cv_pre_tt + ci + 1
                for ci in range(2, NCH):
                    emit_gu_chunk(gi, r, ci)
                gu_last[gi] = ct.n

                cs_pre_silu = cs.n

                def _(s, gi=gi, gu_rec=gu_rec, cv_pre_tt=cv_pre_tt):
                    def ss(ci, inc):
                        nc0, nc1 = N_CHUNKS[ci]
                        cw = nc1 - nc0
                        s.wait_ge(cv.sem, cv_pre_tt + ci + 1)
                        s.activation(q_sg[:, nc0:nc0 + cw],
                                     hbuf[:, nc0:nc0 + cw], AF.Sign)
                        a = s.activation(q_tv[:, nc0:nc0 + cw],
                                         hbuf[:, nc0:nc0 + cw], AF.Square)
                        if inc:
                            cs.inc(a)

                    load_tbl(s, SET_SILU)
                    if gi >= 1:
                        s.wait_ge(cv.sem, hbuf_free[gi - 1])
                    for ci in range(NCH):
                        nc0, nc1 = N_CHUNKS[ci]
                        cw = nc1 - nc0
                        pr = ci % 2
                        s.wait_ge(ct.sem, gu_rec[ci])
                        if ci >= 2:
                            s.wait_ge(cv.sem, cv_pre_tt + ci - 1)
                        cs.inc(s.activation(hbuf[:, nc0:nc0 + cw],
                                            gps[pr][:, 0:cw], AF.Silu))
                        if ci == 1 and last_qd[0] > 0:
                            s.wait_ge(cv.sem, last_qd[0])
                        if ci >= 1:
                            ss(ci - 1, False)
                    ss(NCH - 1, True)
                blk.scalar(_)

                for ci, (nc0, nc1) in enumerate(N_CHUNKS):
                    cw = nc1 - nc0
                    pr = ci % 2

                    def _(v, nc0=nc0, nc1=nc1, cw=cw, pr=pr, ci=ci,
                          cs_pre_silu=cs_pre_silu):
                        v.wait_ge(cs.sem, cs_pre_silu + ci + 1)
                        cv.inc(v.tensor_tensor(hbuf[:, nc0:nc0 + cw],
                                               hbuf[:, nc0:nc0 + cw],
                                               ups[pr][:, 0:cw],
                                               op=ALU.mult))
                        red = v.tensor_reduce(
                            q_bm[:, nc0 // 16:nc1 // 16].rearrange(
                                "p (g o) -> p g o", o=1),
                            hbuf[:, nc0:nc0 + cw].rearrange(
                                "p (g w) -> p g w", w=16),
                            axis=AX.X, op=ALU.max,
                            apply_absolute_value=True)
                        if ci == NCH - 1:
                            cv.inc(red)
                    blk.vector(_)
                assert cv.n == cv_pre_tt + NCH + 1
                nhall_all[gi] = (cv.n, cs.n)

            def stage_c2(b, sb, gi, r):
                n_h_all, n_sgn_h = nhall_all[gi]

                marks = {}
                qd_h = emit_quant(
                    hbuf[:], Hq[:], DI,
                    q_tv, q_ta, q_sg, q_bm, q_sc, q_rs,
                    q_r2, q_slb,
                    wait_v=[(cv, n_h_all),
                            (ct, hq_tp_done.get(gi - 1, 0))],
                    ext_prep=n_sgn_h, marks=marks, pre_red=n_h_all)
                qd_h_hist[gi] = qd_h
                hbuf_free[gi] = marks["bmax"]

            n_gi = n_batch * SB
            emit_x2_load(0)
            emit_x2_load(1)
            stage_a1(0, 0, 0, 0)
            stage_a2(0, 0)
            for b in range(n_batch):
                for sb in range(SB):
                    gi = b * SB + sb
                    r = sb % 2
                    if gi + 1 < n_gi:
                        stage_a1((gi + 1) // SB, (gi + 1) % SB, gi + 1,
                                 (gi + 1) % 2)
                    stage_c1(b, sb, gi, r)
                    if gi >= 1:
                        stage_b1b(gi - 1, (gi - 1) % 2)
                    if gi + 1 < n_gi:
                        stage_a2(gi + 1, (gi + 1) % 2)
                    stage_c2(b, sb, gi, r)
                    stage_b1a(gi, r)
                    if gi >= 1:
                        stage_b2(gi - 1, (gi - 1) % 2)
            stage_b1b(n_gi - 1, (n_gi - 1) % 2)
            stage_b2(n_gi - 1, (n_gi - 1) % 2)

            if debug:
                def _(g):
                    g.wait_ge(cwl.sem, n_ffnw)
                    g.dma_start(out=d_dbg["wgt"], in_=WGT[:]
                                ).then_inc(cst.sem, 16)
                    g.dma_start(out=d_dbg["wut"], in_=WUT[:]
                                ).then_inc(cst.sem, 16)
                    g.dma_start(out=d_dbg["wdt"], in_=WDT[:]
                                ).then_inc(cst.sem, 16)
                    g.wait_ge(cv.sem, cv.n)
                    g.wait_ge(cs.sem, cs.n)
                    last_r = (SB - 1) % 2
                    g.dma_start(out=d_dbg["xq"], in_=xqb[(SB - 1) % 2][:]
                                ).then_inc(cst.sem, 16)
                    g.dma_start(out=d_dbg["hq"], in_=Hq[:]
                                ).then_inc(cst.sem, 16)
                    g.dma_start(out=d_dbg["hbuf"], in_=hbuf[:]
                                ).then_inc(cst.sem, 16)
                    g.dma_start(out=d_dbg["xln2"], in_=x2b[last_r][:]
                                ).then_inc(cst.sem, 16)
                    g.dma_start(out=d_dbg["xqt"], in_=XqT[last_r][:]
                                ).then_inc(cst.sem, 16)
                    g.dma_start(out=d_dbg["hqt"], in_=HqT[last_r][:]
                                ).then_inc(cst.sem, 16)
                    g.dma_start(out=d_dbg["x2"],
                                in_=d_x2.rearrange("b s d -> (b s) d")
                                ).then_inc(cst.sem, 16)
                blk.gpsimd(_)
                cst.n += 160

            def _(g):
                g.wait_ge(cst.sem, cst.n)
                g.wait_ge(cso[0].sem, cso[0].n)
                g.wait_ge(cso[1].sem, cso[1].n)
                g.wait_ge(cfo[0].sem, cfo[0].n)
                g.wait_ge(cfo[1].sem, cfo[1].n)
                g.wait_ge(cqo[0].sem, cqo[0].n)
                g.wait_ge(cqo[1].sem, cqo[1].n)
            blk.gpsimd(_)

    return nc


def kernel(L0_attn_hidden_states, L0_attn_Wq, L0_attn_Wkv, L0_attn_Wo,
           L0_attn_bo, L0_ffn_w_gate, L0_ffn_w_up, L0_ffn_w_down):
    x = np.ascontiguousarray(L0_attn_hidden_states, dtype=np.float32)
    wq = np.ascontiguousarray(L0_attn_Wq, dtype=np.float32)
    wkv = np.ascontiguousarray(L0_attn_Wkv, dtype=np.float32)
    wo = np.ascontiguousarray(L0_attn_Wo, dtype=np.float32)
    bo = np.ascontiguousarray(L0_attn_bo, dtype=np.float32)
    wg = np.ascontiguousarray(L0_ffn_w_gate, dtype=np.float32)
    wu = np.ascontiguousarray(L0_ffn_w_up, dtype=np.float32)
    wd = np.ascontiguousarray(L0_ffn_w_down, dtype=np.float32)

    bo_t = np.ascontiguousarray(np.broadcast_to(bo[None, :], (128, D)))
    ident = np.eye(128, dtype=np.float32)

    nc = build()
    in_maps = []
    for c in range(N_CORES):
        in_maps.append({
            "x": np.ascontiguousarray(x[c * N_BATCH:(c + 1) * N_BATCH]),
            "wq": wq, "wkv": wkv, "wo": wo, "bo_t": bo_t,
            "wg": wg, "wu": wu, "wd": wd, "ident": ident,
        })
    res = run_bass_kernel_spmd(nc, in_maps, core_ids=list(range(N_CORES)))
    return np.concatenate([r["out"] for r in res.results], axis=0)
